# version 01
# baseline (speedup 1.0000x reference)
import numpy as np
import ml_dtypes
import concourse.bass as bass
import concourse.mybir as mybir
from concourse.bass_utils import run_bass_kernel_spmd

F32 = mybir.dt.float32
BF16 = mybir.dt.bfloat16
AF = mybir.ActivationFunctionType
ALU = mybir.AluOpType
AX = mybir.AxisListType
BF = ml_dtypes.bfloat16

B, T, D = 2, 16384, 1024
TPC = T // 4
SUB = 128
NSUB = TPC // SUB
H0, DK, DV = 4, 256, 512
H1, E1 = 16, 64
I0 = 2816
EPSM = 1e-5
EPS0 = 1e-5
EPS1 = 1e-6

GAMMA = [1.0 - 2.0 ** (-5.0 - h) for h in range(H0)]
GC = [g ** SUB for g in GAMMA]

FP4_TH = [0.5, 1.5, 2.5, 3.5, 5.0, 7.0, 10.0]
ACT_CONSTS = [-th for th in FP4_TH] + [-1.0, 1e-5, 1.0 / 512.0]


class Ctr:
    def __init__(self, sem, name):
        self.sem = sem
        self.name = name
        self.v = 0


class NS:
    pass


def _islices(total, step):
    out = []
    a = 0
    while a < total:
        out.append((a, min(step, total - a)))
        a += step
    return out


def _host_tables(t_total=T):
    t = {}
    invf = 10000.0 ** (-np.arange(0, DK, 2, dtype=np.float64) / DK)
    tt = np.arange(t_total, dtype=np.float64)[:, None]
    ang = tt * invf[None, :]
    t["ropec"] = np.cos(ang).astype(np.float32)
    t["ropes"] = np.sin(ang).astype(np.float32)
    s0 = DK ** (-0.5)
    t["ropecq"] = (np.cos(ang) * s0).astype(np.float32)
    t["ropesq"] = (np.sin(ang) * s0).astype(np.float32)
    i_ = np.arange(SUB, dtype=np.float64)[:, None]
    g_ = np.asarray(GAMMA, dtype=np.float64)[None, :]
    t["kdect"] = (g_ ** (SUB - 1.0 - i_)).astype(np.float32)
    t["crosst"] = (g_ ** (i_ + 1.0)).astype(np.float32)
    ii = np.arange(SUB, dtype=np.float64)
    dmk = np.zeros((SUB, H0, SUB), dtype=np.float64)
    for h in range(H0):
        m = np.where(ii[:, None] >= ii[None, :],
                     np.float64(GAMMA[h]) ** (ii[:, None] - ii[None, :]), 0.0)
        dmk[:, h, :] = m.T
    t["dmaskt"] = dmk.astype(np.float32)
    return t


def _core_tables(pos, tpc=TPC):
    w = np.zeros((3, H0), dtype=np.float64)
    for ps in (1, 2, 3):
        if ps > pos:
            for h in range(H0):
                w[ps - 1, h] = np.float64(GAMMA[h]) ** (tpc * (ps - 1 - pos))
    m = np.zeros((4,), dtype=np.float32)
    if pos >= 1:
        m[pos - 1] = 1.0
    return w.astype(np.float32), m


def _emit(p, active, h):
    S = NS()
    S.p = p
    S.active = active
    S.h = h

    S.last = None
    S.pend = {}
    S.pend_all = {e: set() for e in ("sync", "gpsimd", "vector", "scalar", "tensor")}

    def _flush():
        if S.last is not None:
            for ctr, amt in S.pend.items():
                S.last.then_inc(ctr.sem, amt)
        S.pend = {}

    _AUTOC = {"vector": "CV", "scalar": "CS"}

    def op(eng, fn, inc=None):
        ctr0 = getattr(p, _AUTOC[eng]) if eng in _AUTOC else None
        if ctr0 is not None and eng == active and ctr0.v > 0:
            _flush()
            S.h.wait_ge(ctr0.sem, ctr0.v)
        ins = None
        if eng == active:
            _flush()
            ins = fn(S.h)
            S.last = ins
        S.pend_all[eng] = set()
        if ctr0 is not None:
            ctr0.v += 1
            S.pend_all[eng].add(ctr0.name)
            if ins is not None:
                S.pend[ctr0] = S.pend.get(ctr0, 0) + 1
        if inc is not None:
            ctr, amt = inc
            ctr.v += amt
            S.pend_all[eng].add(ctr.name)
            if ins is not None:
                S.pend[ctr] = S.pend.get(ctr, 0) + amt
        return ins

    def mark(eng, ctr):
        if ctr.name in S.pend_all[eng]:
            return
        ctr.v += 1
        S.pend_all[eng].add(ctr.name)
        if eng == active:
            assert S.last is not None
            S.pend[ctr] = S.pend.get(ctr, 0) + 1

    S.flush = _flush

    S.need_drain = {"sync": False, "gpsimd": False}

    def dma(eng, ctr, out, in_, transpose=False):
        if eng in S.need_drain and S.need_drain[eng]:
            S.need_drain[eng] = False
            if eng == active:
                S.h.wait_ge(ctr.sem, ctr.v)
        if eng == active:
            _flush()
            S.h.dma_start(out=out, in_=in_, transpose=transpose).then_inc(ctr.sem, 16)
        ctr.v += 16

    def wait(eng, ctr, val=None):
        if eng in S.need_drain:
            S.need_drain[eng] = True
        v = ctr.v if val is None else val
        assert v <= ctr.v, (ctr.name, v, ctr.v)
        if eng == active:
            S.h.wait_ge(ctr.sem, v)

    S.op, S.dma, S.wait, S.mark = op, dma, wait, mark
    _program(S)
    _flush()


def _program(S):
    p = S.p
    dma, wait = S.dma, S.wait
    LD = p.LDs[0]

    dma("sync", LD, p.actc_sb, p.actc_h)
    dma("sync", LD, p.kdect_sb, p.kdect_h)
    dma("sync", LD, p.crosst_sb, p.crosst_h)
    dma("sync", LD, p.dmask_sb, p.dmaskt_h)
    dma("sync", LD, p.lnw_sb, p.lnw_rep_h)
    dma("sync", LD, p.lnb_sb, p.lnb_rep_h)
    dma("sync", LD, p.gnw_sb, p.gnw_rep_h)
    dma("sync", LD, p.sxw_sb, p.sxw_h)
    dma("sync", LD, p.sxm_sb, p.sxm_h)
    dma("sync", LD, p.onesf_sb, p.onesf_h)
    dma("sync", LD, p.identf_sb, p.identf_h)
    dma("sync", LD, p.clip_sb, p.clipx_h)
    ld_consts = LD.v
    for e in ("vector", "scalar", "tensor", "gpsimd", "sync"):
        wait(e, LD, ld_consts)

    phases = [("A0", _phase_A0), ("X1", _x1_state_exchange), ("A1", _phase_A1),
              ("B1", _phase_B1), ("B2", _phase_B2), ("C", _phase_C),
              ("X2", _x2_kv_max), ("D1", _phase_D1), ("D2", _phase_D2),
              ("X3", _x3_maxh), ("E", _phase_E)]
    for name, fn in phases:
        fn(S)
        st = _STOP[0] or ""
        if name == st or st.startswith(name + "."):
            break

    wait("gpsimd", p.STs[0], p.STs[0].v)
    wait("gpsimd", p.STs[1], p.STs[1].v)


def _emit_ln(S, x_ap, out_ap, affine_out=None):
    p = S.p
    op, wait, mark = S.op, S.wait, S.mark
    CV, CS = p.CV, p.CS
    wait("scalar", CV, CV.v)
    op("scalar", lambda h: h.activation(out=p.trash, in_=x_ap, func=AF.Copy,
                                        accum_out=p.sums[:, 0:1]))
    op("scalar", lambda h: h.activation(out=p.trash2, in_=x_ap, func=AF.Square,
                                        accum_out=p.sums[:, 1:2]), inc=(CS, 1))
    wait("vector", CS, CS.v)
    op("vector", lambda h: h.tensor_scalar(out=p.mu, in0=p.sums[:, 0:1],
                                           scalar1=1.0 / 1024.0, scalar2=None,
                                           op0=ALU.mult))
    op("vector", lambda h: h.tensor_scalar(out=p.rstd, in0=p.sums[:, 1:2],
                                           scalar1=1.0 / 1024.0, scalar2=None,
                                           op0=ALU.mult), inc=(CV, 1))
    wait("vector", CV, CV.v)
    op("vector", lambda h: h.scalar_tensor_tensor(out=p.rstd, in0=p.mu, scalar=p.mu,
                                                  in1=p.rstd, op0=ALU.mult,
                                                  op1=ALU.subtract), inc=(CV, 1))
    wait("scalar", CV, CV.v)
    op("scalar", lambda h: h.activation(out=p.rstd, in_=p.rstd, func=AF.Sqrt,
                                        scale=-1.0, bias=EPSM), inc=(CS, 1))
    wait("vector", CS, CS.v)
    op("vector", lambda h: h.reciprocal(out=p.rstd, in_=p.rstd), inc=(CV, 1))
    wait("vector", CV, CV.v)
    op("vector", lambda h: h.tensor_scalar(out=out_ap, in0=x_ap, scalar1=p.mu,
                                           scalar2=p.rstd, op0=ALU.subtract,
                                           op1=ALU.mult))
    if affine_out is not None:
        op("vector", lambda h: h.tensor_tensor(out=p.trash, in0=out_ap, in1=p.lnw_sb,
                                               op=ALU.mult))
        op("vector", lambda h: h.tensor_tensor(out=affine_out, in0=p.trash,
                                               in1=p.lnb_sb, op=ALU.add))
    mark("vector", CV)
    return CV.v


def _emit_fp4(S, x_ap, out_ap, F):
    p = S.p
    t = p.qt
    op, wait, mark = S.op, S.wait, S.mark
    CV, CS = p.CV, p.CS
    nb = F // 16
    x3 = x_ap.rearrange("p (b k) -> p b k", k=16)
    bm = t.bm[:, 0:nb]
    bm3 = bm.rearrange("p (b o) -> p b o", o=1)
    op("vector", lambda h: h.reduce_max(out=bm3, in_=x3, axis=AX.X,
                                        apply_absolute_value=True))
    op("vector", lambda h: h.reciprocal(out=t.rr[:, 0:nb], in_=bm))
    op("vector", lambda h: h.tensor_scalar(out=t.rr[:, 0:nb], in0=t.rr[:, 0:nb],
                                           scalar1=12.0, scalar2=2e12,
                                           op0=ALU.mult, op1=ALU.min), inc=(CV, 1))
    wait("vector", CV, CV.v)
    rrb = t.rr[:, 0:nb].unsqueeze(2).broadcast_to([128, nb, 16])
    w3 = t.w[:, 0:F].rearrange("p (b k) -> p b k", k=16)
    op("vector", lambda h: h.tensor_tensor(out=w3, in0=x3, in1=rrb, op=ALU.mult),
       inc=(CV, 1))
    wait("scalar", CV, CV.v)
    op("scalar", lambda h: h.activation(out=t.aw[:, 0:F], in_=t.w[:, 0:F],
                                        func=AF.Abs))
    for rnd in range(4):
        js = [2 * rnd, 2 * rnd + 1] if rnd < 3 else [6]
        for i, j in enumerate(js):
            last = (i == len(js) - 1) and rnd != 3
            op("scalar", lambda h, i=i, j=j: h.activation(
                out=t.sg[i][:, 0:F], in_=t.aw[:, 0:F], func=AF.Sign, bias=-FP4_TH[j]),
               inc=(CS, 1) if last else None)
        if rnd == 3:
            op("scalar", lambda h: h.activation(out=t.sgn[:, 0:F], in_=t.w[:, 0:F],
                                                func=AF.Sign), inc=(CS, 1))
        wait("vector", CS, CS.v)
        if rnd == 0:
            op("vector", lambda h: h.tensor_tensor(out=t.acc1[:, 0:F], in0=t.sg[0][:, 0:F],
                                                   in1=t.sg[1][:, 0:F], op=ALU.add))
        elif rnd == 1:
            op("vector", lambda h: h.tensor_tensor(out=t.tmpb[:, 0:F], in0=t.sg[0][:, 0:F],
                                                   in1=t.sg[1][:, 0:F], op=ALU.add))
            op("vector", lambda h: h.tensor_tensor(out=t.acc1[:, 0:F], in0=t.acc1[:, 0:F],
                                                   in1=t.tmpb[:, 0:F], op=ALU.add))
        elif rnd == 2:
            op("vector", lambda h: h.tensor_tensor(out=t.acc2[:, 0:F], in0=t.sg[0][:, 0:F],
                                                   in1=t.sg[1][:, 0:F], op=ALU.add))
        else:
            op("vector", lambda h: h.scalar_tensor_tensor(
                out=t.tmpb[:, 0:F], in0=t.acc1[:, 0:F], scalar=0.25,
                in1=t.sg[0][:, 0:F], op0=ALU.mult, op1=ALU.add))
            op("vector", lambda h: h.scalar_tensor_tensor(
                out=t.tmpb[:, 0:F], in0=t.acc2[:, 0:F], scalar=0.5,
                in1=t.tmpb[:, 0:F], op0=ALU.mult, op1=ALU.add))
        mark("vector", CV)
        if rnd < 3:
            wait("scalar", CV, CV.v)
    op("vector", lambda h: h.tensor_scalar(out=bm, in0=bm, scalar1=1.0 / 6.0,
                                           scalar2=1e-12, op0=ALU.mult, op1=ALU.max))
    scb = bm.unsqueeze(2).broadcast_to([128, nb, 16])
    sg3 = t.sgn[:, 0:F].rearrange("p (b k) -> p b k", k=16)
    ss3 = t.aw[:, 0:F].rearrange("p (b k) -> p b k", k=16)
    op("vector", lambda h: h.tensor_tensor(out=ss3, in0=sg3, in1=scb, op=ALU.mult))
    op("vector", lambda h: h.scalar_tensor_tensor(out=out_ap, in0=t.tmpb[:, 0:F],
                                                  scalar=3.0, in1=t.aw[:, 0:F],
                                                  op0=ALU.add, op1=ALU.mult),
       inc=(CV, 1))
    return CV.v


def _emit_round_clip(S, v_ap, out_ap, F):
    p = S.p
    t = p.qt
    op, wait, mark = S.op, S.wait, S.mark
    CV, CS = p.CV, p.CS
    op("vector", lambda h: h.tensor_scalar(out=v_ap, in0=v_ap, scalar1=-128.0,
                                           scalar2=127.0, op0=ALU.max, op1=ALU.min),
       inc=(CV, 1))
    wait("scalar", CV, CV.v)
    op("scalar", lambda h: h.activation(out=t.y0[:, 0:F], in_=v_ap, func=AF.Abs))
    op("scalar", lambda h: h.activation(out=t.sgn[:, 0:F], in_=v_ap, func=AF.Sign),
       inc=(CS, 1))
    wait("vector", CS, CS.v)
    op("vector", lambda h: h.tensor_copy(out=t.tt[:, 0:F], in_=t.y0[:, 0:F]),
       inc=(CV, 1))
    for m in (128.0, 64.0, 32.0, 16.0, 8.0, 4.0, 2.0, 1.0):
        op("vector", lambda h, m=m: h.tensor_scalar(out=t.d[:, 0:F], in0=t.tt[:, 0:F],
                                                    scalar1=m - 0.5, scalar2=-m,
                                                    op0=ALU.is_ge, op1=ALU.mult))
        op("vector", lambda h: h.tensor_tensor(out=t.tt[:, 0:F], in0=t.tt[:, 0:F],
                                               in1=t.d[:, 0:F], op=ALU.add))
    op("vector", lambda h: h.tensor_tensor(out=t.y0[:, 0:F], in0=t.y0[:, 0:F],
                                           in1=t.tt[:, 0:F], op=ALU.subtract))
    op("vector", lambda h: h.tensor_tensor(out=out_ap, in0=t.y0[:, 0:F],
                                           in1=t.sgn[:, 0:F], op=ALU.mult),
       inc=(CV, 1))
    return CV.v


def _phase_A0(S):
    p = S.p
    op, dma, wait, mark = S.op, S.dma, S.wait, S.mark
    GW, CV, CS, CT = p.GW, p.CV, p.CS, p.CT
    n_sub = p.n_sub
    ev = {}

    for c in range(8):
        dma("gpsimd", GW, p.wkv_sb[:, c, :], p.wkv_h[128 * c:128 * (c + 1), :])
    wait("tensor", GW, GW.v)
    op("vector", lambda h: h.memset(p.S_sb.rearrange("p h t d -> p (h t d)"), 0.0),
       inc=(CV, 1))

    for s in range(n_sub):
        r = s % 2
        LD = p.LDs[r]
        ST = p.STs[r]
        if s >= 2:
            wait("sync", CV, ev[("nb", s - 2)])
        dma("sync", LD, p.a0x[r], p.x_h[SUB * s:SUB * (s + 1), :])
        dma("sync", LD, p.ropec_t[r], p.ropec_h[SUB * s:SUB * (s + 1), :])
        dma("sync", LD, p.ropes_t[r], p.ropes_h[SUB * s:SUB * (s + 1), :])
        ev[("ld", s)] = LD.v

        wait("vector", LD, ev[("ld", s)])
        wait("scalar", LD, ev[("ld", s)])
        if s >= 2:
            wait("vector", ST, ev[("st_xn", s - 2)])
        _emit_ln(S, p.a0x[r], p.a0xn[r])
        ev[("xn", s)] = CV.v
        wait("gpsimd", CV, ev[("xn", s)])
        dma("gpsimd", ST, p.xn_s[SUB * s:SUB * (s + 1), :], p.a0xn[r])
        ev[("st_xn", s)] = ST.v
        if s >= 2:
            wait("vector", ST, ev[("st_nb", s - 2)])
            wait("vector", LD, ev[("nfm", s - 2)])
        _emit_ln(S, p.a0xn[r], p.a0z, affine_out=p.a0nb[r])
        ev[("nb", s)] = CV.v
        wait("gpsimd", CV, ev[("nb", s)])
        dma("gpsimd", ST, p.normed_s[SUB * s:SUB * (s + 1), :], p.a0nb[r])
        ev[("st_nb", s)] = ST.v

        if s >= 2:
            wait("sync", CT, ev[("proj", s - 2)])
        wait("sync", CV, ev[("nb", s)])
        for c in range(8):
            dma("sync", LD, p.normfm[r][:, c, :],
                p.a0nb[r][:, 128 * c:128 * (c + 1)], transpose=True)
        ev[("nfm", s)] = LD.v

        wait("tensor", LD, ev[("nfm", s)])
        if s >= 1:
            wait("tensor", CV, ev[("krope", s - 1)])
            wait("tensor", CS, ev[("vevac", s - 1)])
        for c in range(8):
            op("tensor", lambda h, c=c: h.ldweights(p.normfm[r][:, c, :]))
            for j in range(6):
                op("tensor", lambda h, c=c, j=j: h.matmul(
                    p.PS[:, 512 * j:512 * (j + 1)], p.normfm[r][:, c, :],
                    p.wkv_sb[:, c, 512 * j:512 * (j + 1)],
                    start=(c == 0), stop=(c == 7)))
        mark("tensor", CT)
        ev[("proj", s)] = CT.v

        wait("vector", CT, ev[("proj", s)])
        if s >= 2:
            wait("vector", ST, ev[("st_k", s - 2)])
        kps = p.PS[:, 0:1024].rearrange("p (h d) -> p h d", h=H0)
        for h0 in range(H0):
            k1, k2 = kps[:, h0, 0:128], kps[:, h0, 128:256]
            o1 = p.a0ktm[r][:, h0, 0:128]
            o2 = p.a0ktm[r][:, h0, 128:256]
            op("vector", lambda h, k1=k1: h.tensor_tensor(out=p.tmp1, in0=k1, in1=p.ropec_t[r], op=ALU.mult))
            op("vector", lambda h, k2=k2: h.tensor_tensor(out=p.tmp2, in0=k2, in1=p.ropes_t[r], op=ALU.mult))
            op("vector", lambda h, o1=o1: h.tensor_tensor(out=o1, in0=p.tmp1, in1=p.tmp2, op=ALU.subtract))
            op("vector", lambda h, k2=k2: h.tensor_tensor(out=p.tmp1, in0=k2, in1=p.ropec_t[r], op=ALU.mult))
            op("vector", lambda h, k1=k1: h.tensor_tensor(out=p.tmp2, in0=k1, in1=p.ropes_t[r], op=ALU.mult))
            op("vector", lambda h, o2=o2: h.tensor_tensor(out=o2, in0=p.tmp1, in1=p.tmp2, op=ALU.add))
        mark("vector", CV)
        ev[("krope", s)] = CV.v
        for h0 in range(H0):
            op("vector", lambda h, h0=h0: h.tensor_scalar(
                out=p.a0ktil[:, h0, :], in0=p.a0ktm[r][:, h0, :],
                scalar1=p.kdect_sb[:, h0:h0 + 1], scalar2=None, op0=ALU.mult))
        mark("vector", CV)
        ev[("ktil", s)] = CV.v
        wait("gpsimd", CV, ev[("krope", s)])
        dma("gpsimd", ST, p.ktm_s[SUB * s:SUB * (s + 1), :],
            p.a0ktm[r].rearrange("p h d -> p (h d)"))
        ev[("st_k", s)] = ST.v

        wait("scalar", CT, ev[("proj", s)])
        if s >= 2:
            wait("scalar", ST, ev[("st_v", s - 2)])
        op("scalar", lambda h: h.activation(out=p.a0vtm[r], in_=p.PS[:, 1024:3072],
                                            func=AF.Copy), inc=(CS, 1))
        ev[("vevac", s)] = CS.v
        wait("gpsimd", CS, ev[("vevac", s)])
        dma("gpsimd", ST, p.vtm_s[SUB * s:SUB * (s + 1), :], p.a0vtm[r])
        ev[("st_v", s)] = ST.v

        wait("tensor", CV, ev[("ktil", s)])
        wait("tensor", CS, ev[("vevac", s)])
        for h0 in range(H0):
            if s + h0 > 0:
                prev = ev[("sst", s, h0 - 1)] if h0 > 0 else ev[("sst", s - 1, H0 - 1)]
                wait("tensor", CV, prev)
            for half in range(2):
                op("tensor", lambda h, h0=h0, half=half: h.ldweights(
                    p.a0ktil[:, h0, 128 * half:128 * (half + 1)]))
                op("tensor", lambda h, h0=h0, half=half: h.matmul(
                    p.PS[:, 3072 + 512 * half:3072 + 512 * (half + 1)],
                    p.a0ktil[:, h0, 128 * half:128 * (half + 1)],
                    p.a0vtm[r][:, 512 * h0:512 * (h0 + 1)],
                    start=True, stop=True))
            mark("tensor", CT)
            ev[("Y", s, h0)] = CT.v
            wait("vector", CT, ev[("Y", s, h0)])
            for half in range(2):
                op("vector", lambda h, h0=h0, half=half: h.scalar_tensor_tensor(
                    out=p.S_sb[:, h0, half, :], in0=p.S_sb[:, h0, half, :],
                    scalar=GC[h0], in1=p.PS[:, 3072 + 512 * half:3072 + 512 * (half + 1)],
                    op0=ALU.mult, op1=ALU.add))
            mark("vector", CV)
            ev[("sst", s, h0)] = CV.v

    p.ev_a0 = ev


def _x1_state_exchange(S):
    p = S.p
    op, dma, wait, mark = S.op, S.dma, S.wait, S.mark
    GW, CV, CS, CT, CP = p.GW, p.CV, p.CS, p.CT, p.CP
    xchf = p.xch.rearrange("p a h t d -> p (a h t d)")
    sflat = p.S_sb.rearrange("p h t d -> p (h t d)")
    for ps in range(3):
        for h0 in range(H0):
            op("vector", lambda h, ps=ps, h0=h0: h.tensor_scalar(
                out=p.xch[:, ps, h0, :, :], in0=p.S_sb[:, h0, :, :],
                scalar1=p.sxw_sb[:, ps * H0 + h0:ps * H0 + h0 + 1],
                scalar2=None, op0=ALU.mult))
    mark("vector", CV)
    wait("gpsimd", CV, CV.v)
    dma("gpsimd", p.STs[0], p.ccs_in, xchf)
    wait("gpsimd", p.STs[0], p.STs[0].v)
    wait("gpsimd", p.STs[1], p.STs[1].v)
    op("gpsimd", lambda h: h.collective_compute(
        "AllReduce", ALU.add, replica_groups=[[0, 1, 2, 3], [4, 5, 6, 7]],
        ins=[p.ccs_in.opt()], outs=[p.ccs_out.opt()]), inc=(CP, 1))
    wait("gpsimd", CP, CP.v)
    dma("gpsimd", GW, xchf, p.ccs_out)
    wait("vector", GW, GW.v)
    op("vector", lambda h: h.tensor_scalar(
        out=sflat, in0=p.xch[:, 0, :, :, :].rearrange("p h t d -> p (h t d)"),
        scalar1=p.sxm_sb[:, 0:1], scalar2=None, op0=ALU.mult))
    for ps in (1, 2):
        op("vector", lambda h, ps=ps: h.scalar_tensor_tensor(
            out=sflat, in0=p.xch[:, ps, :, :, :].rearrange("p h t d -> p (h t d)"),
            scalar=p.sxm_sb[:, ps:ps + 1], in1=sflat, op0=ALU.mult, op1=ALU.add))
    mark("vector", CV)
    p.ev_sinit = CV.v


def _phase_A1(S):
    p = S.p
    op, dma, wait, mark = S.op, S.dma, S.wait, S.mark
    GW, CV, CS, CT = p.GW, p.CV, p.CS, p.CT
    n_sub = p.n_sub
    ev = {}

    wait("gpsimd", CT, p.ev_a0[("proj", n_sub - 1)])
    wait("gpsimd", CV, p.ev_sinit)
    for c in range(8):
        dma("gpsimd", GW, p.wqg_sb[:, c, :], p.wqg_h[128 * c:128 * (c + 1), :])
    for c in range(16):
        dma("gpsimd", GW, p.wo0_sb[:, c, :], p.wo0_h[128 * c:128 * (c + 1), :])
    wait("tensor", GW, GW.v)

    for s in range(n_sub):
        r = s % 2
        LD = p.LDs[r]
        ST = p.STs[r]
        if s >= 2:
            wait("sync", CV, ev[("con", s - 2)])
        wait("sync", p.STs[r], p.ev_a0[("st_v", s)])
        for c in range(8):
            dma("sync", LD, p.a1nfm[r][:, c, :],
                p.normed_s[SUB * s:SUB * (s + 1), 128 * c:128 * (c + 1)], transpose=True)
        for c in range(8):
            dma("sync", LD, p.a1kfm[r][:, c, :],
                p.ktm_s[SUB * s:SUB * (s + 1), 128 * c:128 * (c + 1)], transpose=True)
        dma("sync", LD, p.a1ktm[r], p.ktm_s[SUB * s:SUB * (s + 1), :])
        dma("sync", LD, p.a1vtm[r], p.vtm_s[SUB * s:SUB * (s + 1), :])
        dma("sync", LD, p.a1xn[r], p.xn_s[SUB * s:SUB * (s + 1), :])
        dma("sync", LD, p.ropec_t[r], p.ropecq_h[SUB * s:SUB * (s + 1), :])
        dma("sync", LD, p.ropes_t[r], p.ropesq_h[SUB * s:SUB * (s + 1), :])
        ev[("ld", s)] = LD.v

        wait("tensor", LD, ev[("ld", s)])
        if s >= 1:
            wait("tensor", CV, ev[("con", s - 1)])
            wait("tensor", CS, ev[("gsilu", s - 1)])
        if s == 0:
            wait("tensor", CV, p.ev_sinit)
            wait("tensor", CS, p.ev_a0[("vevac", n_sub - 1)])
        for c in range(8):
            op("tensor", lambda h, c=c: h.ldweights(p.a1nfm[r][:, c, :]))
            for j in range(6):
                op("tensor", lambda h, c=c, j=j: h.matmul(
                    p.PS[:, 512 * j:512 * (j + 1)], p.a1nfm[r][:, c, :],
                    p.wqg_sb[:, c, 512 * j:512 * (j + 1)],
                    start=(c == 0), stop=(c == 7)))
        mark("tensor", CT)
        ev[("qg", s)] = CT.v

        wait("vector", CT, ev[("qg", s)])
        wait("vector", LD, ev[("ld", s)])
        if s >= 1:
            wait("vector", p.LDs[1 - r], ev[("qfm", s - 1)])
            wait("vector", p.LDs[1 - r], ev[("ogfm", s - 1)])
        qps = p.PS[:, 0:1024].rearrange("p (h d) -> p h d", h=H0)
        for h0 in range(H0):
            q1, q2 = qps[:, h0, 0:128], qps[:, h0, 128:256]
            o1 = p.a1qtm[:, h0, 0:128]
            o2 = p.a1qtm[:, h0, 128:256]
            op("vector", lambda h, q1=q1: h.tensor_tensor(out=p.tmp1, in0=q1, in1=p.ropec_t[r], op=ALU.mult))
            op("vector", lambda h, q2=q2: h.tensor_tensor(out=p.tmp2, in0=q2, in1=p.ropes_t[r], op=ALU.mult))
            op("vector", lambda h, o1=o1: h.tensor_tensor(out=o1, in0=p.tmp1, in1=p.tmp2, op=ALU.subtract))
            op("vector", lambda h, q2=q2: h.tensor_tensor(out=p.tmp1, in0=q2, in1=p.ropec_t[r], op=ALU.mult))
            op("vector", lambda h, q1=q1: h.tensor_tensor(out=p.tmp2, in0=q1, in1=p.ropes_t[r], op=ALU.mult))
            op("vector", lambda h, o2=o2: h.tensor_tensor(out=o2, in0=p.tmp1, in1=p.tmp2, op=ALU.add))
        for h0 in range(H0):
            op("vector", lambda h, h0=h0: h.tensor_scalar(
                out=p.a1qctm[:, h0, :], in0=p.a1qtm[:, h0, :],
                scalar1=p.crosst_sb[:, h0:h0 + 1], scalar2=None, op0=ALU.mult))
        mark("vector", CV)
        ev[("qrope", s)] = CV.v

        wait("scalar", CT, ev[("qg", s)])
        op("scalar", lambda h: h.activation(out=p.a1gs, in_=p.PS[:, 1024:3072],
                                            func=AF.Silu), inc=(CS, 1))
        ev[("gsilu", s)] = CS.v

        wait("sync", CV, ev[("qrope", s)])
        qflat = p.a1qtm.rearrange("p h d -> p (h d)")
        qcflat = p.a1qctm.rearrange("p h d -> p (h d)")
        for c in range(8):
            dma("sync", LD, p.a1qfm[:, c, :], qflat[:, 128 * c:128 * (c + 1)], transpose=True)
        for c in range(8):
            dma("sync", LD, p.a1qcfm[:, c, :], qcflat[:, 128 * c:128 * (c + 1)], transpose=True)
        ev[("qfm", s)] = LD.v

        wait("vector", LD, ev[("ld", s)])
        ktmv = p.a1ktm[r].rearrange("p (hh d) -> p hh d", hh=H0)
        for h0 in range(H0):
            op("vector", lambda h, h0=h0: h.tensor_scalar(
                out=p.a0ktil[:, h0, :], in0=ktmv[:, h0, :],
                scalar1=p.kdect_sb[:, h0:h0 + 1], scalar2=None, op0=ALU.mult))
        mark("vector", CV)
        ev[("ktil", s)] = CV.v

        wait("tensor", LD, ev[("qfm", s)])
        for h0 in range(H0):
            for half in range(2):
                c = 2 * h0 + half
                op("tensor", lambda h, c=c: h.ldweights(p.a1kfm[r][:, c, :]))
                op("tensor", lambda h, c=c, h0=h0, half=half: h.matmul(
                    p.PS[:, 3072 + 128 * h0:3072 + 128 * (h0 + 1)],
                    p.a1kfm[r][:, c, :], p.a1qfm[:, c, :],
                    start=(half == 0), stop=(half == 1)))
            mark("tensor", CT)
            ev[("attnT", s, h0)] = CT.v
            wait("vector", CT, ev[("attnT", s, h0)])
            op("vector", lambda h, h0=h0: h.tensor_tensor(
                out=p.a1at[:, h0, :], in0=p.PS[:, 3072 + 128 * h0:3072 + 128 * (h0 + 1)],
                in1=p.dmask_sb[:, 128 * h0:128 * (h0 + 1)], op=ALU.mult), inc=(CV, 1))
            ev[("atm", s, h0)] = CV.v
            wait("tensor", CV, ev[("atm", s, h0)])
            if h0 > 0:
                wait("tensor", CV, ev[("og", s, h0 - 1)])
            op("tensor", lambda h, h0=h0: h.ldweights(p.a1at[:, h0, :]))
            op("tensor", lambda h, h0=h0: h.matmul(
                p.PS[:, 3584:4096], p.a1at[:, h0, :],
                p.a1vtm[r][:, 512 * h0:512 * (h0 + 1)], start=True, stop=False))
            for half in range(2):
                c = 2 * h0 + half
                op("tensor", lambda h, c=c: h.ldweights(p.a1qcfm[:, c, :]))
                op("tensor", lambda h, c=c, h0=h0, half=half: h.matmul(
                    p.PS[:, 3584:4096], p.a1qcfm[:, c, :],
                    p.S_sb[:, h0, half, :], start=False, stop=(half == 1)))
            mark("tensor", CT)
            ev[("o", s, h0)] = CT.v
            wait("scalar", CT, ev[("o", s, h0)])
            op("scalar", lambda h: h.activation(out=p.trash[:, 0:512], in_=p.PS[:, 3584:4096],
                                                func=AF.Square, accum_out=p.sums[:, 2:3]),
               inc=(CS, 1))
            wait("scalar", CS, CS.v)
            op("scalar", lambda h: h.activation(out=p.mu2, in_=p.sums[:, 2:3],
                                                func=AF.Sqrt, scale=1.0 / 512.0,
                                                bias=EPS0), inc=(CS, 1))
            ev[("osd", s, h0)] = CS.v
            wait("vector", CS, ev[("osd", s, h0)])
            wait("vector", CS, ev[("gsilu", s)])
            op("vector", lambda h: h.reciprocal(out=p.mu2, in_=p.mu2), inc=(CV, 1))
            wait("vector", CV, CV.v)
            op("vector", lambda h: h.scalar_tensor_tensor(
                out=p.tmp3, in0=p.PS[:, 3584:4096], scalar=p.mu2,
                in1=p.gnw_sb, op0=ALU.mult, op1=ALU.mult))
            op("vector", lambda h, h0=h0: h.tensor_tensor(
                out=p.a1og[:, h0, :], in0=p.tmp3,
                in1=p.a1gs[:, 512 * h0:512 * (h0 + 1)], op=ALU.mult), inc=(CV, 1))
            ev[("og", s, h0)] = CV.v

        wait("tensor", CV, ev[("ktil", s)])
        for h0 in range(H0):
            wait("tensor", CV, ev[("og", s, h0)])
            if h0 > 0:
                wait("tensor", CV, ev[("sst", s, h0 - 1)])
            elif s >= 1:
                wait("tensor", CV, ev[("sst", s - 1, H0 - 1)])
            for half in range(2):
                op("tensor", lambda h, h0=h0, half=half: h.ldweights(
                    p.a0ktil[:, h0, 128 * half:128 * (half + 1)]))
                op("tensor", lambda h, h0=h0, half=half: h.matmul(
                    p.PS[:, 512 * half:512 * (half + 1)],
                    p.a0ktil[:, h0, 128 * half:128 * (half + 1)],
                    p.a1vtm[r][:, 512 * h0:512 * (h0 + 1)], start=True, stop=True))
            mark("tensor", CT)
            ev[("Y", s, h0)] = CT.v
            wait("vector", CT, ev[("Y", s, h0)])
            for half in range(2):
                op("vector", lambda h, h0=h0, half=half: h.scalar_tensor_tensor(
                    out=p.S_sb[:, h0, half, :], in0=p.S_sb[:, h0, half, :],
                    scalar=GC[h0], in1=p.PS[:, 512 * half:512 * (half + 1)],
                    op0=ALU.mult, op1=ALU.add))
            mark("vector", CV)
            ev[("sst", s, h0)] = CV.v

        wait("sync", CV, ev[("og", s, H0 - 1)])
        if s >= 1:
            wait("sync", CT, ev[("wo", s - 1)])
        ogflat = p.a1og.rearrange("p h d -> p (h d)")
        for c in range(16):
            dma("sync", LD, p.a1ogfm[:, c, :], ogflat[:, 128 * c:128 * (c + 1)], transpose=True)
        ev[("ogfm", s)] = LD.v
        wait("tensor", LD, ev[("ogfm", s)])
        if s >= 1:
            wait("tensor", CV, ev[("s0", s - 1)])
        for c in range(16):
            op("tensor", lambda h, c=c: h.ldweights(p.a1ogfm[:, c, :]))
            for j in range(2):
                op("tensor", lambda h, c=c, j=j: h.matmul(
                    p.PS[:, 1024 + 512 * j:1024 + 512 * (j + 1)],
                    p.a1ogfm[:, c, :], p.wo0_sb[:, c, 512 * j:512 * (j + 1)],
                    start=(c == 0), stop=(c == 15)))
        mark("tensor", CT)
        ev[("wo", s)] = CT.v
        wait("vector", CT, ev[("wo", s)])
        if s >= 2:
            wait("vector", ST, ev[("st_s0", s - 2)])
        op("vector", lambda h: h.tensor_tensor(out=p.a1s0[r], in0=p.PS[:, 1024:2048],
                                               in1=p.a1xn[r], op=ALU.add), inc=(CV, 1))
        ev[("s0", s)] = CV.v
        wait("gpsimd", CV, ev[("s0", s)])
        dma("gpsimd", ST, p.s0_s[SUB * s:SUB * (s + 1), :], p.a1s0[r])
        ev[("st_s0", s)] = ST.v
        mark("vector", CV)
        ev[("con", s)] = CV.v

    p.ev_a1 = ev


def _fence(S):
    p = S.p
    for e in ("sync", "gpsimd", "vector", "scalar", "tensor"):
        for c in (p.LDs[0], p.LDs[1], p.GW, p.STs[0], p.STs[1],
                  p.CV, p.CS, p.CT, p.CP):
            S.wait(e, c, c.v)


def _wquant_fp4(S, w_h, wfm_sb, nrow, ncol):
    p = S.p
    t = p.qt
    op, dma, wait, mark = S.op, S.dma, S.wait, S.mark
    CV = p.CV
    guard = [0, 0]
    trv = [p.LDs[0].v, p.LDs[1].v]
    for i in range(nrow // 128):
        r = i % 2
        LD = p.LDs[r]
        if i >= 2:
            wait("sync", CV, guard[r])
        dma("sync", LD, t.wslab[r][:, 0:ncol], w_h[128 * i:128 * (i + 1), :])
        ldv = LD.v
        wait("vector", LD, ldv)
        wait("vector", p.LDs[1 - r], trv[1 - r])
        cvs = []
        for (a, wdt) in _islices(ncol, 1408):
            cvs.append(_emit_fp4(S, t.wslab[r][:, a:a + wdt], t.wqb[:, a:a + wdt], wdt))
        wait("sync", CV, cvs[-1])
        for c in range(ncol // 128):
            dma("sync", LD, wfm_sb[:, c, 128 * i:128 * (i + 1)],
                t.wqb[:, 128 * c:128 * (c + 1)], transpose=True)
        trv[r] = LD.v
        guard[r] = CV.v
    wait("vector", p.LDs[0], p.LDs[0].v)
    wait("vector", p.LDs[1], p.LDs[1].v)
    return 0


def _wquant_int8(S, w_h, wfm_sb, nrow, ncol):
    p = S.p
    t = p.qt
    op, dma, wait, mark = S.op, S.dma, S.wait, S.mark
    CV = p.CV
    guard = [0, 0]
    trv = [p.LDs[0].v, p.LDs[1].v]
    for i in range(nrow // 128):
        r = i % 2
        LD = p.LDs[r]
        if i >= 2:
            wait("sync", CV, guard[r])
        dma("sync", LD, t.wslab[r][:, 0:ncol], w_h[128 * i:128 * (i + 1), :])
        ldv = LD.v
        wait("vector", LD, ldv)
        wait("vector", p.LDs[1 - r], trv[1 - r])
        op("vector", lambda h: h.reduce_max(out=p.rm, in_=t.wslab[r][:, 0:ncol],
                                            axis=AX.X, apply_absolute_value=True))
        op("vector", lambda h: h.reciprocal(out=p.rr1, in_=p.rm))
        op("vector", lambda h: h.tensor_scalar(out=p.rr1, in0=p.rr1, scalar1=127.0,
                                               scalar2=None, op0=ALU.mult))
        op("vector", lambda h: h.tensor_scalar(out=p.rm, in0=p.rm, scalar1=1.0 / 127.0,
                                               scalar2=None, op0=ALU.mult), inc=(CV, 1))
        wait("vector", CV, CV.v)
        for (a, wdt) in _islices(ncol, 1408):
            op("vector", lambda h, a=a, wdt=wdt: h.tensor_scalar(
                out=t.w[:, 0:wdt], in0=t.wslab[r][:, a:a + wdt],
                scalar1=p.rr1, scalar2=None, op0=ALU.mult), inc=(CV, 1))
            _emit_round_clip(S, t.w[:, 0:wdt], t.y0[:, 0:wdt], wdt)
            op("vector", lambda h, a=a, wdt=wdt: h.tensor_scalar(
                out=t.wqb[:, a:a + wdt], in0=t.y0[:, 0:wdt],
                scalar1=p.rm, scalar2=None, op0=ALU.mult), inc=(CV, 1))
        cv = CV.v
        wait("sync", CV, cv)
        for c in range(ncol // 128):
            dma("sync", LD, wfm_sb[:, c, 128 * i:128 * (i + 1)],
                t.wqb[:, 128 * c:128 * (c + 1)], transpose=True)
        trv[r] = LD.v
        guard[r] = CV.v
    wait("vector", p.LDs[0], p.LDs[0].v)
    wait("vector", p.LDs[1], p.LDs[1].v)
    return 0


def _phase_B1(S):
    p = S.p
    op, dma, wait, mark = S.op, S.dma, S.wait, S.mark
    GW, CV, CS, CT = p.GW, p.CV, p.CS, p.CT
    n_sub = p.n_sub
    ev = {}
    _fence(S)
    _wquant_fp4(S, p.wg0_h, p.wg_fm, I0, D)
    _wquant_fp4(S, p.wu0_h, p.wu_fm, I0, D)
    wait("tensor", p.LDs[0], p.LDs[0].v)
    wait("tensor", p.LDs[1], p.LDs[1].v)

    for s in range(n_sub):
        r = s % 2
        LD = p.LDs[r]
        ST = p.STs[r]
        if s >= 1:
            wait("sync", CV, ev[("xq", s - 1)])
        dma("sync", LD, p.bxn[r], p.xn_s[SUB * s:SUB * (s + 1), :])
        ev[("ld", s)] = LD.v
        wait("vector", LD, ev[("ld", s)])
        wait("scalar", LD, ev[("ld", s)])
        if s >= 1:
            wait("vector", CT, ev[("up", s - 1)])
            wait("vector", p.LDs[1 - r], ev[("xqfm", s - 1)])
        cv = _emit_fp4(S, p.bxn[r], p.bxq[:, 0:D], D)
        ev[("xq", s)] = cv
        wait("sync", CV, cv)
        if s >= 1:
            wait("sync", CT, ev[("up", s - 1)])
        for c in range(8):
            dma("sync", LD, p.bxqfm[:, c, :], p.bxq[:, 128 * c:128 * (c + 1)],
                transpose=True)
        ev[("xqfm", s)] = LD.v
        wait("tensor", LD, ev[("xqfm", s)])
        if s >= 1:
            wait("tensor", CV, ev[("hq", s - 1)])
        for c in range(8):
            op("tensor", lambda h, c=c: h.ldweights(p.bxqfm[:, c, :]))
            for (a, wdt) in _islices(I0, 512):
                op("tensor", lambda h, c=c, a=a, wdt=wdt: h.matmul(
                    p.PS[:, a:a + wdt], p.bxqfm[:, c, :], p.wg_fm[:, c, a:a + wdt],
                    start=(c == 0), stop=(c == 7)))
        mark("tensor", CT)
        ev[("gate", s)] = CT.v
        wait("scalar", CT, ev[("gate", s)])
        op("scalar", lambda h: h.activation(out=p.bgs, in_=p.PS[:, 0:I0],
                                            func=AF.Silu), inc=(CS, 1))
        ev[("gs", s)] = CS.v
        wait("tensor", CS, ev[("gs", s)])
        for c in range(8):
            op("tensor", lambda h, c=c: h.ldweights(p.bxqfm[:, c, :]))
            for (a, wdt) in _islices(I0, 512):
                op("tensor", lambda h, c=c, a=a, wdt=wdt: h.matmul(
                    p.PS[:, a:a + wdt], p.bxqfm[:, c, :], p.wu_fm[:, c, a:a + wdt],
                    start=(c == 0), stop=(c == 7)))
        mark("tensor", CT)
        ev[("up", s)] = CT.v
        wait("vector", CT, ev[("up", s)])
        op("vector", lambda h: h.tensor_tensor(out=p.bgs, in0=p.PS[:, 0:I0],
                                               in1=p.bgs, op=ALU.mult), inc=(CV, 1))
        ev[("h", s)] = CV.v
        if s >= 1:
            wait("vector", p.STs[1 - r], ev[("st_hq", s - 1)])
        for (a, wdt) in _islices(I0, 1408):
            cv = _emit_fp4(S, p.bgs[:, a:a + wdt], p.bhq[:, a:a + wdt], wdt)
        ev[("hq", s)] = cv
        wait("gpsimd", CV, ev[("hq", s)])
        dma("gpsimd", ST, p.hq0_s[SUB * s:SUB * (s + 1), :], p.bhq[:, 0:I0])
        ev[("st_hq", s)] = ST.v

    p.ev_b1 = ev


def _phase_B2(S):
    p = S.p
    op, dma, wait, mark = S.op, S.dma, S.wait, S.mark
    GW, CV, CS, CT = p.GW, p.CV, p.CS, p.CT
    n_sub = p.n_sub
    ev = {}
    _fence(S)
    _wquant_fp4(S, p.wd0_h, p.wd_fm, D, I0)
    wait("tensor", p.LDs[0], p.LDs[0].v)
    wait("tensor", p.LDs[1], p.LDs[1].v)

    for s in range(n_sub):
        r = s % 2
        LD = p.LDs[r]
        ST = p.STs[r]
        if s >= 1:
            wait("sync", CV, ev[("x1", s - 1)])
            wait("sync", CT, ev[("down", s - 1)])
        for c in range(22):
            dma("sync", LD, p.bhqfm[:, c, :],
                p.hq0_s[SUB * s:SUB * (s + 1), 128 * c:128 * (c + 1)], transpose=True)
        dma("sync", LD, p.bx[r], p.x_h[SUB * s:SUB * (s + 1), :])
        dma("sync", LD, p.bs0[r], p.s0_s[SUB * s:SUB * (s + 1), :])
        ev[("ld", s)] = LD.v
        wait("tensor", LD, ev[("ld", s)])
        if s >= 1:
            wait("tensor", CS, ev[("g0", s - 1)])
        for c in range(22):
            op("tensor", lambda h, c=c: h.ldweights(p.bhqfm[:, c, :]))
            for j in range(2):
                op("tensor", lambda h, c=c, j=j: h.matmul(
                    p.PS[:, 512 * j:512 * (j + 1)], p.bhqfm[:, c, :],
                    p.wd_fm[:, c, 512 * j:512 * (j + 1)],
                    start=(c == 0), stop=(c == 21)))
        mark("tensor", CT)
        ev[("down", s)] = CT.v
        wait("scalar", CT, ev[("down", s)])
        op("scalar", lambda h: h.activation(out=p.bg0, in_=p.PS[:, 0:1024],
                                            func=AF.Silu), inc=(CS, 1))
        ev[("g0", s)] = CS.v
        wait("vector", CS, ev[("g0", s)])
        wait("vector", LD, ev[("ld", s)])
        if s >= 1:
            wait("vector", p.STs[1 - r], ev[("st", s - 1)])
        op("vector", lambda h: h.tensor_tensor(out=p.bg0, in0=p.bs0[r], in1=p.bg0,
                                               op=ALU.mult))
        op("vector", lambda h: h.tensor_tensor(out=p.bx1[r], in0=p.bg0, in1=p.bx[r],
                                               op=ALU.add), inc=(CV, 1))
        ev[("x1", s)] = CV.v
        wait("gpsimd", CV, ev[("x1", s)])
        dma("gpsimd", ST, p.x1_s[SUB * s:SUB * (s + 1), :], p.bx1[r])
        ev[("st", s)] = ST.v

    p.ev_b2 = ev


def _phase_C(S):
    p = S.p
    op, dma, wait, mark = S.op, S.dma, S.wait, S.mark
    GW, CV, CS, CT = p.GW, p.CV, p.CS, p.CT
    n_sub = p.n_sub
    ev = {}
    _fence(S)
    dma("gpsimd", GW, p.wkv1_sb.rearrange("p a b -> p (a b)"), p.wkv1_h)
    wait("tensor", GW, GW.v)
    op("vector", lambda h: h.memset(p.PS[:, 2048:4096], 0.0))
    op("vector", lambda h: h.memset(p.cvext[0].rearrange("p a b -> p (a b)"), 0.0))
    op("vector", lambda h: h.memset(p.cvext[1].rearrange("p a b -> p (a b)"), 0.0))
    op("vector", lambda h: h.memset(p.cvext[0][:, :, 64:65].rearrange("p a b -> p (a b)"), 1.0))
    op("vector", lambda h: h.memset(p.cvext[1][:, :, 64:65].rearrange("p a b -> p (a b)"), 1.0))
    op("vector", lambda h: h.memset(p.mxg, 0.0), inc=(CV, 1))
    cstop = _STOP[0] if (_STOP[0] or "").startswith("C.") else None
    if cstop == "C.pre":
        return

    for s in range(n_sub):
        r = s % 2
        LD = p.LDs[r]
        ST = p.STs[r]
        if s >= 1:
            wait("sync", CV, ev[("xn1", s - 1)])
        dma("sync", LD, p.cx1[r], p.x1_s[SUB * s:SUB * (s + 1), :])
        ev[("ld", s)] = LD.v
        wait("vector", LD, ev[("ld", s)])
        wait("scalar", LD, ev[("ld", s)])
        if s >= 1:
            wait("vector", p.STs[1 - r], ev[("st", s - 1)])
            wait("vector", CS, ev[("xn1b", s - 1)])
            if ("xfm", s - 1) in ev:
                wait("vector", p.LDs[1 - r], ev[("xfm", s - 1)])
        _emit_ln(S, p.cx1[r], p.cxn1[r])
        ev[("xn1", s)] = CV.v
        wait("scalar", CV, ev[("xn1", s)])
        if ("xfm", s - 1) in ev:
            wait("scalar", p.LDs[1 - r], ev[("xfm", s - 1)])
        op("scalar", lambda h: h.activation(out=p.cxn1b[r], in_=p.cxn1[r],
                                            func=AF.Copy), inc=(CS, 1))
        ev[("xn1b", s)] = CS.v
        wait("gpsimd", CV, ev[("xn1", s)])
        dma("gpsimd", ST, p.xn1_s[SUB * s:SUB * (s + 1), :], p.cxn1[r])
        wait("gpsimd", CS, ev[("xn1b", s)])
        dma("gpsimd", ST, p.xn1b_s[SUB * s:SUB * (s + 1), :], p.cxn1b[r])
        ev[("st", s)] = ST.v
        op("vector", lambda h: h.reduce_max(out=p.mx1, in_=p.cxn1[r], axis=AX.X,
                                            apply_absolute_value=True))
        op("vector", lambda h: h.tensor_tensor(out=p.mxg, in0=p.mxg, in1=p.mx1,
                                               op=ALU.max), inc=(CV, 1))
        if cstop == "C.ln":
            continue
        wait("sync", CS, ev[("xn1b", s)])
        if ("kv", s - 1) in ev:
            wait("sync", CT, ev[("kv", s - 1)])
        for c in range(8):
            dma("sync", LD, p.cxfm[:, c, :], p.cxn1b[r][:, 128 * c:128 * (c + 1)],
                transpose=True)
        ev[("xfm", s)] = LD.v
        if cstop == "C.xfm":
            continue
        wait("tensor", LD, ev[("xfm", s)])
        if ("vex", s - 1) in ev:
            wait("tensor", CV, ev[("vex", s - 1)])
        elif ("phik", s - 1) in ev:
            wait("tensor", CV, ev[("phik", s - 1)])
        for pr in range(8):
            op("tensor", lambda h, pr=pr: h.ldweights(p.cxfm[:, pr, :]))
            op("tensor", lambda h, pr=pr: h.matmul(
                p.PS[:, 256 * pr:256 * (pr + 1)],
                p.cxfm[:, pr, :], p.wkv1_sb[:, pr, :],
                start=(pr % 2 == 0), stop=(pr % 2 == 1)))
        mark("tensor", CT)
        ev[("kvproj", s)] = CT.v
        if cstop == "C.proj":
            continue
        kview = p.PS[:, 0:2048].rearrange("p (hd kv e) -> p hd kv e", kv=2, e=64)
        kq = kview[:, :, 0, :]
        vq = kview[:, :, 1, :]
        ckn3 = p.ckn.rearrange("p (a b) -> p a b", b=64)
        cke3 = p.cke.rearrange("p (a b) -> p a b", b=64)
        cphik3 = p.cphik.rearrange("p (a b) -> p a b", b=64)
        wait("vector", CT, ev[("kvproj", s)])
        op("vector", lambda h: h.tensor_scalar(out=ckn3, in0=kq, scalar1=0.0,
                                               scalar2=None, op0=ALU.min), inc=(CV, 1))
        ev[("kneg", s)] = CV.v
        if cstop == "C.kneg":
            continue
        wait("scalar", CV, ev[("kneg", s)])
        op("scalar", lambda h: h.activation(out=p.cke, in_=p.ckn, func=AF.Exp),
           inc=(CS, 1))
        ev[("kexp", s)] = CS.v
        if cstop == "C.kexp":
            continue
        wait("vector", CS, ev[("kexp", s)])
        if ("kv", s - 1) in ev:
            wait("vector", CT, ev[("kv", s - 1)])
        op("vector", lambda h: h.scalar_tensor_tensor(
            out=cphik3, in0=kq, scalar=0.0, in1=cke3, op0=ALU.max, op1=ALU.add),
           inc=(CV, 1))
        ev[("phik", s)] = CV.v
        if cstop == "C.stt":
            continue
        if ("kv", s - 2) in ev:
            wait("vector", CT, ev[("kv", s - 2)])
        op("vector", lambda h: h.tensor_copy(
            out=p.cvext[r][:, :, 0:64], in_=vq), inc=(CV, 1))
        ev[("vex", s)] = CV.v
        if cstop == "C.phik":
            continue
        wait("tensor", CV, ev[("vex", s)])
        phv = p.cphik.rearrange("p (hd e) -> p hd e", e=64)
        for hd in range(16):
            col = 2048 + 128 * hd
            op("tensor", lambda h, hd=hd: h.ldweights(phv[:, hd, :]))
            op("tensor", lambda h, hd=hd, col=col, s=s: h.matmul(
                p.PS[0:64, col:col + 72], phv[:, hd, :], p.cvext[r][:, hd, :],
                start=(s == 0 and hd % 4 == 0),
                stop=(s == n_sub - 1 and hd % 4 == 3)))
        mark("tensor", CT)
        ev[("kv", s)] = CT.v
        mark("vector", CV)
        ev[("con", s)] = CV.v

    p.ev_c = ev


def _x2_kv_max(S):
    p = S.p
    op, dma, wait, mark = S.op, S.dma, S.wait, S.mark
    GW, CV, CS, CT, CP = p.GW, p.CV, p.CS, p.CT, p.CP
    _fence(S)
    wait("vector", CT, p.ev_c[("kv", p.n_sub - 1)])
    op("vector", lambda h: h.tensor_copy(out=p.kvsb, in_=p.PS[:, 2048:4096]),
       inc=(CV, 1))
    wait("gpsimd", CV, CV.v)
    dma("gpsimd", p.STs[0], p.cckv_in, p.kvsb)
    wait("tensor", CV, CV.v)
    op("tensor", lambda h: h.transpose(p.PS[0:1, 0:128], p.mxg, p.identf_sb))
    mark("tensor", CT)
    wait("vector", CT, CT.v)
    op("vector", lambda h: h.reduce_max(out=p.mxs, in_=p.PS[0:1, 0:128], axis=AX.X),
       inc=(CV, 1))
    wait("tensor", CV, CV.v)
    op("tensor", lambda h: h.matmul(p.PS[:, 512:513], p.onesf_sb, p.mxs,
                                    start=True, stop=True))
    mark("tensor", CT)
    wait("vector", CT, CT.v)
    op("vector", lambda h: h.tensor_copy(out=p.mx1, in_=p.PS[:, 512:513]),
       inc=(CV, 1))
    wait("gpsimd", CV, CV.v)
    dma("gpsimd", p.STs[0], p.ccm1_in, p.mx1)
    wait("gpsimd", p.STs[0], p.STs[0].v)
    op("gpsimd", lambda h: h.collective_compute(
        "AllReduce", ALU.add, replica_groups=[[0, 1, 2, 3], [4, 5, 6, 7]],
        ins=[p.cckv_in.opt()], outs=[p.cckv_out.opt()]), inc=(CP, 1))
    op("gpsimd", lambda h: h.collective_compute(
        "AllReduce", ALU.max, replica_groups=[[0, 1, 2, 3, 4, 5, 6, 7]],
        ins=[p.ccm1_in.opt()], outs=[p.ccm1_out.opt()]), inc=(CP, 1))
    wait("gpsimd", CP, CP.v)
    dma("gpsimd", GW, p.kvsb, p.cckv_out)
    dma("gpsimd", GW, p.mx1, p.ccm1_out)
    wait("vector", GW, GW.v)
    op("vector", lambda h: h.tensor_copy(out=p.kvb, in_=p.kvsb), inc=(CV, 1))
    wait("scalar", GW, GW.v)
    op("scalar", lambda h: h.activation(out=p.clip_sg, in_=p.clip_sb,
                                        func=AF.Sigmoid), inc=(CS, 1))
    wait("tensor", CS, CS.v)
    op("tensor", lambda h: h.matmul(p.PS[:, 520:522], p.onesf_sb, p.clip_sg,
                                    start=True, stop=True))
    mark("tensor", CT)
    wait("vector", CT, CT.v)
    op("vector", lambda h: h.tensor_tensor(out=p.rsx, in0=p.mx1, in1=p.PS[:, 520:521],
                                           op=ALU.mult))
    op("vector", lambda h: h.tensor_scalar(out=p.rsx, in0=p.rsx, scalar1=1.0 / 127.0,
                                           scalar2=None, op0=ALU.mult), inc=(CV, 1))
    wait("vector", CV, CV.v)
    op("vector", lambda h: h.tensor_copy(out=p.sx, in_=p.rsx), inc=(CV, 1))
    wait("vector", CV, CV.v)
    op("vector", lambda h: h.reciprocal(out=p.rsx, in_=p.rsx))
    op("vector", lambda h: h.tensor_copy(out=p.sgd, in_=p.PS[:, 521:522]),
       inc=(CV, 1))
    p.ev_rsx = CV.v


def _phase_D1(S):
    p = S.p
    op, dma, wait, mark = S.op, S.dma, S.wait, S.mark
    GW, CV, CS, CT = p.GW, p.CV, p.CS, p.CT
    n_sub = p.n_sub
    ev = {}
    _fence(S)
    dma("gpsimd", GW, p.wq1_sb.rearrange("p a b -> p (a b)"), p.wq1_h)
    for c in range(8):
        dma("gpsimd", GW, p.wo1_sb[:, c, :], p.wo1_h[128 * c:128 * (c + 1), :])
    wait("tensor", GW, GW.v)

    for s in range(n_sub):
        r = s % 2
        LD = p.LDs[r]
        ST = p.STs[r]
        if s >= 2:
            wait("sync", CV, ev[("con", s - 2)])
        for c in range(8):
            dma("sync", LD, p.cxfm[:, c, :],
                p.xn1b_s[SUB * s:SUB * (s + 1), 128 * c:128 * (c + 1)], transpose=True)
        ev[("ld", s)] = LD.v

        wait("tensor", LD, ev[("ld", s)])
        if s >= 1:
            wait("tensor", CV, ev[("phiq", s - 1)])
            wait("tensor", CS, ev[("s1evac", s - 1)])
        for hd in range(16):
            op("tensor", lambda h, hd=hd: h.ldweights(p.wq1_sb[:, hd, :]))
            op("tensor", lambda h, hd=hd: h.matmul(
                p.PS[0:64, 128 * hd:128 * (hd + 1)],
                p.wq1_sb[:, hd, :], p.cxfm[:, hd // 2, :],
                start=(hd % 4 == 0), stop=(hd % 4 == 3)))
        mark("tensor", CT)
        ev[("qproj", s)] = CT.v
        wait("vector", CT, ev[("qproj", s)])
        op("vector", lambda h: h.tensor_scalar(out=p.dknq[0:64, :],
                                               in0=p.PS[0:64, 0:2048],
                                               scalar1=0.0, scalar2=None, op0=ALU.min),
           inc=(CV, 1))
        ev[("qneg", s)] = CV.v
        wait("scalar", CV, ev[("qneg", s)])
        op("scalar", lambda h: h.activation(out=p.dkeq[0:64, :], in_=p.dknq[0:64, :],
                                            func=AF.Exp), inc=(CS, 1))
        ev[("qexp", s)] = CS.v
        wait("vector", CS, ev[("qexp", s)])
        op("vector", lambda h: h.scalar_tensor_tensor(
            out=p.dphiq[0:64, :], in0=p.PS[0:64, 0:2048], scalar=0.0,
            in1=p.dkeq[0:64, :], op0=ALU.max, op1=ALU.add), inc=(CV, 1))
        ev[("phiq", s)] = CV.v

        wait("tensor", CV, ev[("phiq", s)])
        if s >= 1:
            wait("tensor", CV, ev[("comb", s - 1)])
        for hd in range(16):
            lhs = p.dphiq[0:64, 128 * hd:128 * (hd + 1)]
            op("tensor", lambda h, lhs=lhs: h.ldweights(lhs))
            op("tensor", lambda h, lhs=lhs, hd=hd: h.matmul(
                p.PS[:, 2048 + 128 * hd:2048 + 128 * hd + 72],
                lhs, p.kvb3[0:64, hd, 0:72],
                start=(hd % 4 == 0), stop=(hd % 4 == 3)))
        mark("tensor", CT)
        ev[("ctx", s)] = CT.v
        wait("vector", CT, ev[("ctx", s)])
        if s >= 1:
            wait("vector", p.LDs[1 - r], ev[("combfm", s - 1)])
        ctxv = p.PS[:, 2048:4096].rearrange("p (hd e) -> p hd e", e=128)
        op("vector", lambda h: h.tensor_scalar(
            out=p.drz, in0=ctxv[:, :, 64:65].rearrange("p hd o -> p (hd o)"),
            scalar1=EPS1, scalar2=None, op0=ALU.add))
        op("vector", lambda h: h.reciprocal(out=p.drz, in_=p.drz), inc=(CV, 1))
        wait("vector", CV, CV.v)
        for hd in range(16):
            op("vector", lambda h, hd=hd: h.tensor_scalar(
                out=p.dcomb[:, hd, :], in0=ctxv[:, hd, 0:64],
                scalar1=p.drz[:, hd:hd + 1], scalar2=None, op0=ALU.mult))
        mark("vector", CV)
        ev[("comb", s)] = CV.v
        wait("sync", CV, ev[("comb", s)])
        if s >= 1:
            wait("sync", CT, ev[("s1mm", s - 1)])
        combf = p.dcomb.rearrange("p hd e -> p (hd e)")
        for c in range(8):
            dma("sync", LD, p.dcombfm[:, c, :], combf[:, 128 * c:128 * (c + 1)],
                transpose=True)
        ev[("combfm", s)] = LD.v
        wait("tensor", LD, ev[("combfm", s)])
        if s >= 1:
            wait("tensor", CS, ev[("s1evac", s - 1)])
        for c in range(8):
            op("tensor", lambda h, c=c: h.ldweights(p.dcombfm[:, c, :]))
            for j in range(2):
                op("tensor", lambda h, c=c, j=j: h.matmul(
                    p.PS[:, 1024 + 512 * j:1024 + 512 * (j + 1)],
                    p.dcombfm[:, c, :], p.wo1_sb[:, c, 512 * j:512 * (j + 1)],
                    start=(c == 0), stop=(c == 7)))
        mark("tensor", CT)
        ev[("s1mm", s)] = CT.v
        wait("scalar", CT, ev[("s1mm", s)])
        if s >= 2:
            wait("scalar", ST, ev[("st", s - 2)])
        op("scalar", lambda h: h.activation(out=p.ds1[r], in_=p.PS[:, 1024:2048],
                                            func=AF.Copy), inc=(CS, 1))
        ev[("s1evac", s)] = CS.v
        wait("gpsimd", CS, ev[("s1evac", s)])
        dma("gpsimd", ST, p.s1_s[SUB * s:SUB * (s + 1), :], p.ds1[r])
        ev[("st", s)] = ST.v
        mark("vector", CV)
        ev[("con", s)] = CV.v

    p.ev_d1 = ev


def _phase_D2(S):
    p = S.p
    op, dma, wait, mark = S.op, S.dma, S.wait, S.mark
    GW, CV, CS, CT = p.GW, p.CV, p.CS, p.CT
    n_sub = p.n_sub
    ev = {}
    _fence(S)
    _wquant_int8(S, p.wg1_h, p.wg_fm, I0, D)
    _wquant_int8(S, p.wu1_h, p.wu_fm, I0, D)
    wait("tensor", p.LDs[0], p.LDs[0].v)
    wait("tensor", p.LDs[1], p.LDs[1].v)
    wait("vector", CV, p.ev_rsx)
    op("vector", lambda h: h.memset(p.mxg, 0.0), inc=(CV, 1))

    for s in range(n_sub):
        r = s % 2
        LD = p.LDs[r]
        ST = p.STs[r]
        if s >= 1:
            wait("sync", CV, ev[("xq1", s - 1)])
        dma("sync", LD, p.dxn1[r], p.xn1_s[SUB * s:SUB * (s + 1), :])
        ev[("ld", s)] = LD.v
        wait("vector", LD, ev[("ld", s)])
        if s >= 1:
            wait("vector", CT, ev[("up", s - 1)])
            wait("vector", p.LDs[1 - r], ev[("xqfm", s - 1)])
        op("vector", lambda h: h.tensor_scalar(out=p.qt.w[:, 0:D], in0=p.dxn1[r],
                                               scalar1=p.rsx, scalar2=None,
                                               op0=ALU.mult), inc=(CV, 1))
        cv = _emit_round_clip(S, p.qt.w[:, 0:D], p.bxq[:, 0:D], D)
        ev[("xq1", s)] = cv
        wait("sync", CV, cv)
        if s >= 1:
            wait("sync", CT, ev[("up", s - 1)])
        for c in range(8):
            dma("sync", LD, p.bxqfm[:, c, :], p.bxq[:, 128 * c:128 * (c + 1)],
                transpose=True)
        ev[("xqfm", s)] = LD.v
        wait("tensor", LD, ev[("xqfm", s)])
        if s >= 1:
            wait("tensor", CV, ev[("h1", s - 1)])
        for c in range(8):
            op("tensor", lambda h, c=c: h.ldweights(p.bxqfm[:, c, :]))
            for (a, wdt) in _islices(I0, 512):
                op("tensor", lambda h, c=c, a=a, wdt=wdt: h.matmul(
                    p.PS[:, a:a + wdt], p.bxqfm[:, c, :], p.wg_fm[:, c, a:a + wdt],
                    start=(c == 0), stop=(c == 7)))
        mark("tensor", CT)
        ev[("gate", s)] = CT.v
        wait("scalar", CT, ev[("gate", s)])
        op("scalar", lambda h: h.activation(out=p.bgs, in_=p.PS[:, 0:I0],
                                            func=AF.Silu, scale=p.sx), inc=(CS, 1))
        ev[("gs", s)] = CS.v
        wait("tensor", CS, ev[("gs", s)])
        for c in range(8):
            op("tensor", lambda h, c=c: h.ldweights(p.bxqfm[:, c, :]))
            for (a, wdt) in _islices(I0, 512):
                op("tensor", lambda h, c=c, a=a, wdt=wdt: h.matmul(
                    p.PS[:, a:a + wdt], p.bxqfm[:, c, :], p.wu_fm[:, c, a:a + wdt],
                    start=(c == 0), stop=(c == 7)))
        mark("tensor", CT)
        ev[("up", s)] = CT.v
        wait("vector", CT, ev[("up", s)])
        if s >= 1:
            wait("vector", p.STs[1 - r], ev[("st", s - 1)])
        op("vector", lambda h: h.scalar_tensor_tensor(
            out=p.bh1[r], in0=p.PS[:, 0:I0], scalar=p.sx, in1=p.bgs,
            op0=ALU.mult, op1=ALU.mult), inc=(CV, 1))
        ev[("h1", s)] = CV.v
        wait("gpsimd", CV, ev[("h1", s)])
        dma("gpsimd", ST, p.h1_s[SUB * s:SUB * (s + 1), :], p.bh1[r])
        ev[("st", s)] = ST.v
        op("vector", lambda h: h.reduce_max(out=p.mx1, in_=p.bh1[r], axis=AX.X,
                                            apply_absolute_value=True))
        op("vector", lambda h: h.tensor_tensor(out=p.mxg, in0=p.mxg, in1=p.mx1,
                                               op=ALU.max), inc=(CV, 1))

    p.ev_d2 = ev


def _x3_maxh(S):
    p = S.p
    op, dma, wait, mark = S.op, S.dma, S.wait, S.mark
    GW, CV, CS, CT, CP = p.GW, p.CV, p.CS, p.CT, p.CP
    _fence(S)
    op("tensor", lambda h: h.transpose(p.PS[0:1, 0:128], p.mxg, p.identf_sb))
    mark("tensor", CT)
    wait("vector", CT, CT.v)
    op("vector", lambda h: h.reduce_max(out=p.mxs, in_=p.PS[0:1, 0:128], axis=AX.X),
       inc=(CV, 1))
    wait("tensor", CV, CV.v)
    op("tensor", lambda h: h.matmul(p.PS[:, 512:513], p.onesf_sb, p.mxs,
                                    start=True, stop=True))
    mark("tensor", CT)
    wait("vector", CT, CT.v)
    op("vector", lambda h: h.tensor_copy(out=p.mx1, in_=p.PS[:, 512:513]),
       inc=(CV, 1))
    wait("gpsimd", CV, CV.v)
    dma("gpsimd", p.STs[0], p.ccm2_in, p.mx1)
    wait("gpsimd", p.STs[0], p.STs[0].v)
    op("gpsimd", lambda h: h.collective_compute(
        "AllReduce", ALU.max, replica_groups=[[0, 1, 2, 3, 4, 5, 6, 7]],
        ins=[p.ccm2_in.opt()], outs=[p.ccm2_out.opt()]), inc=(CP, 1))
    wait("gpsimd", CP, CP.v)
    dma("gpsimd", GW, p.mx1, p.ccm2_out)
    wait("vector", GW, GW.v)
    op("vector", lambda h: h.tensor_tensor(out=p.rsh, in0=p.mx1, in1=p.sgd,
                                           op=ALU.mult))
    op("vector", lambda h: h.tensor_scalar(out=p.rsh, in0=p.rsh, scalar1=1.0 / 127.0,
                                           scalar2=None, op0=ALU.mult), inc=(CV, 1))
    wait("vector", CV, CV.v)
    op("vector", lambda h: h.tensor_copy(out=p.sh, in_=p.rsh), inc=(CV, 1))
    wait("vector", CV, CV.v)
    op("vector", lambda h: h.reciprocal(out=p.rsh, in_=p.rsh), inc=(CV, 1))
    p.ev_rsh = CV.v


def _phase_E(S):
    p = S.p
    op, dma, wait, mark = S.op, S.dma, S.wait, S.mark
    GW, CV, CS, CT = p.GW, p.CV, p.CS, p.CT
    n_sub = p.n_sub
    ev = {}
    _fence(S)
    _wquant_int8(S, p.wd1_h, p.wd_fm, D, I0)
    wait("tensor", p.LDs[0], p.LDs[0].v)
    wait("tensor", p.LDs[1], p.LDs[1].v)
    wait("vector", CV, p.ev_rsh)

    for s in range(n_sub):
        r = s % 2
        LD = p.LDs[r]
        ST = p.STs[r]
        if s >= 1:
            wait("sync", CV, ev[("x2", s - 1)])
            wait("sync", CV, ev[("hq", s - 1)])
        dma("sync", LD, p.eh1[r], p.h1_s[SUB * s:SUB * (s + 1), :])
        dma("sync", LD, p.es1[r], p.s1_s[SUB * s:SUB * (s + 1), :])
        dma("sync", LD, p.ex1[r], p.x1_s[SUB * s:SUB * (s + 1), :])
        ev[("ld", s)] = LD.v
        wait("vector", LD, ev[("ld", s)])
        if s >= 1:
            wait("vector", CT, ev[("down", s - 1)])
            wait("vector", p.LDs[1 - r], ev[("hqfm", s - 1)])
        op("vector", lambda h: h.tensor_scalar(out=p.qt.hv[:, 0:I0], in0=p.eh1[r],
                                               scalar1=p.rsh, scalar2=None,
                                               op0=ALU.mult), inc=(CV, 1))
        for (a, wdt) in _islices(I0, 1408):
            cv = _emit_round_clip(S, p.qt.hv[:, a:a + wdt], p.bhq[:, a:a + wdt], wdt)
        ev[("hq", s)] = cv
        wait("sync", CV, cv)
        if s >= 1:
            wait("sync", CT, ev[("down", s - 1)])
        for c in range(22):
            dma("sync", LD, p.bhqfm[:, c, :], p.bhq[:, 128 * c:128 * (c + 1)],
                transpose=True)
        ev[("hqfm", s)] = LD.v
        wait("tensor", LD, ev[("hqfm", s)])
        if s >= 1:
            wait("tensor", CS, ev[("g1", s - 1)])
        for c in range(22):
            op("tensor", lambda h, c=c: h.ldweights(p.bhqfm[:, c, :]))
            for j in range(2):
                op("tensor", lambda h, c=c, j=j: h.matmul(
                    p.PS[:, 512 * j:512 * (j + 1)], p.bhqfm[:, c, :],
                    p.wd_fm[:, c, 512 * j:512 * (j + 1)],
                    start=(c == 0), stop=(c == 21)))
        mark("tensor", CT)
        ev[("down", s)] = CT.v
        wait("scalar", CT, ev[("down", s)])
        op("scalar", lambda h: h.activation(out=p.bg0, in_=p.PS[:, 0:1024],
                                            func=AF.Silu, scale=p.sh), inc=(CS, 1))
        ev[("g1", s)] = CS.v
        wait("vector", CS, ev[("g1", s)])
        if s >= 1:
            wait("vector", p.STs[1 - r], ev[("st", s - 1)])
        op("vector", lambda h: h.tensor_tensor(out=p.bg0, in0=p.es1[r], in1=p.bg0,
                                               op=ALU.mult))
        op("vector", lambda h: h.tensor_tensor(out=p.ex2[r], in0=p.bg0, in1=p.ex1[r],
                                               op=ALU.add), inc=(CV, 1))
        ev[("x2", s)] = CV.v
        wait("gpsimd", CV, ev[("x2", s)])
        dma("gpsimd", ST, p.y_h[SUB * s:SUB * (s + 1), :], p.ex2[r])
        ev[("st", s)] = ST.v

    p.ev_e = ev


def _mk_prog(nc, n_sub, stack):
    p = NS()
    p.n_sub = n_sub
    ntok = n_sub * SUB

    def din(name, shape, dt=F32):
        return nc.dram_tensor(name, list(shape), dt, kind="ExternalInput").ap()

    def scr(name, shape, dt=F32):
        return nc.dram_tensor(name, list(shape), dt, kind="Internal").ap()

    p.x_h = din("x", (ntok, D))
    p.ropec_h = din("ropec", (ntok, 128))
    p.ropes_h = din("ropes", (ntok, 128))
    p.ropecq_h = din("ropecq", (ntok, 128))
    p.ropesq_h = din("ropesq", (ntok, 128))
    p.kdect_h = din("kdect", (SUB, H0))
    p.crosst_h = din("crosst", (SUB, H0))
    p.dmaskt_h = din("dmaskt", (SUB, H0 * SUB))
    p.lnw_rep_h = din("lnwrep", (SUB, D))
    p.lnb_rep_h = din("lnbrep", (SUB, D))
    p.gnw_rep_h = din("gnwrep", (SUB, DV))
    p.sxw_h = din("sxw", (SUB, 3 * H0))
    p.sxm_h = din("sxm", (SUB, 4))
    p.actc_h = din("actc", (SUB, len(ACT_CONSTS)))
    p.onesf_h = din("onesf", (1, SUB))
    p.identf_h = din("identf", (SUB, SUB))
    p.clipx_h = din("clipx", (1, 2))
    p.wkv_h = din("wkv", (D, 3 * D))
    p.wqg_h = din("wqg", (D, 3 * D))
    p.wo0_h = din("wo0", (2 * D, D))
    p.wg0_h = din("wg0", (I0, D))
    p.wu0_h = din("wu0", (I0, D))
    p.wd0_h = din("wd0", (D, I0))
    p.wq1_h = din("wq1", (SUB, 16 * 64))
    p.wkv1_h = din("wkv1", (SUB, 8 * 256))
    p.wo1_h = din("wo1", (D, D))
    p.wg1_h = din("wg1", (I0, D))
    p.wu1_h = din("wu1", (I0, D))
    p.wd1_h = din("wd1", (D, I0))
    p.y_h = nc.dram_tensor("y", [ntok, D], F32, kind="ExternalOutput").ap()

    p.xn_s = scr("xn_s", (ntok, D))
    p.normed_s = scr("normed_s", (ntok, D), BF16)
    p.ktm_s = scr("ktm_s", (ntok, D), BF16)
    p.vtm_s = scr("vtm_s", (ntok, 2 * D), BF16)
    p.s0_s = scr("s0_s", (ntok, D))
    p.x1_s = scr("x1_s", (ntok, D))
    p.xn1_s = scr("xn1_s", (ntok, D))
    p.xn1b_s = scr("xn1b_s", (ntok, D), BF16)
    p.hq0_s = scr("hq0_s", (ntok, I0), BF16)
    p.s1_s = scr("s1_s", (ntok, D))
    p.h1_s = scr("h1_s", (ntok, I0))
    p.ccs_in = scr("ccs_in", (SUB, 3 * H0 * 2 * 512), BF16)
    p.ccs_out = scr("ccs_out", (SUB, 3 * H0 * 2 * 512), BF16)
    p.cckv_in = scr("cckv_in", (SUB, 2048))
    p.cckv_out = scr("cckv_out", (SUB, 2048))
    p.ccm1_in = scr("ccm1_in", (SUB, 1))
    p.ccm1_out = scr("ccm1_out", (SUB, 1))
    p.ccm2_in = scr("ccm2_in", (SUB, 1))
    p.ccm2_out = scr("ccm2_out", (SUB, 1))

    p.PS = stack.enter_context(nc.psum_tensor("PS", [128, 4096], F32))[:]

    def sb(name, shape, dt=F32):
        return stack.enter_context(nc.sbuf_tensor(name, list(shape), dt))[:]

    p.actc_sb = sb("actc_sb", (SUB, len(ACT_CONSTS)))
    for i, v in enumerate(ACT_CONSTS):
        nc.const_aps.register(F32, v, p.actc_sb[:, i:i + 1])
    p.kdect_sb = sb("kdect_sb", (SUB, H0))
    p.crosst_sb = sb("crosst_sb", (SUB, H0))
    p.sxw_sb = sb("sxw_sb", (SUB, 3 * H0))
    p.sxm_sb = sb("sxm_sb", (SUB, 4))
    p.onesf_sb = sb("onesf_sb", (1, SUB))
    p.identf_sb = sb("identf_sb", (SUB, SUB))
    p.clip_sb = sb("clip_sb", (1, 2))
    p.clip_sg = sb("clip_sg", (1, 2))
    p.trash = sb("trash", (SUB, D))
    p.trash2 = sb("trash2", (SUB, D))
    p.sums = sb("sums", (SUB, 4))
    p.mu = sb("mu", (SUB, 1))
    p.rstd = sb("rstd", (SUB, 1))
    p.mu2 = sb("mu2", (SUB, 1))
    p.mx1 = sb("mx1", (SUB, 1))
    p.mxg = sb("mxg", (SUB, 1))
    p.mxs = sb("mxs", (1, 1))
    p.rsx = sb("rsx", (SUB, 1))
    p.sx = sb("sx", (SUB, 1))
    p.rsh = sb("rsh", (SUB, 1))
    p.sh = sb("sh", (SUB, 1))
    p.sgd = sb("sgd", (SUB, 1))
    p.rm = sb("rm", (SUB, 1))
    p.rr1 = sb("rr1", (SUB, 1))
    p.tmp1 = sb("tmp1", (SUB, 128))
    p.tmp2 = sb("tmp2", (SUB, 128))
    p.tmp3 = sb("tmp3", (SUB, 512))
    p.drz = sb("drz", (SUB, 16))

    AR_COLS = 49936
    p.AR = stack.enter_context(nc.sbuf_tensor("AR", [128, AR_COLS], F32))[:]
    p._bump = [0]

    def take(cols, dt=F32, shape=None, at=None):
        a = p._bump[0] if at is None else at
        end = a + ((cols + 15) // 16) * 16
        assert end <= AR_COLS, f"arena overflow {end}"
        if at is None:
            p._bump[0] = end
        ap = p.AR[:, a:a + cols]
        if dt == BF16:
            ap = ap.bitcast(BF16)
        if shape is not None:
            names = " ".join(f"d{i}" for i in range(len(shape)))
            ap = ap.rearrange(f"p ({names}) -> p {names}",
                              **{f"d{i}": v for i, v in enumerate(shape)})
        return ap

    p.take = take
    return p


def _alloc_buffers(p):
    take = p.take
    w0 = p._bump[0]
    assert w0 == 0
    p.wkv_sb = take(12288, BF16, (8, 3072))
    p.wqg_sb = p.wkv_sb
    p.wo0_sb = take(8192, BF16, (16, 1024))
    wg_at = 0
    p.wg_fm = take(11264, BF16, (8, I0), at=0)
    p.wu_fm = take(11264, BF16, (8, I0), at=11264)
    p.wd_fm = take(11264, BF16, (22, 1024), at=0)
    p.wo1_sb = take(4096, BF16, (8, 1024), at=11264)
    p.wq1_sb = take(512, BF16, (16, 64), at=11264 + 4096)
    p.wkv1_sb = take(1024, BF16, (8, 256), at=11264 + 4608)
    p._bump[0] = max(p._bump[0], 22528)
    p.xch = take(6144, BF16, (3, H0, 2, 512), at=12288)

    p.a0x = [take(D) for _ in range(2)]
    p.a0xn = [take(D) for _ in range(2)]
    p.a0z = take(D)
    p.a0nb = [take(512, BF16) for _ in range(2)]
    p.normfm = [take(512, BF16, (8, 128)) for _ in range(2)]
    p.ropec_t = [take(128) for _ in range(2)]
    p.ropes_t = [take(128) for _ in range(2)]
    p.a0ktm = [take(512, BF16, (H0, 256)) for _ in range(2)]
    p.a0ktil = take(512, BF16, (H0, 256))
    p.a0vtm = [take(1024, BF16) for _ in range(2)]
    p.a1nfm = p.normfm
    p.a1kfm = [take(512, BF16, (8, 128)) for _ in range(2)]
    p.a1ktm = [take(512, BF16) for _ in range(2)]
    p.a1vtm = p.a0vtm
    p.a1xn = p.a0x
    p.a1qtm = take(512, BF16, (H0, 256))
    p.a1qctm = take(512, BF16, (H0, 256))
    p.a1qfm = take(512, BF16, (8, 128))
    p.a1qcfm = take(512, BF16, (8, 128))
    p.a1gs = take(2 * D)
    p.a1at = take(256, BF16, (H0, 128))
    p.a1og = take(1024, BF16, (H0, 512))
    p.a1ogfm = take(1024, BF16, (16, 128))
    p.a1s0 = [p.a0xn[0], p.a0xn[1]]
    p.lnw_sb = take(D)
    p.lnb_sb = take(D)
    p.gnw_sb = take(DV)
    p.dmask_sb = take(H0 * SUB)
    p.S_sb = take(H0 * 512, BF16, (H0, 2, 512))
    g1_end = p._bump[0]
    print(f"[kernel] arena group1 end {g1_end}")
    p._bump[0] = 22528

    q0 = p._bump[0]
    q = NS()
    q.aw = take(1408)
    q.y0 = take(1408)
    q.tt = take(1408)
    q.d = take(1408)
    q.sg = [take(704, BF16) for _ in range(2)]
    q.tmpb = take(704, BF16)
    q.acc1 = take(704, BF16)
    q.acc2 = take(704, BF16)
    q.sgn = take(704, BF16)
    q.bm = take(176)
    q.rr = take(176)
    q.w = take(1408)
    p.qt = q
    qend = p._bump[0]
    p.cxfm = take(512, BF16, (8, 128), at=q0)
    p.ckn = take(1024, at=q0 + 512)
    p.cke = take(1024, at=q0 + 1536)
    p.cphik = take(512, BF16, at=q0 + 2560)
    p.cvext = [take(576, BF16, (16, 72), at=q0 + 3072 + 576 * i) for i in range(2)]
    p.dknq = take(2048, at=q0 + 512)
    p.dkeq = take(2048, at=q0 + 2560)
    p.dphiq = take(1024, BF16, at=q0 + 4608)
    p.dcomb = take(512, BF16, (16, 64), at=q0 + 5632)
    p.dcombfm = take(512, BF16, (8, 128), at=q0 + 6144)
    p.ds1 = [take(1024, at=q0 + 6656 + 1024 * i) for i in range(2)]
    p.kvsb = take(2048, at=q0 + 7808)
    p.kvb = take(1024, BF16)
    p.kvb3 = p.kvb.rearrange("p (hd e) -> p hd e", e=128)
    assert q0 + 9856 <= qend, (q0, qend)

    bxn1 = take(D)
    p.bxn = [bxn1, bxn1]
    p.bx = p.bxn
    bs01 = take(D)
    p.bs0 = [bs01, bs01]
    p.bxq = take(512, BF16)
    bhqfm_cols = take(1408, BF16)
    p.bhqfm = bhqfm_cols.rearrange("p (a b) -> p a b", a=22)
    p.bxqfm = take(512, BF16, (8, 128))
    p.bgs = take(I0)
    p.bhq = take(1408, BF16)
    p.bg0 = take(D)
    bx11 = take(D)
    p.bx1 = [bx11, bx11]
    cx11 = bhqfm_cols.bitcast(F32)[:, 0:D]
    p.cx1 = [cx11, cx11]
    cxn11 = p.bgs[:, 0:D]
    p.cxn1 = [cxn11, cxn11]
    cxn1b1 = p.bhq[:, 0:D]
    p.cxn1b = [cxn1b1, cxn1b1]
    bh11 = take(I0)
    p.bh1 = [bh11, bh11]
    q.wslab = [p.bgs, p.bh1[0]]
    q.wqb = p.bhq
    q.hv = p.bgs
    p.dxn1 = p.cx1
    p.eh1 = p.bh1
    p.es1 = p.ds1
    p.ex1 = p.bs0
    p.ex2 = p.bx1
    print(f"[kernel] arena group2 end {p._bump[0]} / 49936 f32 cols")


_STOP = [None]


def build_nc(n_sub=NSUB, stop=None):
    _STOP[0] = stop
    import contextlib
    nc = bass.Bass("TRN2", target_bir_lowering=False)
    with contextlib.ExitStack() as stack:
        p = _mk_prog(nc, n_sub, stack)
        _alloc_buffers(p)
        with (
            nc.semaphore("sLD") as s_ld,
            nc.semaphore("sLD1") as s_ld1,
            nc.semaphore("sGW") as s_gw,
            nc.semaphore("sST") as s_st,
            nc.semaphore("sST1") as s_st1,
            nc.semaphore("sCV") as s_cv,
            nc.semaphore("sCS") as s_cs,
            nc.semaphore("sCT") as s_ct,
            nc.semaphore("sCP") as s_cp,
            nc.Block() as block,
        ):
            def fresh():
                p.LDs = [Ctr(s_ld, "LD0"), Ctr(s_ld1, "LD1")]
                p.GW = Ctr(s_gw, "GW")
                p.STs = [Ctr(s_st, "ST0"), Ctr(s_st1, "ST1")]
                p.CV = Ctr(s_cv, "CV")
                p.CS = Ctr(s_cs, "CS")
                p.CT = Ctr(s_ct, "CT")
                p.CP = Ctr(s_cp, "CP")

            @block.sync
            def _(h):
                fresh()
                _emit(p, "sync", h)

            @block.gpsimd
            def _(h):
                fresh()
                _emit(p, "gpsimd", h)

            @block.vector
            def _(h):
                fresh()
                _emit(p, "vector", h)

            @block.scalar
            def _(h):
                fresh()
                _emit(p, "scalar", h)

            @block.tensor
            def _(h):
                fresh()
                _emit(p, "tensor", h)

    return nc


def _prep_inputs(gi, n_sub=NSUB):
    ntok = n_sub * SUB
    tab = _host_tables(4 * ntok)
    x = np.ascontiguousarray(np.asarray(gi["L0_attn_x"], np.float32))
    wkv = np.ascontiguousarray(np.concatenate(
        [np.asarray(gi["L0_attn_Wk"], np.float32),
         np.asarray(gi["L0_attn_Wv"], np.float32)], axis=1))
    wqg = np.ascontiguousarray(np.concatenate(
        [np.asarray(gi["L0_attn_Wq"], np.float32),
         np.asarray(gi["L0_attn_Wg"], np.float32)], axis=1))
    rep = lambda v: np.ascontiguousarray(
        np.tile(np.asarray(v, np.float32)[None, :], (SUB, 1)))

    wqh = np.asarray(gi["L1_attn_Wq"], np.float32).reshape(8, 2, 64, 64)
    wq1b = np.zeros((8, 2, 2, 64, 64), np.float32)
    for hh in range(2):
        wq1b[:, hh, hh, :, :] = wqh[:, hh]
    wq1 = np.ascontiguousarray(
        wq1b.transpose(2, 3, 0, 1, 4).reshape(SUB, 16 * 64))
    wkh = np.asarray(gi["L1_attn_Wk"], np.float32).reshape(8, 2, 64, 64)
    wvh = np.asarray(gi["L1_attn_Wv"], np.float32).reshape(8, 2, 64, 64)
    wkv1b = np.zeros((8, 2, 64, 2, 2, 64), np.float32)
    for hh in range(2):
        wkv1b[:, hh, :, hh, 0, :] = wkh[:, hh]
        wkv1b[:, hh, :, hh, 1, :] = wvh[:, hh]
    wkv1 = np.ascontiguousarray(
        wkv1b.transpose(1, 2, 0, 3, 4, 5).reshape(SUB, 8 * 256))
    clip = np.array([[float(np.asarray(gi["L1_ffn_clip_x"]).reshape(-1)[0]),
                      float(np.asarray(gi["L1_ffn_clip_down"]).reshape(-1)[0])]],
                    np.float32)
    actc = np.tile(np.asarray(ACT_CONSTS, np.float32)[None, :], (SUB, 1))

    ins = []
    for c in range(8):
        b, pos = c // 4, c % 4
        t0 = pos * ntok
        sxw, sxm = _core_tables(pos, ntok)
        sl = slice(t0, t0 + ntok)
        ins.append({
            "x": np.ascontiguousarray(x[b, t0:t0 + ntok]),
            "ropec": np.ascontiguousarray(tab["ropec"][sl]),
            "ropes": np.ascontiguousarray(tab["ropes"][sl]),
            "ropecq": np.ascontiguousarray(tab["ropecq"][sl]),
            "ropesq": np.ascontiguousarray(tab["ropesq"][sl]),
            "kdect": tab["kdect"],
            "crosst": tab["crosst"],
            "dmaskt": np.ascontiguousarray(tab["dmaskt"].reshape(SUB, H0 * SUB)),
            "lnwrep": rep(gi["L0_attn_ln_w"]),
            "lnbrep": rep(gi["L0_attn_ln_b"]),
            "gnwrep": rep(gi["L0_attn_g_norm_w"]),
            "sxw": np.ascontiguousarray(np.tile(sxw.reshape(1, -1), (SUB, 1))),
            "sxm": np.ascontiguousarray(np.tile(sxm.reshape(1, 4), (SUB, 1))),
            "actc": actc,
            "onesf": np.ones((1, SUB), np.float32),
            "identf": np.eye(SUB, dtype=np.float32),
            "clipx": clip,
            "wkv": wkv, "wqg": wqg,
            "wo0": np.asarray(gi["L0_attn_Wo"], np.float32),
            "wg0": np.asarray(gi["L0_ffn_w_gate"], np.float32),
            "wu0": np.asarray(gi["L0_ffn_w_up"], np.float32),
            "wd0": np.asarray(gi["L0_ffn_w_down"], np.float32),
            "wq1": wq1, "wkv1": wkv1,
            "wo1": np.asarray(gi["L1_attn_Wo"], np.float32),
            "wg1": np.asarray(gi["L1_ffn_w_gate"], np.float32),
            "wu1": np.asarray(gi["L1_ffn_w_up"], np.float32),
            "wd1": np.asarray(gi["L1_ffn_w_down"], np.float32),
        })
    return ins


def kernel(**inputs):
    nc = build_nc(NSUB)
    ins = _prep_inputs(inputs, NSUB)
    res = run_bass_kernel_spmd(nc, ins, core_ids=list(range(8)))
    out = np.empty((B, T, D), np.float32)
    for c in range(8):
        b, pos = c // 4, c % 4
        out[b, pos * TPC:(pos + 1) * TPC] = res.results[c]["y"]
    return out


# version 04
# speedup vs baseline: 1.2609x; 1.2609x over previous
import numpy as np
import ml_dtypes
import concourse.bass as bass
import concourse.mybir as mybir
from concourse.bass_utils import run_bass_kernel_spmd

F32 = mybir.dt.float32
BF16 = mybir.dt.bfloat16
AF = mybir.ActivationFunctionType
ALU = mybir.AluOpType
AX = mybir.AxisListType
BF = ml_dtypes.bfloat16

B, T, D = 2, 16384, 1024
TPC = T // 4
SUB = 128
NSUB = TPC // SUB
H0, DK, DV = 4, 256, 512
H1, E1 = 16, 64
I0 = 2816
EPSM = 1e-5
EPS0 = 1e-5
EPS1 = 1e-6

GAMMA = [1.0 - 2.0 ** (-5.0 - h) for h in range(H0)]
GC = [g ** SUB for g in GAMMA]

FP4_TH = [0.5, 1.5, 2.5, 3.5, 5.0, 7.0, 10.0]
ACT_CONSTS = [-th for th in FP4_TH] + [-1.0, 1e-5, 1.0 / 512.0]


class Ctr:
    def __init__(self, sem, name):
        self.sem = sem
        self.name = name
        self.v = 0


class NS:
    pass


def _islices(total, step):
    out = []
    a = 0
    while a < total:
        out.append((a, min(step, total - a)))
        a += step
    return out


def _host_tables(t_total=T):
    t = {}
    invf = 10000.0 ** (-np.arange(0, DK, 2, dtype=np.float64) / DK)
    tt = np.arange(t_total, dtype=np.float64)[:, None]
    ang = tt * invf[None, :]
    t["ropec"] = np.cos(ang).astype(np.float32)
    t["ropes"] = np.sin(ang).astype(np.float32)
    s0 = DK ** (-0.5)
    t["ropecq"] = (np.cos(ang) * s0).astype(np.float32)
    t["ropesq"] = (np.sin(ang) * s0).astype(np.float32)
    i_ = np.arange(SUB, dtype=np.float64)[:, None]
    g_ = np.asarray(GAMMA, dtype=np.float64)[None, :]
    t["kdect"] = (g_ ** (SUB - 1.0 - i_)).astype(np.float32)
    t["crosst"] = (g_ ** (i_ + 1.0)).astype(np.float32)
    ii = np.arange(SUB, dtype=np.float64)
    dmk = np.zeros((SUB, H0, SUB), dtype=np.float64)
    for h in range(H0):
        m = np.where(ii[:, None] >= ii[None, :],
                     np.float64(GAMMA[h]) ** (ii[:, None] - ii[None, :]), 0.0)
        dmk[:, h, :] = m.T
    t["dmaskt"] = dmk.astype(np.float32)
    return t


def _core_tables(pos, tpc=TPC):
    w = np.zeros((3, H0), dtype=np.float64)
    for ps in (1, 2, 3):
        if ps > pos:
            for h in range(H0):
                w[ps - 1, h] = np.float64(GAMMA[h]) ** (tpc * (ps - 1 - pos))
    m = np.zeros((4,), dtype=np.float32)
    if pos >= 1:
        m[pos - 1] = 1.0
    return w.astype(np.float32), m


def _emit(p, active, h):
    S = NS()
    S.p = p
    S.active = active
    S.h = h

    S.last = None
    S.pend = {}
    S.pend_all = {e: set() for e in ("sync", "gpsimd", "vector", "scalar", "tensor")}

    def _flush():
        if S.last is not None:
            for ctr, amt in S.pend.items():
                S.last.then_inc(ctr.sem, amt)
        S.pend = {}

    _AUTOC = {"vector": "CV", "scalar": "CS"}

    def op(eng, fn, inc=None):
        ctr0 = getattr(p, _AUTOC[eng]) if eng in _AUTOC else None
        if ctr0 is not None and eng == active and ctr0.v > 0:
            _flush()
            S.h.wait_ge(ctr0.sem, ctr0.v)
        ins = None
        if eng == active:
            _flush()
            ins = fn(S.h)
            S.last = ins
        S.pend_all[eng] = set()
        if ctr0 is not None:
            ctr0.v += 1
            S.pend_all[eng].add(ctr0.name)
            if ins is not None:
                S.pend[ctr0] = S.pend.get(ctr0, 0) + 1
        if inc is not None:
            ctr, amt = inc
            ctr.v += amt
            S.pend_all[eng].add(ctr.name)
            if ins is not None:
                S.pend[ctr] = S.pend.get(ctr, 0) + amt
        return ins

    def mark(eng, ctr):
        if ctr.name in S.pend_all[eng]:
            return
        ctr.v += 1
        S.pend_all[eng].add(ctr.name)
        if eng == active:
            assert S.last is not None
            S.pend[ctr] = S.pend.get(ctr, 0) + 1

    S.flush = _flush

    S.drained = {"sync": set(), "gpsimd": set()}

    def dma(eng, ctr, out, in_, transpose=False):
        if eng in S.drained and ctr.name not in S.drained[eng]:
            S.drained[eng].add(ctr.name)
            if eng == active:
                S.h.wait_ge(ctr.sem, ctr.v)
        if eng == active:
            _flush()
            S.h.dma_start(out=out, in_=in_, transpose=transpose).then_inc(ctr.sem, 16)
        ctr.v += 16

    def wait(eng, ctr, val=None):
        if eng in S.drained:
            S.drained[eng] = set()
        v = ctr.v if val is None else val
        assert v <= ctr.v, (ctr.name, v, ctr.v)
        if eng == active:
            S.h.wait_ge(ctr.sem, v)

    S.op, S.dma, S.wait, S.mark = op, dma, wait, mark
    _program(S)
    _flush()


def _program(S):
    p = S.p
    dma, wait = S.dma, S.wait
    LD = p.LDs[0]

    dma("sync", LD, p.actc_sb, p.actc_h)
    dma("sync", LD, p.kdect_sb, p.kdect_h)
    dma("sync", LD, p.crosst_sb, p.crosst_h)
    dma("sync", LD, p.dmask_sb, p.dmaskt_h)
    dma("sync", LD, p.lnw_sb, p.lnw_rep_h)
    dma("sync", LD, p.lnb_sb, p.lnb_rep_h)
    dma("sync", LD, p.gnw_sb, p.gnw_rep_h)
    dma("sync", LD, p.sxw_sb, p.sxw_h)
    dma("sync", LD, p.sxm_sb, p.sxm_h)
    dma("sync", LD, p.onesf_sb, p.onesf_h)
    dma("sync", LD, p.identf_sb, p.identf_h)
    dma("sync", LD, p.clip_sb, p.clipx_h)
    ld_consts = LD.v
    dma("gpsimd", p.GW, p.idb_sb, p.identf_h)
    for e in ("vector", "scalar", "tensor", "gpsimd", "sync"):
        wait(e, LD, ld_consts)
    wait("tensor", p.GW, p.GW.v)

    phases = [("A0", _phase_A0), ("X1", _x1_state_exchange), ("A1", _phase_A1),
              ("B1", _phase_B1), ("B2", _phase_B2), ("C", _phase_C),
              ("X2", _x2_kv_max), ("D1", _phase_D1), ("D2", _phase_D2),
              ("X3", _x3_maxh), ("E", _phase_E)]
    for name, fn in phases:
        fn(S)
        st = _STOP[0] or ""
        if name == st or st.startswith(name + "."):
            break

    wait("gpsimd", p.STs[0], p.STs[0].v)
    wait("gpsimd", p.STs[1], p.STs[1].v)


def _emit_ln(S, x_ap, out_ap, affine_out=None):
    p = S.p
    op, wait, mark = S.op, S.wait, S.mark
    CV, CS = p.CV, p.CS
    wait("scalar", CV, CV.v)
    op("scalar", lambda h: h.activation(out=p.trash, in_=x_ap, func=AF.Copy,
                                        accum_out=p.sums[:, 0:1]))
    op("scalar", lambda h: h.activation(out=p.trash2, in_=x_ap, func=AF.Square,
                                        accum_out=p.sums[:, 1:2]), inc=(CS, 1))
    wait("vector", CS, CS.v)
    op("vector", lambda h: h.tensor_scalar(out=p.mu, in0=p.sums[:, 0:1],
                                           scalar1=1.0 / 1024.0, scalar2=None,
                                           op0=ALU.mult))
    op("vector", lambda h: h.tensor_scalar(out=p.rstd, in0=p.sums[:, 1:2],
                                           scalar1=1.0 / 1024.0, scalar2=None,
                                           op0=ALU.mult), inc=(CV, 1))
    wait("vector", CV, CV.v)
    op("vector", lambda h: h.scalar_tensor_tensor(out=p.rstd, in0=p.mu, scalar=p.mu,
                                                  in1=p.rstd, op0=ALU.mult,
                                                  op1=ALU.subtract), inc=(CV, 1))
    wait("scalar", CV, CV.v)
    op("scalar", lambda h: h.activation(out=p.rstd, in_=p.rstd, func=AF.Sqrt,
                                        scale=-1.0, bias=EPSM), inc=(CS, 1))
    wait("vector", CS, CS.v)
    op("vector", lambda h: h.reciprocal(out=p.rstd, in_=p.rstd), inc=(CV, 1))
    wait("vector", CV, CV.v)
    op("vector", lambda h: h.tensor_scalar(out=out_ap, in0=x_ap, scalar1=p.mu,
                                           scalar2=p.rstd, op0=ALU.subtract,
                                           op1=ALU.mult))
    if affine_out is not None:
        op("vector", lambda h: h.tensor_tensor(out=p.trash, in0=out_ap, in1=p.lnw_sb,
                                               op=ALU.mult))
        op("vector", lambda h: h.tensor_tensor(out=affine_out, in0=p.trash,
                                               in1=p.lnb_sb, op=ALU.add))
    mark("vector", CV)
    return CV.v


def _emit_fp4(S, x_ap, out_ap, F):
    p = S.p
    t = p.qt
    op, wait, mark = S.op, S.wait, S.mark
    CV, CS = p.CV, p.CS
    nb = F // 16
    x3 = x_ap.rearrange("p (b k) -> p b k", k=16)
    bm = t.bm[:, 0:nb]
    bm3 = bm.rearrange("p (b o) -> p b o", o=1)
    op("vector", lambda h: h.reduce_max(out=bm3, in_=x3, axis=AX.X,
                                        apply_absolute_value=True))
    op("vector", lambda h: h.reciprocal(out=t.rr[:, 0:nb], in_=bm))
    op("vector", lambda h: h.tensor_scalar(out=t.rr[:, 0:nb], in0=t.rr[:, 0:nb],
                                           scalar1=12.0, scalar2=2e12,
                                           op0=ALU.mult, op1=ALU.min), inc=(CV, 1))
    wait("vector", CV, CV.v)
    rrb = t.rr[:, 0:nb].unsqueeze(2).broadcast_to([128, nb, 16])
    w3 = t.w[:, 0:F].rearrange("p (b k) -> p b k", k=16)
    op("vector", lambda h: h.tensor_tensor(out=w3, in0=x3, in1=rrb, op=ALU.mult),
       inc=(CV, 1))
    wait("scalar", CV, CV.v)
    op("scalar", lambda h: h.activation(out=t.aw[:, 0:F], in_=t.w[:, 0:F],
                                        func=AF.Abs))
    for rnd in range(4):
        js = [2 * rnd, 2 * rnd + 1] if rnd < 3 else [6]
        for i, j in enumerate(js):
            last = (i == len(js) - 1) and rnd != 3
            op("scalar", lambda h, i=i, j=j: h.activation(
                out=t.sg[i][:, 0:F], in_=t.aw[:, 0:F], func=AF.Sign, bias=-FP4_TH[j]),
               inc=(CS, 1) if last else None)
        if rnd == 3:
            op("scalar", lambda h: h.activation(out=t.sgn[:, 0:F], in_=t.w[:, 0:F],
                                                func=AF.Sign), inc=(CS, 1))
        wait("vector", CS, CS.v)
        if rnd == 0:
            op("vector", lambda h: h.tensor_tensor(out=t.acc1[:, 0:F], in0=t.sg[0][:, 0:F],
                                                   in1=t.sg[1][:, 0:F], op=ALU.add))
        elif rnd == 1:
            op("vector", lambda h: h.tensor_tensor(out=t.tmpb[:, 0:F], in0=t.sg[0][:, 0:F],
                                                   in1=t.sg[1][:, 0:F], op=ALU.add))
            op("vector", lambda h: h.tensor_tensor(out=t.acc1[:, 0:F], in0=t.acc1[:, 0:F],
                                                   in1=t.tmpb[:, 0:F], op=ALU.add))
        elif rnd == 2:
            op("vector", lambda h: h.tensor_tensor(out=t.acc2[:, 0:F], in0=t.sg[0][:, 0:F],
                                                   in1=t.sg[1][:, 0:F], op=ALU.add))
        else:
            op("vector", lambda h: h.scalar_tensor_tensor(
                out=t.tmpb[:, 0:F], in0=t.acc1[:, 0:F], scalar=0.25,
                in1=t.sg[0][:, 0:F], op0=ALU.mult, op1=ALU.add))
            op("vector", lambda h: h.scalar_tensor_tensor(
                out=t.tmpb[:, 0:F], in0=t.acc2[:, 0:F], scalar=0.5,
                in1=t.tmpb[:, 0:F], op0=ALU.mult, op1=ALU.add))
        mark("vector", CV)
        if rnd < 3:
            wait("scalar", CV, CV.v)
    op("vector", lambda h: h.tensor_scalar(out=bm, in0=bm, scalar1=1.0 / 6.0,
                                           scalar2=1e-12, op0=ALU.mult, op1=ALU.max))
    scb = bm.unsqueeze(2).broadcast_to([128, nb, 16])
    sg3 = t.sgn[:, 0:F].rearrange("p (b k) -> p b k", k=16)
    ss3 = t.aw[:, 0:F].rearrange("p (b k) -> p b k", k=16)
    op("vector", lambda h: h.tensor_tensor(out=ss3, in0=sg3, in1=scb, op=ALU.mult))
    op("vector", lambda h: h.scalar_tensor_tensor(out=out_ap, in0=t.tmpb[:, 0:F],
                                                  scalar=3.0, in1=t.aw[:, 0:F],
                                                  op0=ALU.add, op1=ALU.mult),
       inc=(CV, 1))
    return CV.v


def _emit_round_clip(S, v_ap, out_ap, F):
    p = S.p
    t = p.qt
    op, wait, mark = S.op, S.wait, S.mark
    CV, CS = p.CV, p.CS
    op("vector", lambda h: h.tensor_scalar(out=v_ap, in0=v_ap, scalar1=-128.0,
                                           scalar2=127.0, op0=ALU.max, op1=ALU.min),
       inc=(CV, 1))
    wait("scalar", CV, CV.v)
    op("scalar", lambda h: h.activation(out=t.y0[:, 0:F], in_=v_ap, func=AF.Abs))
    op("scalar", lambda h: h.activation(out=t.sgn[:, 0:F], in_=v_ap, func=AF.Sign),
       inc=(CS, 1))
    wait("vector", CS, CS.v)
    op("vector", lambda h: h.tensor_copy(out=t.tt[:, 0:F], in_=t.y0[:, 0:F]),
       inc=(CV, 1))
    for m in (128.0, 64.0, 32.0, 16.0, 8.0, 4.0, 2.0, 1.0):
        op("vector", lambda h, m=m: h.tensor_scalar(out=t.d[:, 0:F], in0=t.tt[:, 0:F],
                                                    scalar1=m - 0.5, scalar2=-m,
                                                    op0=ALU.is_ge, op1=ALU.mult))
        op("vector", lambda h: h.tensor_tensor(out=t.tt[:, 0:F], in0=t.tt[:, 0:F],
                                               in1=t.d[:, 0:F], op=ALU.add))
    op("vector", lambda h: h.tensor_tensor(out=t.y0[:, 0:F], in0=t.y0[:, 0:F],
                                           in1=t.tt[:, 0:F], op=ALU.subtract))
    op("vector", lambda h: h.tensor_tensor(out=out_ap, in0=t.y0[:, 0:F],
                                           in1=t.sgn[:, 0:F], op=ALU.mult),
       inc=(CV, 1))
    return CV.v


def _phase_A0(S):
    p = S.p
    op, dma, wait, mark = S.op, S.dma, S.wait, S.mark
    GW, CV, CS, CT = p.GW, p.CV, p.CS, p.CT
    n_sub = p.n_sub
    ev = {}

    for c in range(8):
        dma("gpsimd", GW, p.wkv_sb[:, c, :], p.wkv_h[128 * c:128 * (c + 1), :])
    wait("tensor", GW, GW.v)
    op("vector", lambda h: h.memset(p.S_sb.rearrange("p h t d -> p (h t d)"), 0.0),
       inc=(CV, 1))

    for s in range(n_sub):
        r = s % 2
        LD = p.LDs[r]
        ST = p.STs[r]
        if s >= 2:
            wait("sync", CV, ev[("krope", s - 2)])
            wait("sync", CS, ev[("vevac", s - 2)])
        dma("sync", LD, p.a0x[r], p.x_h[SUB * s:SUB * (s + 1), :])
        dma("sync", LD, p.ropec_t[r], p.ropec_h[SUB * s:SUB * (s + 1), :])
        dma("sync", LD, p.ropes_t[r], p.ropes_h[SUB * s:SUB * (s + 1), :])
        ev[("ld", s)] = LD.v

        wait("vector", LD, ev[("ld", s)])
        wait("scalar", LD, ev[("ld", s)])
        if s >= 2:
            wait("vector", ST, ev[("st_xn", s - 2)])
        _emit_ln(S, p.a0x[r], p.a0xn[r])
        ev[("xn", s)] = CV.v
        wait("gpsimd", CV, ev[("xn", s)])
        dma("gpsimd", ST, p.xn_s[SUB * s:SUB * (s + 1), :], p.a0xn[r])
        ev[("st_xn", s)] = ST.v
        if s >= 2:
            wait("vector", CT, ev[("nt", s - 2)])
        _emit_ln(S, p.a0xn[r], p.a0z, affine_out=p.a0nb[r])
        ev[("nb", s)] = CV.v

        nb3 = p.a0nb[r].rearrange("p (a b) -> p a b", b=128)
        nfm_f = p.normfm[r].rearrange("p a b -> p (a b)")
        wait("tensor", CV, ev[("nb", s)])
        if s >= 1:
            wait("tensor", CV, ev[("krope", s - 1)])
            wait("tensor", CS, ev[("ktmev", s - 1)])
        for c in range(8):
            op("tensor", lambda h, c=c: h.transpose(
                p.PSB[:, 128 * c:128 * (c + 1)], nb3[:, c, :], p.idb_sb))
        mark("tensor", CT)
        ev[("nt", s)] = CT.v
        wait("scalar", CT, ev[("nt", s)])
        if s >= 2:
            wait("scalar", ST, ev[("st_nf", s - 2)])
        op("scalar", lambda h: h.activation(out=nfm_f, in_=p.PSB[:, 0:1024],
                                            func=AF.Copy), inc=(CS, 1))
        ev[("nev", s)] = CS.v
        wait("gpsimd", CS, ev[("nev", s)])
        dma("gpsimd", ST, p.normed_f_s[:, D * s:D * (s + 1)], nfm_f)
        ev[("st_nf", s)] = ST.v

        wait("tensor", CS, ev[("nev", s)])
        if s >= 1:
            wait("tensor", CS, ev[("vevac", s - 1)])
        for c in range(8):
            op("tensor", lambda h, c=c: h.ldweights(p.normfm[r][:, c, :]))
            for j in range(6):
                op("tensor", lambda h, c=c, j=j: h.matmul(
                    p.PS[:, 512 * j:512 * (j + 1)], p.normfm[r][:, c, :],
                    p.wkv_sb[:, c, 512 * j:512 * (j + 1)],
                    start=(c == 0), stop=(c == 7)))
        mark("tensor", CT)
        ev[("proj", s)] = CT.v

        wait("vector", CT, ev[("proj", s)])
        if s >= 2:
            wait("vector", ST, ev[("st_k", s - 2)])
        kps = p.PS[:, 0:1024].rearrange("p (h d) -> p h d", h=H0)
        for h0 in range(H0):
            k1, k2 = kps[:, h0, 0:128], kps[:, h0, 128:256]
            o1 = p.a0ktm[r][:, h0, 0:128]
            o2 = p.a0ktm[r][:, h0, 128:256]
            op("vector", lambda h, k1=k1: h.tensor_tensor(out=p.tmp1, in0=k1, in1=p.ropec_t[r], op=ALU.mult))
            op("vector", lambda h, k2=k2: h.tensor_tensor(out=p.tmp2, in0=k2, in1=p.ropes_t[r], op=ALU.mult))
            op("vector", lambda h, o1=o1: h.tensor_tensor(out=o1, in0=p.tmp1, in1=p.tmp2, op=ALU.subtract))
            op("vector", lambda h, k2=k2: h.tensor_tensor(out=p.tmp1, in0=k2, in1=p.ropec_t[r], op=ALU.mult))
            op("vector", lambda h, k1=k1: h.tensor_tensor(out=p.tmp2, in0=k1, in1=p.ropes_t[r], op=ALU.mult))
            op("vector", lambda h, o2=o2: h.tensor_tensor(out=o2, in0=p.tmp1, in1=p.tmp2, op=ALU.add))
        mark("vector", CV)
        ev[("krope", s)] = CV.v
        for h0 in range(H0):
            op("vector", lambda h, h0=h0: h.tensor_scalar(
                out=p.a0ktil[:, h0, :], in0=p.a0ktm[r][:, h0, :],
                scalar1=p.kdect_sb[:, h0:h0 + 1], scalar2=None, op0=ALU.mult))
        mark("vector", CV)
        ev[("ktil", s)] = CV.v
        wait("gpsimd", CV, ev[("krope", s)])
        dma("gpsimd", ST, p.ktm_s[SUB * s:SUB * (s + 1), :],
            p.a0ktm[r].rearrange("p h d -> p (h d)"))
        ev[("st_k", s)] = ST.v

        kt3 = p.a0ktm[r].rearrange("p h (c b) -> p (h c) b", b=128)
        kfm_f = p.a1kfm[r].rearrange("p a b -> p (a b)")
        wait("tensor", CV, ev[("krope", s)])
        wait("tensor", CS, ev[("nev", s)])
        for c in range(8):
            op("tensor", lambda h, c=c: h.transpose(
                p.PSB[:, 128 * c:128 * (c + 1)], kt3[:, c, :], p.idb_sb))
        mark("tensor", CT)
        ev[("ktt", s)] = CT.v
        wait("scalar", CT, ev[("ktt", s)])
        if s >= 2:
            wait("scalar", ST, ev[("st_kf", s - 2)])
        op("scalar", lambda h: h.activation(out=kfm_f, in_=p.PSB[:, 0:1024],
                                            func=AF.Copy), inc=(CS, 1))
        ev[("ktmev", s)] = CS.v
        wait("gpsimd", CS, ev[("ktmev", s)])
        dma("gpsimd", ST, p.ktm_f_s[:, D * s:D * (s + 1)], kfm_f)
        ev[("st_kf", s)] = ST.v

        wait("scalar", CT, ev[("proj", s)])
        if s >= 2:
            wait("scalar", ST, ev[("st_v", s - 2)])
        op("scalar", lambda h: h.activation(out=p.a0vtm[r], in_=p.PS[:, 1024:3072],
                                            func=AF.Copy), inc=(CS, 1))
        ev[("vevac", s)] = CS.v
        wait("gpsimd", CS, ev[("vevac", s)])
        dma("gpsimd", ST, p.vtm_s[SUB * s:SUB * (s + 1), :], p.a0vtm[r])
        ev[("st_v", s)] = ST.v

        wait("tensor", CV, ev[("ktil", s)])
        wait("tensor", CS, ev[("vevac", s)])
        for h0 in range(H0):
            if s + h0 > 0:
                prev = ev[("sst", s, h0 - 1)] if h0 > 0 else ev[("sst", s - 1, H0 - 1)]
                wait("tensor", CV, prev)
            for half in range(2):
                op("tensor", lambda h, h0=h0, half=half: h.ldweights(
                    p.a0ktil[:, h0, 128 * half:128 * (half + 1)]))
                op("tensor", lambda h, h0=h0, half=half: h.matmul(
                    p.PS[:, 3072 + 512 * half:3072 + 512 * (half + 1)],
                    p.a0ktil[:, h0, 128 * half:128 * (half + 1)],
                    p.a0vtm[r][:, 512 * h0:512 * (h0 + 1)],
                    start=True, stop=True))
            mark("tensor", CT)
            ev[("Y", s, h0)] = CT.v
            wait("vector", CT, ev[("Y", s, h0)])
            for half in range(2):
                op("vector", lambda h, h0=h0, half=half: h.scalar_tensor_tensor(
                    out=p.S_sb[:, h0, half, :], in0=p.S_sb[:, h0, half, :],
                    scalar=GC[h0], in1=p.PS[:, 3072 + 512 * half:3072 + 512 * (half + 1)],
                    op0=ALU.mult, op1=ALU.add))
            mark("vector", CV)
            ev[("sst", s, h0)] = CV.v

    p.ev_a0 = ev


def _x1_state_exchange(S):
    p = S.p
    op, dma, wait, mark = S.op, S.dma, S.wait, S.mark
    GW, CV, CS, CT, CP = p.GW, p.CV, p.CS, p.CT, p.CP
    xchf = p.xch.rearrange("p a h t d -> p (a h t d)")
    sflat = p.S_sb.rearrange("p h t d -> p (h t d)")
    for ps in range(3):
        for h0 in range(H0):
            op("vector", lambda h, ps=ps, h0=h0: h.tensor_scalar(
                out=p.xch[:, ps, h0, :, :], in0=p.S_sb[:, h0, :, :],
                scalar1=p.sxw_sb[:, ps * H0 + h0:ps * H0 + h0 + 1],
                scalar2=None, op0=ALU.mult))
    mark("vector", CV)
    wait("gpsimd", CV, CV.v)
    dma("gpsimd", p.STs[0], p.ccs_in, xchf)
    wait("gpsimd", p.STs[0], p.STs[0].v)
    wait("gpsimd", p.STs[1], p.STs[1].v)
    op("gpsimd", lambda h: h.collective_compute(
        "AllReduce", ALU.add, replica_groups=[[0, 1, 2, 3], [4, 5, 6, 7]],
        ins=[p.ccs_in.opt()], outs=[p.ccs_out.opt()]), inc=(CP, 1))
    wait("gpsimd", CP, CP.v)
    dma("gpsimd", GW, xchf, p.ccs_out)
    wait("vector", GW, GW.v)
    op("vector", lambda h: h.tensor_scalar(
        out=sflat, in0=p.xch[:, 0, :, :, :].rearrange("p h t d -> p (h t d)"),
        scalar1=p.sxm_sb[:, 0:1], scalar2=None, op0=ALU.mult))
    for ps in (1, 2):
        op("vector", lambda h, ps=ps: h.scalar_tensor_tensor(
            out=sflat, in0=p.xch[:, ps, :, :, :].rearrange("p h t d -> p (h t d)"),
            scalar=p.sxm_sb[:, ps:ps + 1], in1=sflat, op0=ALU.mult, op1=ALU.add))
    mark("vector", CV)
    p.ev_sinit = CV.v


def _phase_A1(S):
    p = S.p
    op, dma, wait, mark = S.op, S.dma, S.wait, S.mark
    GW, CV, CS, CT = p.GW, p.CV, p.CS, p.CT
    n_sub = p.n_sub
    ev = {}

    wait("gpsimd", CT, p.ev_a0[("proj", n_sub - 1)])
    wait("gpsimd", CV, p.ev_sinit)
    for c in range(8):
        dma("gpsimd", GW, p.wqg_sb[:, c, :], p.wqg_h[128 * c:128 * (c + 1), :])
    for c in range(16):
        dma("gpsimd", GW, p.wo0_sb[:, c, :], p.wo0_h[128 * c:128 * (c + 1), :])
    wait("tensor", GW, GW.v)

    for c in (p.LDs[0], p.LDs[1], p.GW, p.STs[0], p.STs[1], p.CV, p.CS, p.CT):
        wait("sync", c, c.v)

    for s in range(n_sub):
        r = s % 2
        LD = p.LDs[r]
        ST = p.STs[r]
        if s >= 2:
            wait("sync", CV, ev[("con", s - 2)])
        dma("sync", LD, p.a1nfm[r].rearrange("p a b -> p (a b)"),
            p.normed_f_s[:, D * s:D * (s + 1)])
        dma("sync", LD, p.a1kfm[r].rearrange("p a b -> p (a b)"),
            p.ktm_f_s[:, D * s:D * (s + 1)])
        dma("sync", LD, p.a1ktm[r], p.ktm_s[SUB * s:SUB * (s + 1), :])
        dma("sync", LD, p.a1vtm[r], p.vtm_s[SUB * s:SUB * (s + 1), :])
        dma("sync", LD, p.a1xn[r], p.xn_s[SUB * s:SUB * (s + 1), :])
        dma("sync", LD, p.ropec_t[r], p.ropecq_h[SUB * s:SUB * (s + 1), :])
        dma("sync", LD, p.ropes_t[r], p.ropesq_h[SUB * s:SUB * (s + 1), :])
        ev[("ld", s)] = LD.v

        wait("tensor", LD, ev[("ld", s)])
        if s >= 1:
            wait("tensor", CV, ev[("con", s - 1)])
            wait("tensor", CS, ev[("gsilu", s - 1)])
        if s == 0:
            wait("tensor", CV, p.ev_sinit)
            wait("tensor", CS, p.ev_a0[("vevac", n_sub - 1)])
        for c in range(8):
            op("tensor", lambda h, c=c: h.ldweights(p.a1nfm[r][:, c, :]))
            for j in range(6):
                op("tensor", lambda h, c=c, j=j: h.matmul(
                    p.PS[:, 512 * j:512 * (j + 1)], p.a1nfm[r][:, c, :],
                    p.wqg_sb[:, c, 512 * j:512 * (j + 1)],
                    start=(c == 0), stop=(c == 7)))
        mark("tensor", CT)
        ev[("qg", s)] = CT.v

        wait("vector", CT, ev[("qg", s)])
        wait("vector", LD, ev[("ld", s)])
        qps = p.PS[:, 0:1024].rearrange("p (h d) -> p h d", h=H0)
        for h0 in range(H0):
            q1, q2 = qps[:, h0, 0:128], qps[:, h0, 128:256]
            o1 = p.a1qtm[:, h0, 0:128]
            o2 = p.a1qtm[:, h0, 128:256]
            op("vector", lambda h, q1=q1: h.tensor_tensor(out=p.tmp1, in0=q1, in1=p.ropec_t[r], op=ALU.mult))
            op("vector", lambda h, q2=q2: h.tensor_tensor(out=p.tmp2, in0=q2, in1=p.ropes_t[r], op=ALU.mult))
            op("vector", lambda h, o1=o1: h.tensor_tensor(out=o1, in0=p.tmp1, in1=p.tmp2, op=ALU.subtract))
            op("vector", lambda h, q2=q2: h.tensor_tensor(out=p.tmp1, in0=q2, in1=p.ropec_t[r], op=ALU.mult))
            op("vector", lambda h, q1=q1: h.tensor_tensor(out=p.tmp2, in0=q1, in1=p.ropes_t[r], op=ALU.mult))
            op("vector", lambda h, o2=o2: h.tensor_tensor(out=o2, in0=p.tmp1, in1=p.tmp2, op=ALU.add))
        for h0 in range(H0):
            op("vector", lambda h, h0=h0: h.tensor_scalar(
                out=p.a1qctm[:, h0, :], in0=p.a1qtm[:, h0, :],
                scalar1=p.crosst_sb[:, h0:h0 + 1], scalar2=None, op0=ALU.mult))
        mark("vector", CV)
        ev[("qrope", s)] = CV.v

        wait("scalar", CT, ev[("qg", s)])
        op("scalar", lambda h: h.activation(out=p.a1gs, in_=p.PS[:, 1024:3072],
                                            func=AF.Silu), inc=(CS, 1))
        ev[("gsilu", s)] = CS.v

        qt3 = p.a1qtm.rearrange("p h (c b) -> p (h c) b", b=128)
        qct3 = p.a1qctm.rearrange("p h (c b) -> p (h c) b", b=128)
        qfm_f = p.a1qfm.rearrange("p a b -> p (a b)")
        qcfm_f = p.a1qcfm.rearrange("p a b -> p (a b)")
        wait("tensor", CV, ev[("qrope", s)])
        if s >= 1:
            wait("tensor", CV, ev[("og", s - 1, H0 - 1)])
            wait("tensor", CS, ev[("qev", s - 1)])
        for c in range(8):
            op("tensor", lambda h, c=c: h.transpose(
                p.PSB[:, 6144 + 128 * c:6144 + 128 * (c + 1)], qt3[:, c, :],
                p.idb_sb))
        for c in range(8):
            op("tensor", lambda h, c=c: h.transpose(
                p.PSB[:, 7168 + 128 * c:7168 + 128 * (c + 1)], qct3[:, c, :],
                p.idb_sb))
        mark("tensor", CT)
        ev[("qt", s)] = CT.v
        wait("scalar", CT, ev[("qt", s)])
        op("scalar", lambda h: h.activation(out=qfm_f, in_=p.PSB[:, 6144:7168],
                                            func=AF.Copy))
        op("scalar", lambda h: h.activation(out=qcfm_f, in_=p.PSB[:, 7168:8192],
                                            func=AF.Copy), inc=(CS, 1))
        ev[("qev", s)] = CS.v

        wait("vector", LD, ev[("ld", s)])
        ktmv = p.a1ktm[r].rearrange("p (hh d) -> p hh d", hh=H0)
        for h0 in range(H0):
            op("vector", lambda h, h0=h0: h.tensor_scalar(
                out=p.a0ktil[:, h0, :], in0=ktmv[:, h0, :],
                scalar1=p.kdect_sb[:, h0:h0 + 1], scalar2=None, op0=ALU.mult))
        mark("vector", CV)
        ev[("ktil", s)] = CV.v

        wait("tensor", CS, ev[("qev", s)])
        for h0 in range(H0):
            for half in range(2):
                c = 2 * h0 + half
                op("tensor", lambda h, c=c: h.ldweights(p.a1kfm[r][:, c, :]))
                op("tensor", lambda h, c=c, h0=h0, half=half: h.matmul(
                    p.PS[:, 3072 + 128 * h0:3072 + 128 * (h0 + 1)],
                    p.a1kfm[r][:, c, :], p.a1qfm[:, c, :],
                    start=(half == 0), stop=(half == 1)))
            mark("tensor", CT)
            ev[("attnT", s, h0)] = CT.v
            wait("vector", CT, ev[("attnT", s, h0)])
            op("vector", lambda h, h0=h0: h.tensor_tensor(
                out=p.a1at[:, h0, :], in0=p.PS[:, 3072 + 128 * h0:3072 + 128 * (h0 + 1)],
                in1=p.dmask_sb[:, 128 * h0:128 * (h0 + 1)], op=ALU.mult), inc=(CV, 1))
            ev[("atm", s, h0)] = CV.v
            wait("tensor", CV, ev[("atm", s, h0)])
            if h0 > 0:
                wait("tensor", CV, ev[("og", s, h0 - 1)])
            op("tensor", lambda h, h0=h0: h.ldweights(p.a1at[:, h0, :]))
            op("tensor", lambda h, h0=h0: h.matmul(
                p.PS[:, 3584:4096], p.a1at[:, h0, :],
                p.a1vtm[r][:, 512 * h0:512 * (h0 + 1)], start=True, stop=False))
            for half in range(2):
                c = 2 * h0 + half
                op("tensor", lambda h, c=c: h.ldweights(p.a1qcfm[:, c, :]))
                op("tensor", lambda h, c=c, h0=h0, half=half: h.matmul(
                    p.PS[:, 3584:4096], p.a1qcfm[:, c, :],
                    p.S_sb[:, h0, half, :], start=False, stop=(half == 1)))
            mark("tensor", CT)
            ev[("o", s, h0)] = CT.v
            wait("scalar", CT, ev[("o", s, h0)])
            op("scalar", lambda h: h.activation(out=p.trash[:, 0:512], in_=p.PS[:, 3584:4096],
                                                func=AF.Square, accum_out=p.sums[:, 2:3]),
               inc=(CS, 1))
            wait("scalar", CS, CS.v)
            op("scalar", lambda h: h.activation(out=p.mu2, in_=p.sums[:, 2:3],
                                                func=AF.Sqrt, scale=1.0 / 512.0,
                                                bias=EPS0), inc=(CS, 1))
            ev[("osd", s, h0)] = CS.v
            wait("vector", CS, ev[("osd", s, h0)])
            wait("vector", CS, ev[("gsilu", s)])
            op("vector", lambda h: h.reciprocal(out=p.mu2, in_=p.mu2), inc=(CV, 1))
            wait("vector", CV, CV.v)
            op("vector", lambda h: h.scalar_tensor_tensor(
                out=p.tmp3, in0=p.PS[:, 3584:4096], scalar=p.mu2,
                in1=p.gnw_sb, op0=ALU.mult, op1=ALU.mult))
            op("vector", lambda h, h0=h0: h.tensor_tensor(
                out=p.a1og[:, h0, :], in0=p.tmp3,
                in1=p.a1gs[:, 512 * h0:512 * (h0 + 1)], op=ALU.mult), inc=(CV, 1))
            ev[("og", s, h0)] = CV.v

        wait("tensor", CV, ev[("ktil", s)])
        for h0 in range(H0):
            wait("tensor", CV, ev[("og", s, h0)])
            if h0 > 0:
                wait("tensor", CV, ev[("sst", s, h0 - 1)])
            elif s >= 1:
                wait("tensor", CV, ev[("sst", s - 1, H0 - 1)])
            for half in range(2):
                op("tensor", lambda h, h0=h0, half=half: h.ldweights(
                    p.a0ktil[:, h0, 128 * half:128 * (half + 1)]))
                op("tensor", lambda h, h0=h0, half=half: h.matmul(
                    p.PS[:, 512 * half:512 * (half + 1)],
                    p.a0ktil[:, h0, 128 * half:128 * (half + 1)],
                    p.a1vtm[r][:, 512 * h0:512 * (h0 + 1)], start=True, stop=True))
            mark("tensor", CT)
            ev[("Y", s, h0)] = CT.v
            wait("vector", CT, ev[("Y", s, h0)])
            for half in range(2):
                op("vector", lambda h, h0=h0, half=half: h.scalar_tensor_tensor(
                    out=p.S_sb[:, h0, half, :], in0=p.S_sb[:, h0, half, :],
                    scalar=GC[h0], in1=p.PS[:, 512 * half:512 * (half + 1)],
                    op0=ALU.mult, op1=ALU.add))
            mark("vector", CV)
            ev[("sst", s, h0)] = CV.v

        og3 = p.a1og.rearrange("p h (c b) -> p (h c) b", b=128)
        ogfm_f = p.a1ogfm.rearrange("p a b -> p (a b)")
        wait("tensor", CV, ev[("og", s, H0 - 1)])
        wait("tensor", CV, ev[("sst", s, H0 - 1)])
        if s >= 1:
            wait("tensor", CS, ev[("ogev", s - 1)])
        for c in range(16):
            op("tensor", lambda h, c=c: h.transpose(
                p.PSB[:, 128 * c:128 * (c + 1)], og3[:, c, :], p.idb_sb))
        mark("tensor", CT)
        ev[("ogt", s)] = CT.v
        wait("scalar", CT, ev[("ogt", s)])
        op("scalar", lambda h: h.activation(out=ogfm_f, in_=p.PSB[:, 0:2048],
                                            func=AF.Copy), inc=(CS, 1))
        ev[("ogev", s)] = CS.v
        wait("tensor", CS, ev[("ogev", s)])
        if s >= 1:
            wait("tensor", CV, ev[("s0", s - 1)])
        for c in range(16):
            op("tensor", lambda h, c=c: h.ldweights(p.a1ogfm[:, c, :]))
            for j in range(2):
                op("tensor", lambda h, c=c, j=j: h.matmul(
                    p.PS[:, 1024 + 512 * j:1024 + 512 * (j + 1)],
                    p.a1ogfm[:, c, :], p.wo0_sb[:, c, 512 * j:512 * (j + 1)],
                    start=(c == 0), stop=(c == 15)))
        mark("tensor", CT)
        ev[("wo", s)] = CT.v
        wait("vector", CT, ev[("wo", s)])
        if s >= 2:
            wait("vector", ST, ev[("st_s0", s - 2)])
        op("vector", lambda h: h.tensor_tensor(out=p.a1s0[r], in0=p.PS[:, 1024:2048],
                                               in1=p.a1xn[r], op=ALU.add), inc=(CV, 1))
        ev[("s0", s)] = CV.v
        wait("gpsimd", CV, ev[("s0", s)])
        dma("gpsimd", ST, p.s0_s[SUB * s:SUB * (s + 1), :], p.a1s0[r])
        ev[("st_s0", s)] = ST.v
        mark("vector", CV)
        ev[("con", s)] = CV.v

    p.ev_a1 = ev


def _fence(S):
    p = S.p
    for e in ("sync", "gpsimd", "vector", "scalar", "tensor"):
        for c in (p.LDs[0], p.LDs[1], p.GW, p.STs[0], p.STs[1],
                  p.CV, p.CS, p.CT, p.CP):
            S.wait(e, c, c.v)


def _wquant_fp4(S, w_h, wfm_sb, nrow, ncol):
    p = S.p
    t = p.qt
    op, dma, wait, mark = S.op, S.dma, S.wait, S.mark
    CV, CT = p.CV, p.CT
    guard = [0, 0]
    nblk = ncol // 128
    wqb3 = t.wqb[:, 0:ncol].rearrange("p (a b) -> p a b", b=128)
    psv = p.PSB[:, 0:ncol].rearrange("p (a b) -> p a b", b=128)
    evv = 0
    wait("sync", CV, CV.v)
    for i in range(nrow // 128):
        r = i % 2
        LD = p.LDs[r]
        if i >= 2:
            wait("sync", CV, guard[r])
        dma("sync", LD, t.wslab[r][:, 0:ncol], w_h[128 * i:128 * (i + 1), :])
        ldv = LD.v
        wait("vector", LD, ldv)
        if i >= 1:
            wait("vector", CT, trv)
        cvs = []
        for (a, wdt) in _islices(ncol, 1408):
            cvs.append(_emit_fp4(S, t.wslab[r][:, a:a + wdt], t.wqb[:, a:a + wdt], wdt))
        wait("tensor", CV, cvs[-1])
        if i >= 1:
            wait("tensor", CV, evv)
        for c in range(nblk):
            op("tensor", lambda h, c=c: h.transpose(
                p.PSB[:, 128 * c:128 * (c + 1)], wqb3[:, c, :], p.idb_sb))
        mark("tensor", CT)
        trv = CT.v
        wait("vector", CT, trv)
        op("vector", lambda h, i=i: h.tensor_copy(
            out=wfm_sb[:, :, 128 * i:128 * (i + 1)], in_=psv), inc=(CV, 1))
        evv = CV.v
        guard[r] = CV.v
    return evv


def _wquant_int8(S, w_h, wfm_sb, nrow, ncol):
    p = S.p
    t = p.qt
    op, dma, wait, mark = S.op, S.dma, S.wait, S.mark
    CV, CT = p.CV, p.CT
    guard = [0, 0]
    nblk = ncol // 128
    wqb3 = t.wqb[:, 0:ncol].rearrange("p (a b) -> p a b", b=128)
    psv = p.PSB[:, 0:ncol].rearrange("p (a b) -> p a b", b=128)
    evv = 0
    trv = 0
    wait("sync", CV, CV.v)
    for i in range(nrow // 128):
        r = i % 2
        LD = p.LDs[r]
        if i >= 2:
            wait("sync", CV, guard[r])
        dma("sync", LD, t.wslab[r][:, 0:ncol], w_h[128 * i:128 * (i + 1), :])
        ldv = LD.v
        wait("vector", LD, ldv)
        if i >= 1:
            wait("vector", CT, trv)
        op("vector", lambda h: h.reduce_max(out=p.rm, in_=t.wslab[r][:, 0:ncol],
                                            axis=AX.X, apply_absolute_value=True))
        op("vector", lambda h: h.reciprocal(out=p.rr1, in_=p.rm))
        op("vector", lambda h: h.tensor_scalar(out=p.rr1, in0=p.rr1, scalar1=127.0,
                                               scalar2=None, op0=ALU.mult))
        op("vector", lambda h: h.tensor_scalar(out=p.rm, in0=p.rm, scalar1=1.0 / 127.0,
                                               scalar2=None, op0=ALU.mult), inc=(CV, 1))
        wait("vector", CV, CV.v)
        for (a, wdt) in _islices(ncol, 1408):
            op("vector", lambda h, a=a, wdt=wdt: h.tensor_scalar(
                out=t.w[:, 0:wdt], in0=t.wslab[r][:, a:a + wdt],
                scalar1=p.rr1, scalar2=None, op0=ALU.mult), inc=(CV, 1))
            _emit_round_clip(S, t.w[:, 0:wdt], t.y0[:, 0:wdt], wdt)
            op("vector", lambda h, a=a, wdt=wdt: h.tensor_scalar(
                out=t.wqb[:, a:a + wdt], in0=t.y0[:, 0:wdt],
                scalar1=p.rm, scalar2=None, op0=ALU.mult), inc=(CV, 1))
        cv = CV.v
        wait("tensor", CV, cv)
        if i >= 1:
            wait("tensor", CV, evv)
        for c in range(nblk):
            op("tensor", lambda h, c=c: h.transpose(
                p.PSB[:, 128 * c:128 * (c + 1)], wqb3[:, c, :], p.idb_sb))
        mark("tensor", CT)
        trv = CT.v
        wait("vector", CT, trv)
        op("vector", lambda h, i=i: h.tensor_copy(
            out=wfm_sb[:, :, 128 * i:128 * (i + 1)], in_=psv), inc=(CV, 1))
        evv = CV.v
        guard[r] = CV.v
    return evv


def _phase_B1(S):
    p = S.p
    op, dma, wait, mark = S.op, S.dma, S.wait, S.mark
    GW, CV, CS, CT = p.GW, p.CV, p.CS, p.CT
    n_sub = p.n_sub
    ev = {}
    _fence(S)
    _wquant_fp4(S, p.wg0_h, p.wg_fm, I0, D)
    wq_ev = _wquant_fp4(S, p.wu0_h, p.wu_fm, I0, D)
    wait("tensor", CV, wq_ev)
    wait("tensor", p.LDs[0], p.LDs[0].v)
    wait("tensor", p.LDs[1], p.LDs[1].v)
    wait("sync", CV, wq_ev)
    wait("sync", p.LDs[0], p.LDs[0].v)
    wait("sync", p.LDs[1], p.LDs[1].v)

    for s in range(n_sub):
        r = s % 2
        LD = p.LDs[r]
        ST = p.STs[r]
        if s >= 1:
            wait("sync", CV, ev[("xq", s - 1)])
        dma("sync", LD, p.bxn[r], p.xn_s[SUB * s:SUB * (s + 1), :])
        ev[("ld", s)] = LD.v
        wait("vector", LD, ev[("ld", s)])
        wait("scalar", LD, ev[("ld", s)])
        if s >= 1:
            wait("vector", CT, ev[("up", s - 1)])
        cv = _emit_fp4(S, p.bxn[r], p.bxq[:, 0:D], D)
        ev[("xq", s)] = cv
        bxq3 = p.bxq[:, 0:D].rearrange("p (a b) -> p a b", b=128)
        bxqfm_f = p.bxqfm.rearrange("p a b -> p (a b)")
        wait("tensor", CV, cv)
        if s >= 1:
            wait("tensor", CS, ev[("xqev", s - 1)])
            wait("tensor", CV, ev[("hq", s - 1)])
        for c in range(8):
            op("tensor", lambda h, c=c: h.transpose(
                p.PSB[:, 7168 + 128 * c:7168 + 128 * (c + 1)], bxq3[:, c, :],
                p.idb_sb))
        mark("tensor", CT)
        ev[("xqt", s)] = CT.v
        wait("scalar", CT, ev[("xqt", s)])
        op("scalar", lambda h: h.activation(out=bxqfm_f, in_=p.PSB[:, 7168:8192],
                                            func=AF.Copy), inc=(CS, 1))
        ev[("xqev", s)] = CS.v
        wait("tensor", CS, ev[("xqev", s)])
        for c in range(8):
            op("tensor", lambda h, c=c: h.ldweights(p.bxqfm[:, c, :]))
            for (a, wdt) in _islices(I0, 512):
                op("tensor", lambda h, c=c, a=a, wdt=wdt: h.matmul(
                    p.PS[:, a:a + wdt], p.bxqfm[:, c, :], p.wg_fm[:, c, a:a + wdt],
                    start=(c == 0), stop=(c == 7)))
        mark("tensor", CT)
        ev[("gate", s)] = CT.v
        wait("scalar", CT, ev[("gate", s)])
        op("scalar", lambda h: h.activation(out=p.bgs, in_=p.PS[:, 0:I0],
                                            func=AF.Silu), inc=(CS, 1))
        ev[("gs", s)] = CS.v
        wait("tensor", CS, ev[("gs", s)])
        for c in range(8):
            op("tensor", lambda h, c=c: h.ldweights(p.bxqfm[:, c, :]))
            for (a, wdt) in _islices(I0, 512):
                op("tensor", lambda h, c=c, a=a, wdt=wdt: h.matmul(
                    p.PS[:, a:a + wdt], p.bxqfm[:, c, :], p.wu_fm[:, c, a:a + wdt],
                    start=(c == 0), stop=(c == 7)))
        mark("tensor", CT)
        ev[("up", s)] = CT.v
        wait("vector", CT, ev[("up", s)])
        op("vector", lambda h: h.tensor_tensor(out=p.bgs, in0=p.PS[:, 0:I0],
                                               in1=p.bgs, op=ALU.mult), inc=(CV, 1))
        ev[("h", s)] = CV.v
        for (a, wdt) in _islices(I0, 1408):
            cv = _emit_fp4(S, p.bgs[:, a:a + wdt], p.bhq[:, a:a + wdt], wdt)
        ev[("hq", s)] = cv
        bhq3 = p.bhq[:, 0:I0].rearrange("p (a b) -> p a b", b=128)
        bhqfm_f = p.bhqfm.rearrange("p a b -> p (a b)")
        wait("tensor", CV, ev[("hq", s)])
        if s >= 1:
            wait("tensor", p.STs[1 - r], ev[("st_hq", s - 1)])
            wait("tensor", CS, ev[("hqev", s - 1, 1)])
        for w in range(2):
            if w == 1:
                wait("tensor", CS, ev[("hqev", s, 0)])
            for c in range(11):
                op("tensor", lambda h, w=w, c=c: h.transpose(
                    p.PSB[:, 5632 + 128 * c:5632 + 128 * (c + 1)],
                    bhq3[:, 11 * w + c, :], p.idb_sb))
            mark("tensor", CT)
            ev[("hqt", s, w)] = CT.v
            wait("scalar", CT, ev[("hqt", s, w)])
            op("scalar", lambda h, w=w: h.activation(
                out=bhqfm_f[:, 1408 * w:1408 * (w + 1)], in_=p.PSB[:, 5632:7040],
                func=AF.Copy), inc=(CS, 1))
            ev[("hqev", s, w)] = CS.v
        wait("gpsimd", CS, ev[("hqev", s, 1)])
        dma("gpsimd", ST, p.hq0f_s[:, I0 * s:I0 * (s + 1)], bhqfm_f)
        ev[("st_hq", s)] = ST.v

    p.ev_b1 = ev


def _phase_B2(S):
    p = S.p
    op, dma, wait, mark = S.op, S.dma, S.wait, S.mark
    GW, CV, CS, CT = p.GW, p.CV, p.CS, p.CT
    n_sub = p.n_sub
    ev = {}
    _fence(S)
    wq_ev = _wquant_fp4(S, p.wd0_h, p.wd_fm, D, I0)
    wait("tensor", CV, wq_ev)
    wait("tensor", p.LDs[0], p.LDs[0].v)
    wait("tensor", p.LDs[1], p.LDs[1].v)
    wait("sync", CV, wq_ev)
    wait("sync", p.LDs[0], p.LDs[0].v)
    wait("sync", p.LDs[1], p.LDs[1].v)

    for s in range(n_sub):
        r = s % 2
        LD = p.LDs[r]
        ST = p.STs[r]
        if s >= 1:
            wait("sync", CV, ev[("x1", s - 1)])
            wait("sync", CT, ev[("down", s - 1)])
        dma("sync", LD, p.bhqfm.rearrange("p a b -> p (a b)"),
            p.hq0f_s[:, I0 * s:I0 * (s + 1)])
        dma("sync", LD, p.bx[r], p.x_h[SUB * s:SUB * (s + 1), :])
        dma("sync", LD, p.bs0[r], p.s0_s[SUB * s:SUB * (s + 1), :])
        ev[("ld", s)] = LD.v
        wait("tensor", LD, ev[("ld", s)])
        if s >= 1:
            wait("tensor", CS, ev[("g0", s - 1)])
        for c in range(22):
            op("tensor", lambda h, c=c: h.ldweights(p.bhqfm[:, c, :]))
            for j in range(2):
                op("tensor", lambda h, c=c, j=j: h.matmul(
                    p.PS[:, 512 * j:512 * (j + 1)], p.bhqfm[:, c, :],
                    p.wd_fm[:, c, 512 * j:512 * (j + 1)],
                    start=(c == 0), stop=(c == 21)))
        mark("tensor", CT)
        ev[("down", s)] = CT.v
        wait("scalar", CT, ev[("down", s)])
        op("scalar", lambda h: h.activation(out=p.bg0, in_=p.PS[:, 0:1024],
                                            func=AF.Silu), inc=(CS, 1))
        ev[("g0", s)] = CS.v
        wait("vector", CS, ev[("g0", s)])
        wait("vector", LD, ev[("ld", s)])
        if s >= 1:
            wait("vector", p.STs[1 - r], ev[("st", s - 1)])
        op("vector", lambda h: h.tensor_tensor(out=p.bg0, in0=p.bs0[r], in1=p.bg0,
                                               op=ALU.mult))
        op("vector", lambda h: h.tensor_tensor(out=p.bx1[r], in0=p.bg0, in1=p.bx[r],
                                               op=ALU.add), inc=(CV, 1))
        ev[("x1", s)] = CV.v
        wait("gpsimd", CV, ev[("x1", s)])
        dma("gpsimd", ST, p.x1_s[SUB * s:SUB * (s + 1), :], p.bx1[r])
        ev[("st", s)] = ST.v

    p.ev_b2 = ev


def _phase_C(S):
    p = S.p
    op, dma, wait, mark = S.op, S.dma, S.wait, S.mark
    GW, CV, CS, CT = p.GW, p.CV, p.CS, p.CT
    n_sub = p.n_sub
    ev = {}
    _fence(S)
    dma("gpsimd", GW, p.wkv1_sb.rearrange("p a b -> p (a b)"), p.wkv1_h)
    wait("tensor", GW, GW.v)
    op("vector", lambda h: h.memset(p.PS[:, 2048:4096], 0.0))
    op("vector", lambda h: h.memset(p.cvext[0].rearrange("p a b -> p (a b)"), 0.0))
    op("vector", lambda h: h.memset(p.cvext[1].rearrange("p a b -> p (a b)"), 0.0))
    op("vector", lambda h: h.memset(p.cvext[0][:, :, 64:65].rearrange("p a b -> p (a b)"), 1.0))
    op("vector", lambda h: h.memset(p.cvext[1][:, :, 64:65].rearrange("p a b -> p (a b)"), 1.0))
    op("vector", lambda h: h.memset(p.mxg, 0.0), inc=(CV, 1))
    cstop = _STOP[0] if (_STOP[0] or "").startswith("C.") else None
    if cstop == "C.pre":
        return

    for s in range(n_sub):
        r = s % 2
        LD = p.LDs[r]
        ST = p.STs[r]
        if s >= 1:
            wait("sync", CV, ev[("xn1", s - 1)])
        dma("sync", LD, p.cx1[r], p.x1_s[SUB * s:SUB * (s + 1), :])
        ev[("ld", s)] = LD.v
        wait("vector", LD, ev[("ld", s)])
        wait("scalar", LD, ev[("ld", s)])
        if s >= 1:
            wait("vector", p.STs[1 - r], ev[("st", s - 1)])
            wait("vector", CS, ev[("xn1b", s - 1)])
            if ("xnt", s - 1) in ev:
                wait("vector", CT, ev[("xnt", s - 1)])
        _emit_ln(S, p.cx1[r], p.cxn1[r])
        ev[("xn1", s)] = CV.v
        wait("scalar", CV, ev[("xn1", s)])
        if ("xnt", s - 2) in ev:
            wait("scalar", CT, ev[("xnt", s - 2)])
        op("scalar", lambda h: h.activation(out=p.cxn1b[r], in_=p.cxn1[r],
                                            func=AF.Copy), inc=(CS, 1))
        ev[("xn1b", s)] = CS.v
        wait("gpsimd", CV, ev[("xn1", s)])
        dma("gpsimd", ST, p.xn1_s[SUB * s:SUB * (s + 1), :], p.cxn1[r])
        ev[("st", s)] = ST.v
        op("vector", lambda h: h.reduce_max(out=p.mx1, in_=p.cxn1[r], axis=AX.X,
                                            apply_absolute_value=True))
        op("vector", lambda h: h.tensor_tensor(out=p.mxg, in0=p.mxg, in1=p.mx1,
                                               op=ALU.max), inc=(CV, 1))
        if cstop == "C.ln":
            continue
        cxb3 = p.cxn1b[r].rearrange("p (a b) -> p a b", b=128)
        cxfm_f = p.cxfm.rearrange("p a b -> p (a b)")
        wait("tensor", CS, ev[("xn1b", s)])
        if ("vex", s - 1) in ev:
            wait("tensor", CV, ev[("vex", s - 1)])
        if ("xfmev", s - 1) in ev:
            wait("tensor", CS, ev[("xfmev", s - 1)])
        for c in range(8):
            op("tensor", lambda h, c=c: h.transpose(
                p.PSB[:, 128 * c:128 * (c + 1)], cxb3[:, c, :], p.idb_sb))
        mark("tensor", CT)
        ev[("xnt", s)] = CT.v
        wait("scalar", CT, ev[("xnt", s)])
        if s >= 1:
            wait("scalar", p.STs[1 - r], ev[("st_xf", s - 1)])
        op("scalar", lambda h: h.activation(out=cxfm_f, in_=p.PSB[:, 0:1024],
                                            func=AF.Copy), inc=(CS, 1))
        ev[("xfmev", s)] = CS.v
        wait("gpsimd", CS, ev[("xfmev", s)])
        dma("gpsimd", ST, p.xn1f_s[:, D * s:D * (s + 1)], cxfm_f)
        ev[("st_xf", s)] = ST.v
        if cstop == "C.xfm":
            continue
        wait("tensor", CS, ev[("xfmev", s)])
        for pr in range(8):
            op("tensor", lambda h, pr=pr: h.ldweights(p.cxfm[:, pr, :]))
            op("tensor", lambda h, pr=pr: h.matmul(
                p.PS[:, 256 * pr:256 * (pr + 1)],
                p.cxfm[:, pr, :], p.wkv1_sb[:, pr, :],
                start=(pr % 2 == 0), stop=(pr % 2 == 1)))
        mark("tensor", CT)
        ev[("kvproj", s)] = CT.v
        if cstop == "C.proj":
            continue
        kview = p.PS[:, 0:2048].rearrange("p (hd kv e) -> p hd kv e", kv=2, e=64)
        kq = kview[:, :, 0, :]
        vq = kview[:, :, 1, :]
        ckn3 = p.ckn.rearrange("p (a b) -> p a b", b=64)
        cke3 = p.cke.rearrange("p (a b) -> p a b", b=64)
        cphik3 = p.cphik.rearrange("p (a b) -> p a b", b=64)
        wait("vector", CT, ev[("kvproj", s)])
        op("vector", lambda h: h.tensor_scalar(out=ckn3, in0=kq, scalar1=0.0,
                                               scalar2=None, op0=ALU.min), inc=(CV, 1))
        ev[("kneg", s)] = CV.v
        if cstop == "C.kneg":
            continue
        wait("scalar", CV, ev[("kneg", s)])
        op("scalar", lambda h: h.activation(out=p.cke, in_=p.ckn, func=AF.Exp),
           inc=(CS, 1))
        ev[("kexp", s)] = CS.v
        if cstop == "C.kexp":
            continue
        wait("vector", CS, ev[("kexp", s)])
        if ("kv", s - 1) in ev:
            wait("vector", CT, ev[("kv", s - 1)])
        op("vector", lambda h: h.scalar_tensor_tensor(
            out=cphik3, in0=kq, scalar=0.0, in1=cke3, op0=ALU.max, op1=ALU.add),
           inc=(CV, 1))
        ev[("phik", s)] = CV.v
        if cstop == "C.stt":
            continue
        if ("kv", s - 2) in ev:
            wait("vector", CT, ev[("kv", s - 2)])
        op("vector", lambda h: h.tensor_copy(
            out=p.cvext[r][:, :, 0:64], in_=vq), inc=(CV, 1))
        ev[("vex", s)] = CV.v
        if cstop == "C.phik":
            continue
        wait("tensor", CV, ev[("vex", s)])
        phv = p.cphik.rearrange("p (hd e) -> p hd e", e=64)
        for hd in range(16):
            col = 2048 + 128 * hd
            op("tensor", lambda h, hd=hd: h.ldweights(phv[:, hd, :]))
            op("tensor", lambda h, hd=hd, col=col, s=s: h.matmul(
                p.PS[0:64, col:col + 72], phv[:, hd, :], p.cvext[r][:, hd, :],
                start=(s == 0 and hd % 4 == 0),
                stop=(s == n_sub - 1 and hd % 4 == 3)))
        mark("tensor", CT)
        ev[("kv", s)] = CT.v
        mark("vector", CV)
        ev[("con", s)] = CV.v

    p.ev_c = ev


def _x2_kv_max(S):
    p = S.p
    op, dma, wait, mark = S.op, S.dma, S.wait, S.mark
    GW, CV, CS, CT, CP = p.GW, p.CV, p.CS, p.CT, p.CP
    _fence(S)
    wait("vector", CT, p.ev_c[("kv", p.n_sub - 1)])
    op("vector", lambda h: h.tensor_copy(out=p.kvsb, in_=p.PS[:, 2048:4096]),
       inc=(CV, 1))
    wait("gpsimd", CV, CV.v)
    dma("gpsimd", p.STs[0], p.cckv_in, p.kvsb)
    wait("tensor", CV, CV.v)
    op("tensor", lambda h: h.transpose(p.PS[0:1, 0:128], p.mxg, p.identf_sb))
    mark("tensor", CT)
    wait("vector", CT, CT.v)
    op("vector", lambda h: h.reduce_max(out=p.mxs, in_=p.PS[0:1, 0:128], axis=AX.X),
       inc=(CV, 1))
    wait("tensor", CV, CV.v)
    op("tensor", lambda h: h.matmul(p.PS[:, 512:513], p.onesf_sb, p.mxs,
                                    start=True, stop=True))
    mark("tensor", CT)
    wait("vector", CT, CT.v)
    op("vector", lambda h: h.tensor_copy(out=p.mx1, in_=p.PS[:, 512:513]),
       inc=(CV, 1))
    wait("gpsimd", CV, CV.v)
    dma("gpsimd", p.STs[0], p.ccm1_in, p.mx1)
    wait("gpsimd", p.STs[0], p.STs[0].v)
    op("gpsimd", lambda h: h.collective_compute(
        "AllReduce", ALU.add, replica_groups=[[0, 1, 2, 3], [4, 5, 6, 7]],
        ins=[p.cckv_in.opt()], outs=[p.cckv_out.opt()]), inc=(CP, 1))
    op("gpsimd", lambda h: h.collective_compute(
        "AllReduce", ALU.max, replica_groups=[[0, 1, 2, 3, 4, 5, 6, 7]],
        ins=[p.ccm1_in.opt()], outs=[p.ccm1_out.opt()]), inc=(CP, 1))
    wait("gpsimd", CP, CP.v)
    dma("gpsimd", GW, p.kvsb, p.cckv_out)
    dma("gpsimd", GW, p.mx1, p.ccm1_out)
    wait("vector", GW, GW.v)
    op("vector", lambda h: h.tensor_copy(out=p.kvb, in_=p.kvsb), inc=(CV, 1))
    wait("scalar", GW, GW.v)
    op("scalar", lambda h: h.activation(out=p.clip_sg, in_=p.clip_sb,
                                        func=AF.Sigmoid), inc=(CS, 1))
    wait("tensor", CS, CS.v)
    op("tensor", lambda h: h.matmul(p.PS[:, 520:522], p.onesf_sb, p.clip_sg,
                                    start=True, stop=True))
    mark("tensor", CT)
    wait("vector", CT, CT.v)
    op("vector", lambda h: h.tensor_tensor(out=p.rsx, in0=p.mx1, in1=p.PS[:, 520:521],
                                           op=ALU.mult))
    op("vector", lambda h: h.tensor_scalar(out=p.rsx, in0=p.rsx, scalar1=1.0 / 127.0,
                                           scalar2=None, op0=ALU.mult), inc=(CV, 1))
    wait("vector", CV, CV.v)
    op("vector", lambda h: h.tensor_copy(out=p.sx, in_=p.rsx), inc=(CV, 1))
    wait("vector", CV, CV.v)
    op("vector", lambda h: h.reciprocal(out=p.rsx, in_=p.rsx))
    op("vector", lambda h: h.tensor_copy(out=p.sgd, in_=p.PS[:, 521:522]),
       inc=(CV, 1))
    p.ev_rsx = CV.v


def _phase_D1(S):
    p = S.p
    op, dma, wait, mark = S.op, S.dma, S.wait, S.mark
    GW, CV, CS, CT = p.GW, p.CV, p.CS, p.CT
    n_sub = p.n_sub
    ev = {}
    _fence(S)
    dma("gpsimd", GW, p.wq1_sb.rearrange("p a b -> p (a b)"), p.wq1_h)
    for c in range(8):
        dma("gpsimd", GW, p.wo1_sb[:, c, :], p.wo1_h[128 * c:128 * (c + 1), :])
    wait("tensor", GW, GW.v)

    for s in range(n_sub):
        r = s % 2
        LD = p.LDs[r]
        ST = p.STs[r]
        if s >= 2:
            wait("sync", CV, ev[("con", s - 2)])
        if s >= 1:
            wait("sync", CT, ev[("qproj", s - 1)])
        dma("sync", LD, p.cxfm.rearrange("p a b -> p (a b)"),
            p.xn1f_s[:, D * s:D * (s + 1)])
        ev[("ld", s)] = LD.v

        wait("tensor", LD, ev[("ld", s)])
        if s >= 1:
            wait("tensor", CV, ev[("phiq", s - 1)])
            wait("tensor", CS, ev[("s1evac", s - 1)])
        for hd in range(16):
            op("tensor", lambda h, hd=hd: h.ldweights(p.wq1_sb[:, hd, :]))
            op("tensor", lambda h, hd=hd: h.matmul(
                p.PS[0:64, 128 * hd:128 * (hd + 1)],
                p.wq1_sb[:, hd, :], p.cxfm[:, hd // 2, :],
                start=(hd % 4 == 0), stop=(hd % 4 == 3)))
        mark("tensor", CT)
        ev[("qproj", s)] = CT.v
        wait("vector", CT, ev[("qproj", s)])
        op("vector", lambda h: h.tensor_scalar(out=p.dknq[0:64, :],
                                               in0=p.PS[0:64, 0:2048],
                                               scalar1=0.0, scalar2=None, op0=ALU.min),
           inc=(CV, 1))
        ev[("qneg", s)] = CV.v
        wait("scalar", CV, ev[("qneg", s)])
        op("scalar", lambda h: h.activation(out=p.dkeq[0:64, :], in_=p.dknq[0:64, :],
                                            func=AF.Exp), inc=(CS, 1))
        ev[("qexp", s)] = CS.v
        wait("vector", CS, ev[("qexp", s)])
        op("vector", lambda h: h.scalar_tensor_tensor(
            out=p.dphiq[0:64, :], in0=p.PS[0:64, 0:2048], scalar=0.0,
            in1=p.dkeq[0:64, :], op0=ALU.max, op1=ALU.add), inc=(CV, 1))
        ev[("phiq", s)] = CV.v

        wait("tensor", CV, ev[("phiq", s)])
        if s >= 1:
            wait("tensor", CV, ev[("comb", s - 1)])
        for hd in range(16):
            lhs = p.dphiq[0:64, 128 * hd:128 * (hd + 1)]
            op("tensor", lambda h, lhs=lhs: h.ldweights(lhs))
            op("tensor", lambda h, lhs=lhs, hd=hd: h.matmul(
                p.PS[:, 2048 + 128 * hd:2048 + 128 * hd + 72],
                lhs, p.kvb3[0:64, hd, 0:72],
                start=(hd % 4 == 0), stop=(hd % 4 == 3)))
        mark("tensor", CT)
        ev[("ctx", s)] = CT.v
        wait("vector", CT, ev[("ctx", s)])
        if s >= 1:
            wait("vector", CT, ev[("dct", s - 1)])
        ctxv = p.PS[:, 2048:4096].rearrange("p (hd e) -> p hd e", e=128)
        op("vector", lambda h: h.tensor_scalar(
            out=p.drz, in0=ctxv[:, :, 64:65].rearrange("p hd o -> p (hd o)"),
            scalar1=EPS1, scalar2=None, op0=ALU.add))
        op("vector", lambda h: h.reciprocal(out=p.drz, in_=p.drz), inc=(CV, 1))
        wait("vector", CV, CV.v)
        for hd in range(16):
            op("vector", lambda h, hd=hd: h.tensor_scalar(
                out=p.dcomb[:, hd, :], in0=ctxv[:, hd, 0:64],
                scalar1=p.drz[:, hd:hd + 1], scalar2=None, op0=ALU.mult))
        mark("vector", CV)
        ev[("comb", s)] = CV.v
        dc3 = p.dcomb.rearrange("p hd (c b) -> p (hd c) b", b=64)
        dcfm_f = p.dcombfm.rearrange("p a b -> p (a b)")
        wait("tensor", CV, ev[("comb", s)])
        if s >= 1:
            wait("tensor", CS, ev[("combev", s - 1)])
        for c in range(8):
            op("tensor", lambda h, c=c: h.transpose(
                p.PSB[:, 4096 + 128 * c:4096 + 128 * (c + 1)],
                p.dcomb.rearrange("p hd e -> p (hd e)")[:, 128 * c:128 * (c + 1)],
                p.idb_sb))
        mark("tensor", CT)
        ev[("dct", s)] = CT.v
        wait("scalar", CT, ev[("dct", s)])
        op("scalar", lambda h: h.activation(out=dcfm_f, in_=p.PSB[:, 4096:5120],
                                            func=AF.Copy), inc=(CS, 1))
        ev[("combev", s)] = CS.v
        wait("tensor", CS, ev[("combev", s)])
        if s >= 1:
            wait("tensor", CS, ev[("s1evac", s - 1)])
        for c in range(8):
            op("tensor", lambda h, c=c: h.ldweights(p.dcombfm[:, c, :]))
            for j in range(2):
                op("tensor", lambda h, c=c, j=j: h.matmul(
                    p.PS[:, 1024 + 512 * j:1024 + 512 * (j + 1)],
                    p.dcombfm[:, c, :], p.wo1_sb[:, c, 512 * j:512 * (j + 1)],
                    start=(c == 0), stop=(c == 7)))
        mark("tensor", CT)
        ev[("s1mm", s)] = CT.v
        wait("scalar", CT, ev[("s1mm", s)])
        if s >= 2:
            wait("scalar", ST, ev[("st", s - 2)])
        op("scalar", lambda h: h.activation(out=p.ds1[r], in_=p.PS[:, 1024:2048],
                                            func=AF.Copy), inc=(CS, 1))
        ev[("s1evac", s)] = CS.v
        wait("gpsimd", CS, ev[("s1evac", s)])
        dma("gpsimd", ST, p.s1_s[SUB * s:SUB * (s + 1), :], p.ds1[r])
        ev[("st", s)] = ST.v
        mark("vector", CV)
        ev[("con", s)] = CV.v

    p.ev_d1 = ev


def _phase_D2(S):
    p = S.p
    op, dma, wait, mark = S.op, S.dma, S.wait, S.mark
    GW, CV, CS, CT = p.GW, p.CV, p.CS, p.CT
    n_sub = p.n_sub
    ev = {}
    _fence(S)
    _wquant_int8(S, p.wg1_h, p.wg_fm, I0, D)
    wq_ev = _wquant_int8(S, p.wu1_h, p.wu_fm, I0, D)
    wait("tensor", CV, wq_ev)
    wait("tensor", p.LDs[0], p.LDs[0].v)
    wait("tensor", p.LDs[1], p.LDs[1].v)
    wait("sync", CV, wq_ev)
    wait("sync", p.LDs[0], p.LDs[0].v)
    wait("sync", p.LDs[1], p.LDs[1].v)
    wait("vector", CV, p.ev_rsx)
    op("vector", lambda h: h.memset(p.mxg, 0.0), inc=(CV, 1))

    for s in range(n_sub):
        r = s % 2
        LD = p.LDs[r]
        ST = p.STs[r]
        if s >= 1:
            wait("sync", CV, ev[("xq1", s - 1)])
        dma("sync", LD, p.dxn1[r], p.xn1_s[SUB * s:SUB * (s + 1), :])
        ev[("ld", s)] = LD.v
        wait("vector", LD, ev[("ld", s)])
        if s >= 1:
            wait("vector", CT, ev[("up", s - 1)])
        op("vector", lambda h: h.tensor_scalar(out=p.qt.w[:, 0:D], in0=p.dxn1[r],
                                               scalar1=p.rsx, scalar2=None,
                                               op0=ALU.mult), inc=(CV, 1))
        cv = _emit_round_clip(S, p.qt.w[:, 0:D], p.bxq[:, 0:D], D)
        ev[("xq1", s)] = cv
        bxq3 = p.bxq[:, 0:D].rearrange("p (a b) -> p a b", b=128)
        bxqfm_f = p.bxqfm.rearrange("p a b -> p (a b)")
        wait("tensor", CV, cv)
        if s >= 1:
            wait("tensor", CS, ev[("xqev", s - 1)])
            wait("tensor", CV, ev[("h1", s - 1)])
        for c in range(8):
            op("tensor", lambda h, c=c: h.transpose(
                p.PSB[:, 7168 + 128 * c:7168 + 128 * (c + 1)], bxq3[:, c, :],
                p.idb_sb))
        mark("tensor", CT)
        ev[("xqt", s)] = CT.v
        wait("scalar", CT, ev[("xqt", s)])
        op("scalar", lambda h: h.activation(out=bxqfm_f, in_=p.PSB[:, 7168:8192],
                                            func=AF.Copy), inc=(CS, 1))
        ev[("xqev", s)] = CS.v
        wait("tensor", CS, ev[("xqev", s)])
        for c in range(8):
            op("tensor", lambda h, c=c: h.ldweights(p.bxqfm[:, c, :]))
            for (a, wdt) in _islices(I0, 512):
                op("tensor", lambda h, c=c, a=a, wdt=wdt: h.matmul(
                    p.PS[:, a:a + wdt], p.bxqfm[:, c, :], p.wg_fm[:, c, a:a + wdt],
                    start=(c == 0), stop=(c == 7)))
        mark("tensor", CT)
        ev[("gate", s)] = CT.v
        wait("scalar", CT, ev[("gate", s)])
        op("scalar", lambda h: h.activation(out=p.bgs, in_=p.PS[:, 0:I0],
                                            func=AF.Silu, scale=p.sx), inc=(CS, 1))
        ev[("gs", s)] = CS.v
        wait("tensor", CS, ev[("gs", s)])
        for c in range(8):
            op("tensor", lambda h, c=c: h.ldweights(p.bxqfm[:, c, :]))
            for (a, wdt) in _islices(I0, 512):
                op("tensor", lambda h, c=c, a=a, wdt=wdt: h.matmul(
                    p.PS[:, a:a + wdt], p.bxqfm[:, c, :], p.wu_fm[:, c, a:a + wdt],
                    start=(c == 0), stop=(c == 7)))
        mark("tensor", CT)
        ev[("up", s)] = CT.v
        wait("vector", CT, ev[("up", s)])
        if s >= 1:
            wait("vector", p.STs[1 - r], ev[("st", s - 1)])
        op("vector", lambda h: h.scalar_tensor_tensor(
            out=p.bh1[r], in0=p.PS[:, 0:I0], scalar=p.sx, in1=p.bgs,
            op0=ALU.mult, op1=ALU.mult), inc=(CV, 1))
        ev[("h1", s)] = CV.v
        wait("gpsimd", CV, ev[("h1", s)])
        dma("gpsimd", ST, p.h1_s[SUB * s:SUB * (s + 1), :], p.bh1[r])
        ev[("st", s)] = ST.v
        op("vector", lambda h: h.reduce_max(out=p.mx1, in_=p.bh1[r], axis=AX.X,
                                            apply_absolute_value=True))
        op("vector", lambda h: h.tensor_tensor(out=p.mxg, in0=p.mxg, in1=p.mx1,
                                               op=ALU.max), inc=(CV, 1))

    p.ev_d2 = ev


def _x3_maxh(S):
    p = S.p
    op, dma, wait, mark = S.op, S.dma, S.wait, S.mark
    GW, CV, CS, CT, CP = p.GW, p.CV, p.CS, p.CT, p.CP
    _fence(S)
    op("tensor", lambda h: h.transpose(p.PS[0:1, 0:128], p.mxg, p.identf_sb))
    mark("tensor", CT)
    wait("vector", CT, CT.v)
    op("vector", lambda h: h.reduce_max(out=p.mxs, in_=p.PS[0:1, 0:128], axis=AX.X),
       inc=(CV, 1))
    wait("tensor", CV, CV.v)
    op("tensor", lambda h: h.matmul(p.PS[:, 512:513], p.onesf_sb, p.mxs,
                                    start=True, stop=True))
    mark("tensor", CT)
    wait("vector", CT, CT.v)
    op("vector", lambda h: h.tensor_copy(out=p.mx1, in_=p.PS[:, 512:513]),
       inc=(CV, 1))
    wait("gpsimd", CV, CV.v)
    dma("gpsimd", p.STs[0], p.ccm2_in, p.mx1)
    wait("gpsimd", p.STs[0], p.STs[0].v)
    op("gpsimd", lambda h: h.collective_compute(
        "AllReduce", ALU.max, replica_groups=[[0, 1, 2, 3, 4, 5, 6, 7]],
        ins=[p.ccm2_in.opt()], outs=[p.ccm2_out.opt()]), inc=(CP, 1))
    wait("gpsimd", CP, CP.v)
    dma("gpsimd", GW, p.mx1, p.ccm2_out)
    wait("vector", GW, GW.v)
    op("vector", lambda h: h.tensor_tensor(out=p.rsh, in0=p.mx1, in1=p.sgd,
                                           op=ALU.mult))
    op("vector", lambda h: h.tensor_scalar(out=p.rsh, in0=p.rsh, scalar1=1.0 / 127.0,
                                           scalar2=None, op0=ALU.mult), inc=(CV, 1))
    wait("vector", CV, CV.v)
    op("vector", lambda h: h.tensor_copy(out=p.sh, in_=p.rsh), inc=(CV, 1))
    wait("vector", CV, CV.v)
    op("vector", lambda h: h.reciprocal(out=p.rsh, in_=p.rsh), inc=(CV, 1))
    p.ev_rsh = CV.v


def _phase_E(S):
    p = S.p
    op, dma, wait, mark = S.op, S.dma, S.wait, S.mark
    GW, CV, CS, CT = p.GW, p.CV, p.CS, p.CT
    n_sub = p.n_sub
    ev = {}
    _fence(S)
    wq_ev = _wquant_int8(S, p.wd1_h, p.wd_fm, D, I0)
    wait("tensor", CV, wq_ev)
    wait("tensor", p.LDs[0], p.LDs[0].v)
    wait("tensor", p.LDs[1], p.LDs[1].v)
    wait("sync", CV, wq_ev)
    wait("sync", p.LDs[0], p.LDs[0].v)
    wait("sync", p.LDs[1], p.LDs[1].v)
    wait("vector", CV, p.ev_rsh)

    for s in range(n_sub):
        r = s % 2
        LD = p.LDs[r]
        ST = p.STs[r]
        if s >= 1:
            wait("sync", CV, ev[("x2", s - 1)])
            wait("sync", CV, ev[("hq", s - 1)])
        dma("sync", LD, p.eh1[r], p.h1_s[SUB * s:SUB * (s + 1), :])
        dma("sync", LD, p.es1[r], p.s1_s[SUB * s:SUB * (s + 1), :])
        dma("sync", LD, p.ex1[r], p.x1_s[SUB * s:SUB * (s + 1), :])
        ev[("ld", s)] = LD.v
        wait("vector", LD, ev[("ld", s)])
        if s >= 1:
            wait("vector", CT, ev[("down", s - 1)])
        op("vector", lambda h: h.tensor_scalar(out=p.qt.hv[:, 0:I0], in0=p.eh1[r],
                                               scalar1=p.rsh, scalar2=None,
                                               op0=ALU.mult), inc=(CV, 1))
        for (a, wdt) in _islices(I0, 1408):
            cv = _emit_round_clip(S, p.qt.hv[:, a:a + wdt], p.bhq[:, a:a + wdt], wdt)
        ev[("hq", s)] = cv
        bhq3 = p.bhq[:, 0:I0].rearrange("p (a b) -> p a b", b=128)
        bhqfm_f = p.bhqfm.rearrange("p a b -> p (a b)")
        wait("tensor", CV, ev[("hq", s)])
        if s >= 1:
            wait("tensor", CS, ev[("hqev", s - 1)])
        for c in range(22):
            op("tensor", lambda h, c=c: h.transpose(
                p.PSB[:, 4096 + 128 * c:4096 + 128 * (c + 1)], bhq3[:, c, :],
                p.idb_sb))
        mark("tensor", CT)
        ev[("hqt", s)] = CT.v
        wait("scalar", CT, ev[("hqt", s)])
        op("scalar", lambda h: h.activation(out=bhqfm_f, in_=p.PSB[:, 4096:6912],
                                            func=AF.Copy), inc=(CS, 1))
        ev[("hqev", s)] = CS.v
        wait("tensor", CS, ev[("hqev", s)])
        if s >= 1:
            wait("tensor", CS, ev[("g1", s - 1)])
        for c in range(22):
            op("tensor", lambda h, c=c: h.ldweights(p.bhqfm[:, c, :]))
            for j in range(2):
                op("tensor", lambda h, c=c, j=j: h.matmul(
                    p.PS[:, 512 * j:512 * (j + 1)], p.bhqfm[:, c, :],
                    p.wd_fm[:, c, 512 * j:512 * (j + 1)],
                    start=(c == 0), stop=(c == 21)))
        mark("tensor", CT)
        ev[("down", s)] = CT.v
        wait("scalar", CT, ev[("down", s)])
        op("scalar", lambda h: h.activation(out=p.bg0, in_=p.PS[:, 0:1024],
                                            func=AF.Silu, scale=p.sh), inc=(CS, 1))
        ev[("g1", s)] = CS.v
        wait("vector", CS, ev[("g1", s)])
        if s >= 1:
            wait("vector", p.STs[1 - r], ev[("st", s - 1)])
        op("vector", lambda h: h.tensor_tensor(out=p.bg0, in0=p.es1[r], in1=p.bg0,
                                               op=ALU.mult))
        op("vector", lambda h: h.tensor_tensor(out=p.ex2[r], in0=p.bg0, in1=p.ex1[r],
                                               op=ALU.add), inc=(CV, 1))
        ev[("x2", s)] = CV.v
        wait("gpsimd", CV, ev[("x2", s)])
        dma("gpsimd", ST, p.y_h[SUB * s:SUB * (s + 1), :], p.ex2[r])
        ev[("st", s)] = ST.v

    p.ev_e = ev


def _mk_prog(nc, n_sub, stack):
    p = NS()
    p.n_sub = n_sub
    ntok = n_sub * SUB

    def din(name, shape, dt=F32):
        return nc.dram_tensor(name, list(shape), dt, kind="ExternalInput").ap()

    def scr(name, shape, dt=F32):
        return nc.dram_tensor(name, list(shape), dt, kind="Internal").ap()

    p.x_h = din("x", (ntok, D))
    p.ropec_h = din("ropec", (ntok, 128))
    p.ropes_h = din("ropes", (ntok, 128))
    p.ropecq_h = din("ropecq", (ntok, 128))
    p.ropesq_h = din("ropesq", (ntok, 128))
    p.kdect_h = din("kdect", (SUB, H0))
    p.crosst_h = din("crosst", (SUB, H0))
    p.dmaskt_h = din("dmaskt", (SUB, H0 * SUB))
    p.lnw_rep_h = din("lnwrep", (SUB, D))
    p.lnb_rep_h = din("lnbrep", (SUB, D))
    p.gnw_rep_h = din("gnwrep", (SUB, DV))
    p.sxw_h = din("sxw", (SUB, 3 * H0))
    p.sxm_h = din("sxm", (SUB, 4))
    p.actc_h = din("actc", (SUB, len(ACT_CONSTS)))
    p.onesf_h = din("onesf", (1, SUB))
    p.identf_h = din("identf", (SUB, SUB))
    p.clipx_h = din("clipx", (1, 2))
    p.wkv_h = din("wkv", (D, 3 * D))
    p.wqg_h = din("wqg", (D, 3 * D))
    p.wo0_h = din("wo0", (2 * D, D))
    p.wg0_h = din("wg0", (I0, D))
    p.wu0_h = din("wu0", (I0, D))
    p.wd0_h = din("wd0", (D, I0))
    p.wq1_h = din("wq1", (SUB, 16 * 64))
    p.wkv1_h = din("wkv1", (SUB, 8 * 256))
    p.wo1_h = din("wo1", (D, D))
    p.wg1_h = din("wg1", (I0, D))
    p.wu1_h = din("wu1", (I0, D))
    p.wd1_h = din("wd1", (D, I0))
    p.y_h = nc.dram_tensor("y", [ntok, D], F32, kind="ExternalOutput").ap()

    p.xn_s = scr("xn_s", (ntok, D))
    p.normed_f_s = scr("normed_f_s", (SUB, n_sub * D), BF16)
    p.ktm_f_s = scr("ktm_f_s", (SUB, n_sub * D), BF16)
    p.ktm_s = scr("ktm_s", (ntok, D), BF16)
    p.vtm_s = scr("vtm_s", (ntok, 2 * D), BF16)
    p.s0_s = scr("s0_s", (ntok, D))
    p.x1_s = scr("x1_s", (ntok, D))
    p.xn1_s = scr("xn1_s", (ntok, D))
    p.xn1f_s = scr("xn1f_s", (SUB, n_sub * D), BF16)
    p.hq0f_s = scr("hq0f_s", (SUB, n_sub * I0), BF16)
    p.s1_s = scr("s1_s", (ntok, D))
    p.h1_s = scr("h1_s", (ntok, I0))
    p.ccs_in = scr("ccs_in", (SUB, 3 * H0 * 2 * 512), BF16)
    p.ccs_out = scr("ccs_out", (SUB, 3 * H0 * 2 * 512), BF16)
    p.cckv_in = scr("cckv_in", (SUB, 2048))
    p.cckv_out = scr("cckv_out", (SUB, 2048))
    p.ccm1_in = scr("ccm1_in", (SUB, 1))
    p.ccm1_out = scr("ccm1_out", (SUB, 1))
    p.ccm2_in = scr("ccm2_in", (SUB, 1))
    p.ccm2_out = scr("ccm2_out", (SUB, 1))

    p.PS = stack.enter_context(nc.psum_tensor("PS", [128, 4096], F32))[:]
    p.PSB = p.PS.bitcast(BF16)

    def sb(name, shape, dt=F32):
        return stack.enter_context(nc.sbuf_tensor(name, list(shape), dt))[:]

    p.actc_sb = sb("actc_sb", (SUB, len(ACT_CONSTS)))
    for i, v in enumerate(ACT_CONSTS):
        nc.const_aps.register(F32, v, p.actc_sb[:, i:i + 1])
    p.kdect_sb = sb("kdect_sb", (SUB, H0))
    p.crosst_sb = sb("crosst_sb", (SUB, H0))
    p.sxw_sb = sb("sxw_sb", (SUB, 3 * H0))
    p.sxm_sb = sb("sxm_sb", (SUB, 4))
    p.onesf_sb = sb("onesf_sb", (1, SUB))
    p.identf_sb = sb("identf_sb", (SUB, SUB))
    p.idb_sb = sb("idb_sb", (SUB, SUB), BF16)
    p.clip_sb = sb("clip_sb", (1, 2))
    p.clip_sg = sb("clip_sg", (1, 2))
    p.trash = sb("trash", (SUB, D))
    p.trash2 = sb("trash2", (SUB, D))
    p.sums = sb("sums", (SUB, 4))
    p.mu = sb("mu", (SUB, 1))
    p.rstd = sb("rstd", (SUB, 1))
    p.mu2 = sb("mu2", (SUB, 1))
    p.mx1 = sb("mx1", (SUB, 1))
    p.mxg = sb("mxg", (SUB, 1))
    p.mxs = sb("mxs", (1, 1))
    p.rsx = sb("rsx", (SUB, 1))
    p.sx = sb("sx", (SUB, 1))
    p.rsh = sb("rsh", (SUB, 1))
    p.sh = sb("sh", (SUB, 1))
    p.sgd = sb("sgd", (SUB, 1))
    p.rm = sb("rm", (SUB, 1))
    p.rr1 = sb("rr1", (SUB, 1))
    p.tmp1 = sb("tmp1", (SUB, 128))
    p.tmp2 = sb("tmp2", (SUB, 128))
    p.tmp3 = sb("tmp3", (SUB, 512))
    p.drz = sb("drz", (SUB, 16))

    AR_COLS = 49872
    p.AR = stack.enter_context(nc.sbuf_tensor("AR", [128, AR_COLS], F32))[:]
    p._bump = [0]

    def take(cols, dt=F32, shape=None, at=None):
        a = p._bump[0] if at is None else at
        end = a + ((cols + 15) // 16) * 16
        assert end <= AR_COLS, f"arena overflow {end}"
        if at is None:
            p._bump[0] = end
        ap = p.AR[:, a:a + cols]
        if dt == BF16:
            ap = ap.bitcast(BF16)
        if shape is not None:
            names = " ".join(f"d{i}" for i in range(len(shape)))
            ap = ap.rearrange(f"p ({names}) -> p {names}",
                              **{f"d{i}": v for i, v in enumerate(shape)})
        return ap

    p.take = take
    return p


def _alloc_buffers(p):
    take = p.take
    w0 = p._bump[0]
    assert w0 == 0
    p.wkv_sb = take(12288, BF16, (8, 3072))
    p.wqg_sb = p.wkv_sb
    p.wo0_sb = take(8192, BF16, (16, 1024))
    wg_at = 0
    p.wg_fm = take(11264, BF16, (8, I0), at=0)
    p.wu_fm = take(11264, BF16, (8, I0), at=11264)
    p.wd_fm = take(11264, BF16, (22, 1024), at=0)
    p.wo1_sb = take(4096, BF16, (8, 1024), at=11264)
    p.wq1_sb = take(512, BF16, (16, 64), at=11264 + 4096)
    p.wkv1_sb = take(1024, BF16, (8, 256), at=11264 + 4608)
    p._bump[0] = max(p._bump[0], 22528)
    p.xch = take(6144, BF16, (3, H0, 2, 512), at=12288)

    p.a0x = [take(D) for _ in range(2)]
    p.a0xn = [take(D) for _ in range(2)]
    p.a0z = take(D)
    p.a0nb = [take(512, BF16) for _ in range(2)]
    p.normfm = [take(512, BF16, (8, 128)) for _ in range(2)]
    p.ropec_t = [take(128) for _ in range(2)]
    p.ropes_t = [take(128) for _ in range(2)]
    p.a0ktm = [take(512, BF16, (H0, 256)) for _ in range(2)]
    p.a0ktil = take(512, BF16, (H0, 256))
    p.a0vtm = [take(1024, BF16) for _ in range(2)]
    p.a1nfm = p.normfm
    p.a1kfm = [take(512, BF16, (8, 128)) for _ in range(2)]
    p.a1ktm = [take(512, BF16) for _ in range(2)]
    p.a1vtm = p.a0vtm
    p.a1xn = p.a0x
    p.a1qtm = take(512, BF16, (H0, 256))
    p.a1qctm = take(512, BF16, (H0, 256))
    p.a1qfm = take(512, BF16, (8, 128))
    p.a1qcfm = take(512, BF16, (8, 128))
    p.a1gs = take(2 * D)
    p.a1at = take(256, BF16, (H0, 128))
    p.a1og = take(1024, BF16, (H0, 512))
    p.a1ogfm = take(1024, BF16, (16, 128))
    p.a1s0 = [p.a0xn[0], p.a0xn[1]]
    p.lnw_sb = take(D)
    p.lnb_sb = take(D)
    p.gnw_sb = take(DV)
    p.dmask_sb = take(H0 * SUB)
    p.S_sb = take(H0 * 512, BF16, (H0, 2, 512))
    g1_end = p._bump[0]
    print(f"[kernel] arena group1 end {g1_end}")
    p._bump[0] = 22528

    q0 = p._bump[0]
    q = NS()
    q.aw = take(1408)
    q.y0 = take(1408)
    q.tt = take(1408)
    q.d = take(1408)
    q.sg = [take(704, BF16) for _ in range(2)]
    q.tmpb = take(704, BF16)
    q.acc1 = take(704, BF16)
    q.acc2 = take(704, BF16)
    q.sgn = take(704, BF16)
    q.bm = take(176)
    q.rr = take(176)
    q.w = take(1408)
    p.qt = q
    qend = p._bump[0]
    p.cxfm = take(512, BF16, (8, 128), at=q0)
    p.ckn = take(1024, at=q0 + 512)
    p.cke = take(1024, at=q0 + 1536)
    p.cphik = take(512, BF16, at=q0 + 2560)
    p.cvext = [take(576, BF16, (16, 72), at=q0 + 3072 + 576 * i) for i in range(2)]
    p.dknq = take(2048, at=q0 + 512)
    p.dkeq = take(2048, at=q0 + 2560)
    p.dphiq = take(1024, BF16, at=q0 + 4608)
    p.dcomb = take(512, BF16, (16, 64), at=q0 + 5632)
    p.dcombfm = take(512, BF16, (8, 128), at=q0 + 6144)
    p.ds1 = [take(1024, at=q0 + 6656 + 1024 * i) for i in range(2)]
    p.kvsb = take(2048, at=q0 + 7808)
    p.kvb = take(1024, BF16)
    p.kvb3 = p.kvb.rearrange("p (hd e) -> p hd e", e=128)
    assert q0 + 9856 <= qend, (q0, qend)

    bxn1 = take(D)
    p.bxn = [bxn1, bxn1]
    p.bx = p.bxn
    bs01 = take(D)
    p.bs0 = [bs01, bs01]
    p.bxq = take(512, BF16)
    bhqfm_cols = take(1408, BF16)
    p.bhqfm = bhqfm_cols.rearrange("p (a b) -> p a b", a=22)
    p.bxqfm = take(512, BF16, (8, 128))
    p.bgs = take(I0)
    p.bhq = take(1408, BF16)
    p.bg0 = take(D)
    bx11 = take(D)
    p.bx1 = [bx11, bx11]
    cx11 = bhqfm_cols.bitcast(F32)[:, 0:D]
    p.cx1 = [cx11, cx11]
    cxn11 = p.bgs[:, 0:D]
    p.cxn1 = [cxn11, cxn11]
    cxn1b1 = p.bhq[:, 0:D]
    p.cxn1b = [cxn1b1, cxn1b1]
    bh11 = take(I0)
    p.bh1 = [bh11, bh11]
    q.wslab = [p.bgs, p.bh1[0]]
    q.wqb = p.bhq
    q.hv = p.bgs
    p.dxn1 = p.cx1
    p.eh1 = p.bh1
    p.es1 = p.ds1
    p.ex1 = p.bs0
    p.ex2 = p.bx1
    print(f"[kernel] arena group2 end {p._bump[0]} / 49936 f32 cols")


_STOP = [None]


def build_nc(n_sub=NSUB, stop=None):
    _STOP[0] = stop
    import contextlib
    nc = bass.Bass("TRN2", target_bir_lowering=False)
    with contextlib.ExitStack() as stack:
        p = _mk_prog(nc, n_sub, stack)
        _alloc_buffers(p)
        with (
            nc.semaphore("sLD") as s_ld,
            nc.semaphore("sLD1") as s_ld1,
            nc.semaphore("sGW") as s_gw,
            nc.semaphore("sST") as s_st,
            nc.semaphore("sST1") as s_st1,
            nc.semaphore("sCV") as s_cv,
            nc.semaphore("sCS") as s_cs,
            nc.semaphore("sCT") as s_ct,
            nc.semaphore("sCP") as s_cp,
            nc.Block() as block,
        ):
            def fresh():
                p.LDs = [Ctr(s_ld, "LD0"), Ctr(s_ld1, "LD1")]
                p.GW = Ctr(s_gw, "GW")
                p.STs = [Ctr(s_st, "ST0"), Ctr(s_st1, "ST1")]
                p.CV = Ctr(s_cv, "CV")
                p.CS = Ctr(s_cs, "CS")
                p.CT = Ctr(s_ct, "CT")
                p.CP = Ctr(s_cp, "CP")

            @block.sync
            def _(h):
                fresh()
                _emit(p, "sync", h)

            @block.gpsimd
            def _(h):
                fresh()
                _emit(p, "gpsimd", h)

            @block.vector
            def _(h):
                fresh()
                _emit(p, "vector", h)

            @block.scalar
            def _(h):
                fresh()
                _emit(p, "scalar", h)

            @block.tensor
            def _(h):
                fresh()
                _emit(p, "tensor", h)

    return nc


def _prep_inputs(gi, n_sub=NSUB):
    ntok = n_sub * SUB
    tab = _host_tables(4 * ntok)
    x = np.ascontiguousarray(np.asarray(gi["L0_attn_x"], np.float32))
    wkv = np.ascontiguousarray(np.concatenate(
        [np.asarray(gi["L0_attn_Wk"], np.float32),
         np.asarray(gi["L0_attn_Wv"], np.float32)], axis=1))
    wqg = np.ascontiguousarray(np.concatenate(
        [np.asarray(gi["L0_attn_Wq"], np.float32),
         np.asarray(gi["L0_attn_Wg"], np.float32)], axis=1))
    rep = lambda v: np.ascontiguousarray(
        np.tile(np.asarray(v, np.float32)[None, :], (SUB, 1)))

    wqh = np.asarray(gi["L1_attn_Wq"], np.float32).reshape(8, 2, 64, 64)
    wq1b = np.zeros((8, 2, 2, 64, 64), np.float32)
    for hh in range(2):
        wq1b[:, hh, hh, :, :] = wqh[:, hh]
    wq1 = np.ascontiguousarray(
        wq1b.transpose(2, 3, 0, 1, 4).reshape(SUB, 16 * 64))
    wkh = np.asarray(gi["L1_attn_Wk"], np.float32).reshape(8, 2, 64, 64)
    wvh = np.asarray(gi["L1_attn_Wv"], np.float32).reshape(8, 2, 64, 64)
    wkv1b = np.zeros((8, 2, 64, 2, 2, 64), np.float32)
    for hh in range(2):
        wkv1b[:, hh, :, hh, 0, :] = wkh[:, hh]
        wkv1b[:, hh, :, hh, 1, :] = wvh[:, hh]
    wkv1 = np.ascontiguousarray(
        wkv1b.transpose(1, 2, 0, 3, 4, 5).reshape(SUB, 8 * 256))
    clip = np.array([[float(np.asarray(gi["L1_ffn_clip_x"]).reshape(-1)[0]),
                      float(np.asarray(gi["L1_ffn_clip_down"]).reshape(-1)[0])]],
                    np.float32)
    actc = np.tile(np.asarray(ACT_CONSTS, np.float32)[None, :], (SUB, 1))

    ins = []
    for c in range(8):
        b, pos = c // 4, c % 4
        t0 = pos * ntok
        sxw, sxm = _core_tables(pos, ntok)
        sl = slice(t0, t0 + ntok)
        ins.append({
            "x": np.ascontiguousarray(x[b, t0:t0 + ntok]),
            "ropec": np.ascontiguousarray(tab["ropec"][sl]),
            "ropes": np.ascontiguousarray(tab["ropes"][sl]),
            "ropecq": np.ascontiguousarray(tab["ropecq"][sl]),
            "ropesq": np.ascontiguousarray(tab["ropesq"][sl]),
            "kdect": tab["kdect"],
            "crosst": tab["crosst"],
            "dmaskt": np.ascontiguousarray(tab["dmaskt"].reshape(SUB, H0 * SUB)),
            "lnwrep": rep(gi["L0_attn_ln_w"]),
            "lnbrep": rep(gi["L0_attn_ln_b"]),
            "gnwrep": rep(gi["L0_attn_g_norm_w"]),
            "sxw": np.ascontiguousarray(np.tile(sxw.reshape(1, -1), (SUB, 1))),
            "sxm": np.ascontiguousarray(np.tile(sxm.reshape(1, 4), (SUB, 1))),
            "actc": actc,
            "onesf": np.ones((1, SUB), np.float32),
            "identf": np.eye(SUB, dtype=np.float32),
            "clipx": clip,
            "wkv": wkv, "wqg": wqg,
            "wo0": np.asarray(gi["L0_attn_Wo"], np.float32),
            "wg0": np.asarray(gi["L0_ffn_w_gate"], np.float32),
            "wu0": np.asarray(gi["L0_ffn_w_up"], np.float32),
            "wd0": np.asarray(gi["L0_ffn_w_down"], np.float32),
            "wq1": wq1, "wkv1": wkv1,
            "wo1": np.asarray(gi["L1_attn_Wo"], np.float32),
            "wg1": np.asarray(gi["L1_ffn_w_gate"], np.float32),
            "wu1": np.asarray(gi["L1_ffn_w_up"], np.float32),
            "wd1": np.asarray(gi["L1_ffn_w_down"], np.float32),
        })
    return ins


def kernel(**inputs):
    nc = build_nc(NSUB)
    ins = _prep_inputs(inputs, NSUB)
    res = run_bass_kernel_spmd(nc, ins, core_ids=list(range(8)))
    out = np.empty((B, T, D), np.float32)
    for c in range(8):
        b, pos = c // 4, c % 4
        out[b, pos * TPC:(pos + 1) * TPC] = res.results[c]["y"]
    return out


# version 05
# speedup vs baseline: 1.2725x; 1.0092x over previous
import numpy as np
import ml_dtypes
import concourse.bass as bass
import concourse.mybir as mybir
from concourse.bass_utils import run_bass_kernel_spmd

F32 = mybir.dt.float32
BF16 = mybir.dt.bfloat16
AF = mybir.ActivationFunctionType
ALU = mybir.AluOpType
AX = mybir.AxisListType
BF = ml_dtypes.bfloat16

B, T, D = 2, 16384, 1024
TPC = T // 4
SUB = 128
NSUB = TPC // SUB
H0, DK, DV = 4, 256, 512
H1, E1 = 16, 64
I0 = 2816
EPSM = 1e-5
EPS0 = 1e-5
EPS1 = 1e-6

GAMMA = [1.0 - 2.0 ** (-5.0 - h) for h in range(H0)]
GC = [g ** SUB for g in GAMMA]

FP4_TH = [0.5, 1.5, 2.5, 3.5, 5.0, 7.0, 10.0]
ACT_CONSTS = [-th for th in FP4_TH] + [-1.0, 1e-5, 1.0 / 512.0]


class Ctr:
    def __init__(self, sem, name):
        self.sem = sem
        self.name = name
        self.v = 0


class NS:
    pass


def _islices(total, step):
    out = []
    a = 0
    while a < total:
        out.append((a, min(step, total - a)))
        a += step
    return out


def _host_tables(t_total=T):
    t = {}
    invf = 10000.0 ** (-np.arange(0, DK, 2, dtype=np.float64) / DK)
    tt = np.arange(t_total, dtype=np.float64)[:, None]
    ang = tt * invf[None, :]
    t["ropec"] = np.cos(ang).astype(np.float32)
    t["ropes"] = np.sin(ang).astype(np.float32)
    s0 = DK ** (-0.5)
    t["ropecq"] = (np.cos(ang) * s0).astype(np.float32)
    t["ropesq"] = (np.sin(ang) * s0).astype(np.float32)
    i_ = np.arange(SUB, dtype=np.float64)[:, None]
    g_ = np.asarray(GAMMA, dtype=np.float64)[None, :]
    t["kdect"] = (g_ ** (SUB - 1.0 - i_)).astype(np.float32)
    t["crosst"] = (g_ ** (i_ + 1.0)).astype(np.float32)
    ii = np.arange(SUB, dtype=np.float64)
    dmk = np.zeros((SUB, H0, SUB), dtype=np.float64)
    for h in range(H0):
        m = np.where(ii[:, None] >= ii[None, :],
                     np.float64(GAMMA[h]) ** (ii[:, None] - ii[None, :]), 0.0)
        dmk[:, h, :] = m.T
    t["dmaskt"] = dmk.astype(np.float32)
    return t


def _core_tables(pos, tpc=TPC):
    w = np.zeros((3, H0), dtype=np.float64)
    for ps in (1, 2, 3):
        if ps > pos:
            for h in range(H0):
                w[ps - 1, h] = np.float64(GAMMA[h]) ** (tpc * (ps - 1 - pos))
    m = np.zeros((4,), dtype=np.float32)
    if pos >= 1:
        m[pos - 1] = 1.0
    return w.astype(np.float32), m


def _emit(p, active, h):
    S = NS()
    S.p = p
    S.active = active
    S.h = h

    S.last = None
    S.pend = {}
    S.pend_all = {e: set() for e in ("sync", "gpsimd", "vector", "scalar", "tensor")}

    def _flush():
        if S.last is not None:
            for ctr, amt in S.pend.items():
                S.last.then_inc(ctr.sem, amt)
        S.pend = {}

    _AUTOC = {"vector": "CV", "scalar": "CS"}

    def op(eng, fn, inc=None):
        ctr0 = getattr(p, _AUTOC[eng]) if eng in _AUTOC else None
        if ctr0 is not None and eng == active and ctr0.v > 0:
            _flush()
            S.h.wait_ge(ctr0.sem, ctr0.v)
        ins = None
        if eng == active:
            _flush()
            ins = fn(S.h)
            S.last = ins
        S.pend_all[eng] = set()
        if ctr0 is not None:
            ctr0.v += 1
            S.pend_all[eng].add(ctr0.name)
            if ins is not None:
                S.pend[ctr0] = S.pend.get(ctr0, 0) + 1
        if inc is not None:
            ctr, amt = inc
            ctr.v += amt
            S.pend_all[eng].add(ctr.name)
            if ins is not None:
                S.pend[ctr] = S.pend.get(ctr, 0) + amt
        return ins

    def mark(eng, ctr):
        if ctr.name in S.pend_all[eng]:
            return
        ctr.v += 1
        S.pend_all[eng].add(ctr.name)
        if eng == active:
            assert S.last is not None
            S.pend[ctr] = S.pend.get(ctr, 0) + 1

    S.flush = _flush

    S.drained = {"sync": set(), "gpsimd": set()}

    def dma(eng, ctr, out, in_, transpose=False):
        if eng in S.drained and ctr.name not in S.drained[eng]:
            S.drained[eng].add(ctr.name)
            if eng == active:
                S.h.wait_ge(ctr.sem, ctr.v)
        if eng == active:
            _flush()
            S.h.dma_start(out=out, in_=in_, transpose=transpose).then_inc(ctr.sem, 16)
        ctr.v += 16

    def wait(eng, ctr, val=None):
        if eng in S.drained:
            S.drained[eng] = set()
        v = ctr.v if val is None else val
        assert v <= ctr.v, (ctr.name, v, ctr.v)
        if eng == active:
            S.h.wait_ge(ctr.sem, v)

    S.op, S.dma, S.wait, S.mark = op, dma, wait, mark
    _program(S)
    _flush()


def _program(S):
    p = S.p
    dma, wait = S.dma, S.wait
    LD = p.LDs[0]

    dma("sync", LD, p.actc_sb, p.actc_h)
    dma("sync", LD, p.kdect_sb, p.kdect_h)
    dma("sync", LD, p.crosst_sb, p.crosst_h)
    dma("sync", LD, p.dmask_sb, p.dmaskt_h)
    dma("sync", LD, p.lnw_sb, p.lnw_rep_h)
    dma("sync", LD, p.lnb_sb, p.lnb_rep_h)
    dma("sync", LD, p.gnw_sb, p.gnw_rep_h)
    dma("sync", LD, p.sxw_sb, p.sxw_h)
    dma("sync", LD, p.sxm_sb, p.sxm_h)
    dma("sync", LD, p.onesf_sb, p.onesf_h)
    dma("sync", LD, p.identf_sb, p.identf_h)
    dma("sync", LD, p.clip_sb, p.clipx_h)
    ld_consts = LD.v
    dma("gpsimd", p.GW, p.idb_sb, p.identf_h)
    for e in ("vector", "scalar", "tensor", "gpsimd", "sync"):
        wait(e, LD, ld_consts)
    wait("tensor", p.GW, p.GW.v)

    phases = [("A0", _phase_A0), ("X1", _x1_state_exchange), ("A1", _phase_A1),
              ("B1", _phase_B1), ("B2", _phase_B2), ("C", _phase_C),
              ("X2", _x2_kv_max), ("D1", _phase_D1), ("D2", _phase_D2),
              ("X3", _x3_maxh), ("E", _phase_E)]
    for name, fn in phases:
        fn(S)
        st = _STOP[0] or ""
        if name == st or st.startswith(name + "."):
            break

    wait("gpsimd", p.STs[0], p.STs[0].v)
    wait("gpsimd", p.STs[1], p.STs[1].v)


def _emit_ln(S, x_ap, out_ap, affine_out=None):
    p = S.p
    op, wait, mark = S.op, S.wait, S.mark
    CV, CS = p.CV, p.CS
    wait("scalar", CV, CV.v)
    op("scalar", lambda h: h.activation(out=p.trash, in_=x_ap, func=AF.Copy,
                                        accum_out=p.sums[:, 0:1]))
    op("scalar", lambda h: h.activation(out=p.trash2, in_=x_ap, func=AF.Square,
                                        accum_out=p.sums[:, 1:2]), inc=(CS, 1))
    wait("vector", CS, CS.v)
    op("vector", lambda h: h.tensor_scalar(out=p.mu, in0=p.sums[:, 0:1],
                                           scalar1=1.0 / 1024.0, scalar2=None,
                                           op0=ALU.mult))
    op("vector", lambda h: h.tensor_scalar(out=p.rstd, in0=p.sums[:, 1:2],
                                           scalar1=1.0 / 1024.0, scalar2=None,
                                           op0=ALU.mult), inc=(CV, 1))
    wait("vector", CV, CV.v)
    op("vector", lambda h: h.scalar_tensor_tensor(out=p.rstd, in0=p.mu, scalar=p.mu,
                                                  in1=p.rstd, op0=ALU.mult,
                                                  op1=ALU.subtract), inc=(CV, 1))
    wait("scalar", CV, CV.v)
    op("scalar", lambda h: h.activation(out=p.rstd, in_=p.rstd, func=AF.Sqrt,
                                        scale=-1.0, bias=EPSM), inc=(CS, 1))
    wait("vector", CS, CS.v)
    op("vector", lambda h: h.reciprocal(out=p.rstd, in_=p.rstd), inc=(CV, 1))
    wait("vector", CV, CV.v)
    op("vector", lambda h: h.tensor_scalar(out=out_ap, in0=x_ap, scalar1=p.mu,
                                           scalar2=p.rstd, op0=ALU.subtract,
                                           op1=ALU.mult))
    if affine_out is not None:
        op("vector", lambda h: h.tensor_tensor(out=p.trash, in0=out_ap, in1=p.lnw_sb,
                                               op=ALU.mult))
        op("vector", lambda h: h.tensor_tensor(out=affine_out, in0=p.trash,
                                               in1=p.lnb_sb, op=ALU.add))
    mark("vector", CV)
    return CV.v


def _emit_fp4(S, x_ap, out_ap, F):
    p = S.p
    t = p.qt
    op, wait, mark = S.op, S.wait, S.mark
    CV, CS = p.CV, p.CS
    nb = F // 16
    x3 = x_ap.rearrange("p (b k) -> p b k", k=16)
    bm = t.bm[:, 0:nb]
    bm3 = bm.rearrange("p (b o) -> p b o", o=1)
    op("vector", lambda h: h.reduce_max(out=bm3, in_=x3, axis=AX.X,
                                        apply_absolute_value=True))
    op("vector", lambda h: h.reciprocal(out=t.rr[:, 0:nb], in_=bm))
    op("vector", lambda h: h.tensor_scalar(out=t.rr[:, 0:nb], in0=t.rr[:, 0:nb],
                                           scalar1=12.0, scalar2=2e12,
                                           op0=ALU.mult, op1=ALU.min), inc=(CV, 1))
    wait("vector", CV, CV.v)
    rrb = t.rr[:, 0:nb].unsqueeze(2).broadcast_to([128, nb, 16])
    w3 = t.w[:, 0:F].rearrange("p (b k) -> p b k", k=16)
    op("vector", lambda h: h.tensor_tensor(out=w3, in0=x3, in1=rrb, op=ALU.mult),
       inc=(CV, 1))
    wait("scalar", CV, CV.v)
    op("scalar", lambda h: h.activation(out=t.aw[:, 0:F], in_=t.w[:, 0:F],
                                        func=AF.Abs))
    for rnd in range(4):
        js = [2 * rnd, 2 * rnd + 1] if rnd < 3 else [6]
        for i, j in enumerate(js):
            last = (i == len(js) - 1) and rnd != 3
            op("scalar", lambda h, i=i, j=j: h.activation(
                out=t.sg[i][:, 0:F], in_=t.aw[:, 0:F], func=AF.Sign, bias=-FP4_TH[j]),
               inc=(CS, 1) if last else None)
        if rnd == 3:
            op("scalar", lambda h: h.activation(out=t.sgn[:, 0:F], in_=t.w[:, 0:F],
                                                func=AF.Sign), inc=(CS, 1))
        wait("vector", CS, CS.v)
        if rnd == 0:
            op("vector", lambda h: h.tensor_tensor(out=t.acc1[:, 0:F], in0=t.sg[0][:, 0:F],
                                                   in1=t.sg[1][:, 0:F], op=ALU.add))
        elif rnd == 1:
            op("vector", lambda h: h.tensor_tensor(out=t.tmpb[:, 0:F], in0=t.sg[0][:, 0:F],
                                                   in1=t.sg[1][:, 0:F], op=ALU.add))
            op("vector", lambda h: h.tensor_tensor(out=t.acc1[:, 0:F], in0=t.acc1[:, 0:F],
                                                   in1=t.tmpb[:, 0:F], op=ALU.add))
        elif rnd == 2:
            op("vector", lambda h: h.tensor_tensor(out=t.acc2[:, 0:F], in0=t.sg[0][:, 0:F],
                                                   in1=t.sg[1][:, 0:F], op=ALU.add))
        else:
            op("vector", lambda h: h.scalar_tensor_tensor(
                out=t.tmpb[:, 0:F], in0=t.acc1[:, 0:F], scalar=0.25,
                in1=t.sg[0][:, 0:F], op0=ALU.mult, op1=ALU.add))
            op("vector", lambda h: h.scalar_tensor_tensor(
                out=t.tmpb[:, 0:F], in0=t.acc2[:, 0:F], scalar=0.5,
                in1=t.tmpb[:, 0:F], op0=ALU.mult, op1=ALU.add))
        mark("vector", CV)
        if rnd < 3:
            wait("scalar", CV, CV.v)
    op("vector", lambda h: h.tensor_scalar(out=bm, in0=bm, scalar1=1.0 / 6.0,
                                           scalar2=1e-12, op0=ALU.mult, op1=ALU.max))
    scb = bm.unsqueeze(2).broadcast_to([128, nb, 16])
    sg3 = t.sgn[:, 0:F].rearrange("p (b k) -> p b k", k=16)
    ss3 = t.aw[:, 0:F].rearrange("p (b k) -> p b k", k=16)
    op("vector", lambda h: h.tensor_tensor(out=ss3, in0=sg3, in1=scb, op=ALU.mult))
    op("vector", lambda h: h.scalar_tensor_tensor(out=out_ap, in0=t.tmpb[:, 0:F],
                                                  scalar=3.0, in1=t.aw[:, 0:F],
                                                  op0=ALU.add, op1=ALU.mult),
       inc=(CV, 1))
    return CV.v


def _emit_round_clip(S, v_ap, out_ap, F):
    p = S.p
    t = p.qt
    op, wait, mark = S.op, S.wait, S.mark
    CV, CS = p.CV, p.CS
    op("vector", lambda h: h.tensor_scalar(out=v_ap, in0=v_ap, scalar1=-128.0,
                                           scalar2=127.0, op0=ALU.max, op1=ALU.min),
       inc=(CV, 1))
    wait("scalar", CV, CV.v)
    op("scalar", lambda h: h.activation(out=t.y0[:, 0:F], in_=v_ap, func=AF.Abs))
    op("scalar", lambda h: h.activation(out=t.sgn[:, 0:F], in_=v_ap, func=AF.Sign),
       inc=(CS, 1))
    wait("vector", CS, CS.v)
    op("vector", lambda h: h.tensor_copy(out=t.tt[:, 0:F], in_=t.y0[:, 0:F]),
       inc=(CV, 1))
    for m in (128.0, 64.0, 32.0, 16.0, 8.0, 4.0, 2.0, 1.0):
        op("vector", lambda h, m=m: h.tensor_scalar(out=t.d[:, 0:F], in0=t.tt[:, 0:F],
                                                    scalar1=m - 0.5, scalar2=-m,
                                                    op0=ALU.is_ge, op1=ALU.mult))
        op("vector", lambda h: h.tensor_tensor(out=t.tt[:, 0:F], in0=t.tt[:, 0:F],
                                               in1=t.d[:, 0:F], op=ALU.add))
    op("vector", lambda h: h.tensor_tensor(out=t.y0[:, 0:F], in0=t.y0[:, 0:F],
                                           in1=t.tt[:, 0:F], op=ALU.subtract))
    op("vector", lambda h: h.tensor_tensor(out=out_ap, in0=t.y0[:, 0:F],
                                           in1=t.sgn[:, 0:F], op=ALU.mult),
       inc=(CV, 1))
    return CV.v


def _phase_A0(S):
    p = S.p
    op, dma, wait, mark = S.op, S.dma, S.wait, S.mark
    GW, CV, CS, CT = p.GW, p.CV, p.CS, p.CT
    n_sub = p.n_sub
    ev = {}

    for c in range(8):
        dma("gpsimd", GW, p.wkv_sb[:, c, :], p.wkv_h[128 * c:128 * (c + 1), :])
    wait("tensor", GW, GW.v)
    op("vector", lambda h: h.memset(p.S_sb.rearrange("p h t d -> p (h t d)"), 0.0),
       inc=(CV, 1))

    for s in range(n_sub):
        r = s % 2
        LD = p.LDs[r]
        ST = p.STs[r]
        if s >= 2:
            wait("sync", CV, ev[("krope", s - 2)])
            wait("sync", CS, ev[("vevac", s - 2)])
        dma("sync", LD, p.a0x[r], p.x_h[SUB * s:SUB * (s + 1), :])
        dma("sync", LD, p.ropec_t[r], p.ropec_h[SUB * s:SUB * (s + 1), :])
        dma("sync", LD, p.ropes_t[r], p.ropes_h[SUB * s:SUB * (s + 1), :])
        ev[("ld", s)] = LD.v

        wait("vector", LD, ev[("ld", s)])
        wait("scalar", LD, ev[("ld", s)])
        if s >= 2:
            wait("vector", ST, ev[("st_xn", s - 2)])
        _emit_ln(S, p.a0x[r], p.a0xn[r])
        ev[("xn", s)] = CV.v
        wait("gpsimd", CV, ev[("xn", s)])
        dma("gpsimd", ST, p.xn_s[SUB * s:SUB * (s + 1), :], p.a0xn[r])
        ev[("st_xn", s)] = ST.v
        if s >= 2:
            wait("vector", CT, ev[("nt", s - 2)])
        _emit_ln(S, p.a0xn[r], p.a0z, affine_out=p.a0nb[r])
        ev[("nb", s)] = CV.v

        nb3 = p.a0nb[r].rearrange("p (a b) -> p a b", b=128)
        nfm_f = p.normfm[r].rearrange("p a b -> p (a b)")
        wait("tensor", CV, ev[("nb", s)])
        if s >= 1:
            wait("tensor", CV, ev[("krope", s - 1)])
            wait("tensor", CS, ev[("ktmev", s - 1)])
        for c in range(8):
            op("tensor", lambda h, c=c: h.transpose(
                p.PSB[:, 128 * c:128 * (c + 1)], nb3[:, c, :], p.idb_sb))
        mark("tensor", CT)
        ev[("nt", s)] = CT.v
        wait("scalar", CT, ev[("nt", s)])
        if s >= 2:
            wait("scalar", ST, ev[("st_nf", s - 2)])
        op("scalar", lambda h: h.activation(out=nfm_f, in_=p.PSB[:, 0:1024],
                                            func=AF.Copy), inc=(CS, 1))
        ev[("nev", s)] = CS.v
        wait("gpsimd", CS, ev[("nev", s)])
        dma("gpsimd", ST, p.normed_f_s[:, D * s:D * (s + 1)], nfm_f)
        ev[("st_nf", s)] = ST.v

        wait("tensor", CS, ev[("nev", s)])
        if s >= 1:
            wait("tensor", CS, ev[("vevac", s - 1)])
        for c in range(8):
            op("tensor", lambda h, c=c: h.ldweights(p.normfm[r][:, c, :]))
            for j in range(6):
                op("tensor", lambda h, c=c, j=j: h.matmul(
                    p.PS[:, 512 * j:512 * (j + 1)], p.normfm[r][:, c, :],
                    p.wkv_sb[:, c, 512 * j:512 * (j + 1)],
                    start=(c == 0), stop=(c == 7)))
        mark("tensor", CT)
        ev[("proj", s)] = CT.v

        wait("vector", CT, ev[("proj", s)])
        if s >= 2:
            wait("vector", ST, ev[("st_k", s - 2)])
        kps = p.PS[:, 0:1024].rearrange("p (h d) -> p h d", h=H0)
        for h0 in range(H0):
            k1, k2 = kps[:, h0, 0:128], kps[:, h0, 128:256]
            o1 = p.a0ktm[r][:, h0, 0:128]
            o2 = p.a0ktm[r][:, h0, 128:256]
            op("vector", lambda h, k1=k1: h.tensor_tensor(out=p.tmp1, in0=k1, in1=p.ropec_t[r], op=ALU.mult))
            op("vector", lambda h, k2=k2: h.tensor_tensor(out=p.tmp2, in0=k2, in1=p.ropes_t[r], op=ALU.mult))
            op("vector", lambda h, o1=o1: h.tensor_tensor(out=o1, in0=p.tmp1, in1=p.tmp2, op=ALU.subtract))
            op("vector", lambda h, k2=k2: h.tensor_tensor(out=p.tmp1, in0=k2, in1=p.ropec_t[r], op=ALU.mult))
            op("vector", lambda h, k1=k1: h.tensor_tensor(out=p.tmp2, in0=k1, in1=p.ropes_t[r], op=ALU.mult))
            op("vector", lambda h, o2=o2: h.tensor_tensor(out=o2, in0=p.tmp1, in1=p.tmp2, op=ALU.add))
        mark("vector", CV)
        ev[("krope", s)] = CV.v
        for h0 in range(H0):
            op("vector", lambda h, h0=h0: h.tensor_scalar(
                out=p.a0ktil[:, h0, :], in0=p.a0ktm[r][:, h0, :],
                scalar1=p.kdect_sb[:, h0:h0 + 1], scalar2=None, op0=ALU.mult))
        mark("vector", CV)
        ev[("ktil", s)] = CV.v
        wait("gpsimd", CV, ev[("krope", s)])
        dma("gpsimd", ST, p.ktm_s[SUB * s:SUB * (s + 1), :],
            p.a0ktm[r].rearrange("p h d -> p (h d)"))
        ev[("st_k", s)] = ST.v

        kt3 = p.a0ktm[r].rearrange("p h (c b) -> p (h c) b", b=128)
        kfm_f = p.a1kfm[r].rearrange("p a b -> p (a b)")
        wait("tensor", CV, ev[("krope", s)])
        wait("tensor", CS, ev[("nev", s)])
        for c in range(8):
            op("tensor", lambda h, c=c: h.transpose(
                p.PSB[:, 128 * c:128 * (c + 1)], kt3[:, c, :], p.idb_sb))
        mark("tensor", CT)
        ev[("ktt", s)] = CT.v
        wait("scalar", CT, ev[("ktt", s)])
        if s >= 2:
            wait("scalar", ST, ev[("st_kf", s - 2)])
        op("scalar", lambda h: h.activation(out=kfm_f, in_=p.PSB[:, 0:1024],
                                            func=AF.Copy), inc=(CS, 1))
        ev[("ktmev", s)] = CS.v
        wait("gpsimd", CS, ev[("ktmev", s)])
        dma("gpsimd", ST, p.ktm_f_s[:, D * s:D * (s + 1)], kfm_f)
        ev[("st_kf", s)] = ST.v

        wait("scalar", CT, ev[("proj", s)])
        if s >= 2:
            wait("scalar", ST, ev[("st_v", s - 2)])
        op("scalar", lambda h: h.activation(out=p.a0vtm[r], in_=p.PS[:, 1024:3072],
                                            func=AF.Copy), inc=(CS, 1))
        ev[("vevac", s)] = CS.v
        wait("gpsimd", CS, ev[("vevac", s)])
        dma("gpsimd", ST, p.vtm_s[SUB * s:SUB * (s + 1), :], p.a0vtm[r])
        ev[("st_v", s)] = ST.v

        wait("tensor", CV, ev[("ktil", s)])
        wait("tensor", CS, ev[("vevac", s)])
        for h0 in range(H0):
            if s + h0 > 0:
                prev = ev[("sst", s, h0 - 1)] if h0 > 0 else ev[("sst", s - 1, H0 - 1)]
                wait("tensor", CV, prev)
            for half in range(2):
                op("tensor", lambda h, h0=h0, half=half: h.ldweights(
                    p.a0ktil[:, h0, 128 * half:128 * (half + 1)]))
                op("tensor", lambda h, h0=h0, half=half: h.matmul(
                    p.PS[:, 3072 + 512 * half:3072 + 512 * (half + 1)],
                    p.a0ktil[:, h0, 128 * half:128 * (half + 1)],
                    p.a0vtm[r][:, 512 * h0:512 * (h0 + 1)],
                    start=True, stop=True))
            mark("tensor", CT)
            ev[("Y", s, h0)] = CT.v
            wait("vector", CT, ev[("Y", s, h0)])
            for half in range(2):
                op("vector", lambda h, h0=h0, half=half: h.scalar_tensor_tensor(
                    out=p.S_sb[:, h0, half, :], in0=p.S_sb[:, h0, half, :],
                    scalar=GC[h0], in1=p.PS[:, 3072 + 512 * half:3072 + 512 * (half + 1)],
                    op0=ALU.mult, op1=ALU.add))
            mark("vector", CV)
            ev[("sst", s, h0)] = CV.v

    p.ev_a0 = ev


def _x1_state_exchange(S):
    p = S.p
    op, dma, wait, mark = S.op, S.dma, S.wait, S.mark
    GW, CV, CS, CT, CP = p.GW, p.CV, p.CS, p.CT, p.CP
    xchf = p.xch.rearrange("p a h t d -> p (a h t d)")
    sflat = p.S_sb.rearrange("p h t d -> p (h t d)")
    for ps in range(3):
        for h0 in range(H0):
            op("vector", lambda h, ps=ps, h0=h0: h.tensor_scalar(
                out=p.xch[:, ps, h0, :, :], in0=p.S_sb[:, h0, :, :],
                scalar1=p.sxw_sb[:, ps * H0 + h0:ps * H0 + h0 + 1],
                scalar2=None, op0=ALU.mult))
    mark("vector", CV)
    wait("gpsimd", CV, CV.v)
    dma("gpsimd", p.STs[0], p.ccs_in, xchf)
    wait("gpsimd", p.STs[0], p.STs[0].v)
    wait("gpsimd", p.STs[1], p.STs[1].v)
    op("gpsimd", lambda h: h.collective_compute(
        "AllReduce", ALU.add, replica_groups=[[0, 1, 2, 3], [4, 5, 6, 7]],
        ins=[p.ccs_in.opt()], outs=[p.ccs_out.opt()]), inc=(CP, 1))
    wait("gpsimd", CP, CP.v)
    dma("gpsimd", GW, xchf, p.ccs_out)
    wait("vector", GW, GW.v)
    op("vector", lambda h: h.tensor_scalar(
        out=sflat, in0=p.xch[:, 0, :, :, :].rearrange("p h t d -> p (h t d)"),
        scalar1=p.sxm_sb[:, 0:1], scalar2=None, op0=ALU.mult))
    for ps in (1, 2):
        op("vector", lambda h, ps=ps: h.scalar_tensor_tensor(
            out=sflat, in0=p.xch[:, ps, :, :, :].rearrange("p h t d -> p (h t d)"),
            scalar=p.sxm_sb[:, ps:ps + 1], in1=sflat, op0=ALU.mult, op1=ALU.add))
    mark("vector", CV)
    p.ev_sinit = CV.v


def _phase_A1(S):
    p = S.p
    op, dma, wait, mark = S.op, S.dma, S.wait, S.mark
    GW, CV, CS, CT = p.GW, p.CV, p.CS, p.CT
    n_sub = p.n_sub
    ev = {}

    wait("gpsimd", CT, p.ev_a0[("proj", n_sub - 1)])
    wait("gpsimd", CV, p.ev_sinit)
    for c in range(8):
        dma("gpsimd", GW, p.wqg_sb[:, c, :], p.wqg_h[128 * c:128 * (c + 1), :])
    for c in range(16):
        dma("gpsimd", GW, p.wo0_sb[:, c, :], p.wo0_h[128 * c:128 * (c + 1), :])
    wait("tensor", GW, GW.v)

    for c in (p.LDs[0], p.LDs[1], p.GW, p.STs[0], p.STs[1], p.CV, p.CS, p.CT):
        wait("sync", c, c.v)

    for s in range(n_sub):
        r = s % 2
        LD = p.LDs[r]
        ST = p.STs[r]
        if s >= 2:
            wait("sync", CV, ev[("con", s - 2)])
        dma("sync", LD, p.a1nfm[r].rearrange("p a b -> p (a b)"),
            p.normed_f_s[:, D * s:D * (s + 1)])
        dma("sync", LD, p.a1kfm[r].rearrange("p a b -> p (a b)"),
            p.ktm_f_s[:, D * s:D * (s + 1)])
        dma("sync", LD, p.a1ktm[r], p.ktm_s[SUB * s:SUB * (s + 1), :])
        dma("sync", LD, p.a1vtm[r], p.vtm_s[SUB * s:SUB * (s + 1), :])
        dma("sync", LD, p.a1xn[r], p.xn_s[SUB * s:SUB * (s + 1), :])
        dma("sync", LD, p.ropec_t[r], p.ropecq_h[SUB * s:SUB * (s + 1), :])
        dma("sync", LD, p.ropes_t[r], p.ropesq_h[SUB * s:SUB * (s + 1), :])
        ev[("ld", s)] = LD.v

        wait("tensor", LD, ev[("ld", s)])
        if s >= 1:
            wait("tensor", CV, ev[("con", s - 1)])
            wait("tensor", CS, ev[("gsilu", s - 1)])
        if s == 0:
            wait("tensor", CV, p.ev_sinit)
            wait("tensor", CS, p.ev_a0[("vevac", n_sub - 1)])
        for c in range(8):
            op("tensor", lambda h, c=c: h.ldweights(p.a1nfm[r][:, c, :]))
            for j in range(6):
                op("tensor", lambda h, c=c, j=j: h.matmul(
                    p.PS[:, 512 * j:512 * (j + 1)], p.a1nfm[r][:, c, :],
                    p.wqg_sb[:, c, 512 * j:512 * (j + 1)],
                    start=(c == 0), stop=(c == 7)))
        mark("tensor", CT)
        ev[("qg", s)] = CT.v

        wait("vector", CT, ev[("qg", s)])
        wait("vector", LD, ev[("ld", s)])
        qps = p.PS[:, 0:1024].rearrange("p (h d) -> p h d", h=H0)
        for h0 in range(H0):
            q1, q2 = qps[:, h0, 0:128], qps[:, h0, 128:256]
            o1 = p.a1qtm[:, h0, 0:128]
            o2 = p.a1qtm[:, h0, 128:256]
            op("vector", lambda h, q1=q1: h.tensor_tensor(out=p.tmp1, in0=q1, in1=p.ropec_t[r], op=ALU.mult))
            op("vector", lambda h, q2=q2: h.tensor_tensor(out=p.tmp2, in0=q2, in1=p.ropes_t[r], op=ALU.mult))
            op("vector", lambda h, o1=o1: h.tensor_tensor(out=o1, in0=p.tmp1, in1=p.tmp2, op=ALU.subtract))
            op("vector", lambda h, q2=q2: h.tensor_tensor(out=p.tmp1, in0=q2, in1=p.ropec_t[r], op=ALU.mult))
            op("vector", lambda h, q1=q1: h.tensor_tensor(out=p.tmp2, in0=q1, in1=p.ropes_t[r], op=ALU.mult))
            op("vector", lambda h, o2=o2: h.tensor_tensor(out=o2, in0=p.tmp1, in1=p.tmp2, op=ALU.add))
        for h0 in range(H0):
            op("vector", lambda h, h0=h0: h.tensor_scalar(
                out=p.a1qctm[:, h0, :], in0=p.a1qtm[:, h0, :],
                scalar1=p.crosst_sb[:, h0:h0 + 1], scalar2=None, op0=ALU.mult))
        mark("vector", CV)
        ev[("qrope", s)] = CV.v

        wait("scalar", CT, ev[("qg", s)])
        op("scalar", lambda h: h.activation(out=p.a1gs, in_=p.PS[:, 1024:3072],
                                            func=AF.Silu), inc=(CS, 1))
        ev[("gsilu", s)] = CS.v

        qt3 = p.a1qtm.rearrange("p h (c b) -> p (h c) b", b=128)
        qct3 = p.a1qctm.rearrange("p h (c b) -> p (h c) b", b=128)
        qfm_f = p.a1qfm.rearrange("p a b -> p (a b)")
        qcfm_f = p.a1qcfm.rearrange("p a b -> p (a b)")
        wait("tensor", CV, ev[("qrope", s)])
        if s >= 1:
            wait("tensor", CV, ev[("og", s - 1, H0 - 1)])
            wait("tensor", CS, ev[("qev", s - 1)])
        for c in range(8):
            op("tensor", lambda h, c=c: h.transpose(
                p.PSB[:, 6144 + 128 * c:6144 + 128 * (c + 1)], qt3[:, c, :],
                p.idb_sb))
        for c in range(8):
            op("tensor", lambda h, c=c: h.transpose(
                p.PSB[:, 7168 + 128 * c:7168 + 128 * (c + 1)], qct3[:, c, :],
                p.idb_sb))
        mark("tensor", CT)
        ev[("qt", s)] = CT.v
        wait("scalar", CT, ev[("qt", s)])
        op("scalar", lambda h: h.activation(out=qfm_f, in_=p.PSB[:, 6144:7168],
                                            func=AF.Copy))
        op("scalar", lambda h: h.activation(out=qcfm_f, in_=p.PSB[:, 7168:8192],
                                            func=AF.Copy), inc=(CS, 1))
        ev[("qev", s)] = CS.v

        wait("vector", LD, ev[("ld", s)])
        ktmv = p.a1ktm[r].rearrange("p (hh d) -> p hh d", hh=H0)
        for h0 in range(H0):
            op("vector", lambda h, h0=h0: h.tensor_scalar(
                out=p.a0ktil[:, h0, :], in0=ktmv[:, h0, :],
                scalar1=p.kdect_sb[:, h0:h0 + 1], scalar2=None, op0=ALU.mult))
        mark("vector", CV)
        ev[("ktil", s)] = CV.v

        wait("tensor", CS, ev[("qev", s)])
        for h0 in range(H0):
            for half in range(2):
                c = 2 * h0 + half
                op("tensor", lambda h, c=c: h.ldweights(p.a1kfm[r][:, c, :]))
                op("tensor", lambda h, c=c, h0=h0, half=half: h.matmul(
                    p.PS[:, 3072 + 128 * h0:3072 + 128 * (h0 + 1)],
                    p.a1kfm[r][:, c, :], p.a1qfm[:, c, :],
                    start=(half == 0), stop=(half == 1)))
            mark("tensor", CT)
            ev[("attnT", s, h0)] = CT.v
            wait("vector", CT, ev[("attnT", s, h0)])
            op("vector", lambda h, h0=h0: h.tensor_tensor(
                out=p.a1at[:, h0, :], in0=p.PS[:, 3072 + 128 * h0:3072 + 128 * (h0 + 1)],
                in1=p.dmask_sb[:, 128 * h0:128 * (h0 + 1)], op=ALU.mult), inc=(CV, 1))
            ev[("atm", s, h0)] = CV.v
            wait("tensor", CV, ev[("atm", s, h0)])
            if h0 > 0:
                wait("tensor", CV, ev[("og", s, h0 - 1)])
            op("tensor", lambda h, h0=h0: h.ldweights(p.a1at[:, h0, :]))
            op("tensor", lambda h, h0=h0: h.matmul(
                p.PS[:, 3584:4096], p.a1at[:, h0, :],
                p.a1vtm[r][:, 512 * h0:512 * (h0 + 1)], start=True, stop=False))
            for half in range(2):
                c = 2 * h0 + half
                op("tensor", lambda h, c=c: h.ldweights(p.a1qcfm[:, c, :]))
                op("tensor", lambda h, c=c, h0=h0, half=half: h.matmul(
                    p.PS[:, 3584:4096], p.a1qcfm[:, c, :],
                    p.S_sb[:, h0, half, :], start=False, stop=(half == 1)))
            mark("tensor", CT)
            ev[("o", s, h0)] = CT.v
            wait("scalar", CT, ev[("o", s, h0)])
            op("scalar", lambda h: h.activation(out=p.trash[:, 0:512], in_=p.PS[:, 3584:4096],
                                                func=AF.Square, accum_out=p.sums[:, 2:3]),
               inc=(CS, 1))
            wait("scalar", CS, CS.v)
            op("scalar", lambda h: h.activation(out=p.mu2, in_=p.sums[:, 2:3],
                                                func=AF.Sqrt, scale=1.0 / 512.0,
                                                bias=EPS0), inc=(CS, 1))
            ev[("osd", s, h0)] = CS.v
            wait("vector", CS, ev[("osd", s, h0)])
            wait("vector", CS, ev[("gsilu", s)])
            op("vector", lambda h: h.reciprocal(out=p.mu2, in_=p.mu2), inc=(CV, 1))
            wait("vector", CV, CV.v)
            op("vector", lambda h: h.scalar_tensor_tensor(
                out=p.tmp3, in0=p.PS[:, 3584:4096], scalar=p.mu2,
                in1=p.gnw_sb, op0=ALU.mult, op1=ALU.mult))
            op("vector", lambda h, h0=h0: h.tensor_tensor(
                out=p.a1og[:, h0, :], in0=p.tmp3,
                in1=p.a1gs[:, 512 * h0:512 * (h0 + 1)], op=ALU.mult), inc=(CV, 1))
            ev[("og", s, h0)] = CV.v

        wait("tensor", CV, ev[("ktil", s)])
        for h0 in range(H0):
            wait("tensor", CV, ev[("og", s, h0)])
            if h0 > 0:
                wait("tensor", CV, ev[("sst", s, h0 - 1)])
            elif s >= 1:
                wait("tensor", CV, ev[("sst", s - 1, H0 - 1)])
            for half in range(2):
                op("tensor", lambda h, h0=h0, half=half: h.ldweights(
                    p.a0ktil[:, h0, 128 * half:128 * (half + 1)]))
                op("tensor", lambda h, h0=h0, half=half: h.matmul(
                    p.PS[:, 512 * half:512 * (half + 1)],
                    p.a0ktil[:, h0, 128 * half:128 * (half + 1)],
                    p.a1vtm[r][:, 512 * h0:512 * (h0 + 1)], start=True, stop=True))
            mark("tensor", CT)
            ev[("Y", s, h0)] = CT.v
            wait("vector", CT, ev[("Y", s, h0)])
            for half in range(2):
                op("vector", lambda h, h0=h0, half=half: h.scalar_tensor_tensor(
                    out=p.S_sb[:, h0, half, :], in0=p.S_sb[:, h0, half, :],
                    scalar=GC[h0], in1=p.PS[:, 512 * half:512 * (half + 1)],
                    op0=ALU.mult, op1=ALU.add))
            mark("vector", CV)
            ev[("sst", s, h0)] = CV.v

        og3 = p.a1og.rearrange("p h (c b) -> p (h c) b", b=128)
        ogfm_f = p.a1ogfm.rearrange("p a b -> p (a b)")
        wait("tensor", CV, ev[("og", s, H0 - 1)])
        wait("tensor", CV, ev[("sst", s, H0 - 1)])
        if s >= 1:
            wait("tensor", CS, ev[("ogev", s - 1)])
        for c in range(16):
            op("tensor", lambda h, c=c: h.transpose(
                p.PSB[:, 128 * c:128 * (c + 1)], og3[:, c, :], p.idb_sb))
        mark("tensor", CT)
        ev[("ogt", s)] = CT.v
        wait("scalar", CT, ev[("ogt", s)])
        op("scalar", lambda h: h.activation(out=ogfm_f, in_=p.PSB[:, 0:2048],
                                            func=AF.Copy), inc=(CS, 1))
        ev[("ogev", s)] = CS.v
        wait("tensor", CS, ev[("ogev", s)])
        if s >= 1:
            wait("tensor", CV, ev[("s0", s - 1)])
        for c in range(16):
            op("tensor", lambda h, c=c: h.ldweights(p.a1ogfm[:, c, :]))
            for j in range(2):
                op("tensor", lambda h, c=c, j=j: h.matmul(
                    p.PS[:, 1024 + 512 * j:1024 + 512 * (j + 1)],
                    p.a1ogfm[:, c, :], p.wo0_sb[:, c, 512 * j:512 * (j + 1)],
                    start=(c == 0), stop=(c == 15)))
        mark("tensor", CT)
        ev[("wo", s)] = CT.v
        wait("vector", CT, ev[("wo", s)])
        if s >= 2:
            wait("vector", ST, ev[("st_s0", s - 2)])
        op("vector", lambda h: h.tensor_tensor(out=p.a1s0[r], in0=p.PS[:, 1024:2048],
                                               in1=p.a1xn[r], op=ALU.add), inc=(CV, 1))
        ev[("s0", s)] = CV.v
        wait("gpsimd", CV, ev[("s0", s)])
        dma("gpsimd", ST, p.s0_s[SUB * s:SUB * (s + 1), :], p.a1s0[r])
        ev[("st_s0", s)] = ST.v
        mark("vector", CV)
        ev[("con", s)] = CV.v

    p.ev_a1 = ev


def _fence(S):
    p = S.p
    for e in ("sync", "gpsimd", "vector", "scalar", "tensor"):
        for c in (p.LDs[0], p.LDs[1], p.GW, p.STs[0], p.STs[1],
                  p.CV, p.CS, p.CT, p.CP):
            S.wait(e, c, c.v)


def _wquant_fp4(S, w_h, wfm_sb, nrow, ncol):
    p = S.p
    t = p.qt
    op, dma, wait, mark = S.op, S.dma, S.wait, S.mark
    CV, CT = p.CV, p.CT
    guard = [0, 0]
    nblk = ncol // 128
    wqb3 = t.wqb[:, 0:ncol].rearrange("p (a b) -> p a b", b=128)
    psv = p.PSB[:, 0:ncol].rearrange("p (a b) -> p a b", b=128)
    evv = 0
    wait("sync", CV, CV.v)
    for i in range(nrow // 128):
        r = i % 2
        LD = p.LDs[r]
        if i >= 2:
            wait("sync", CV, guard[r])
        dma("sync", LD, t.wslab[r][:, 0:ncol], w_h[128 * i:128 * (i + 1), :])
        ldv = LD.v
        wait("vector", LD, ldv)
        if i >= 1:
            wait("vector", CT, trv)
        cvs = []
        for (a, wdt) in _islices(ncol, 1408):
            cvs.append(_emit_fp4(S, t.wslab[r][:, a:a + wdt], t.wqb[:, a:a + wdt], wdt))
        wait("tensor", CV, cvs[-1])
        if i >= 1:
            wait("tensor", CV, evv)
        for c in range(nblk):
            op("tensor", lambda h, c=c: h.transpose(
                p.PSB[:, 128 * c:128 * (c + 1)], wqb3[:, c, :], p.idb_sb))
        mark("tensor", CT)
        trv = CT.v
        wait("vector", CT, trv)
        op("vector", lambda h, i=i: h.tensor_copy(
            out=wfm_sb[:, :, 128 * i:128 * (i + 1)], in_=psv), inc=(CV, 1))
        evv = CV.v
        guard[r] = CV.v
    return evv


def _wquant_int8(S, w_h, wfm_sb, nrow, ncol):
    p = S.p
    t = p.qt
    op, dma, wait, mark = S.op, S.dma, S.wait, S.mark
    CV, CT = p.CV, p.CT
    guard = [0, 0]
    nblk = ncol // 128
    wqb3 = t.wqb[:, 0:ncol].rearrange("p (a b) -> p a b", b=128)
    psv = p.PSB[:, 0:ncol].rearrange("p (a b) -> p a b", b=128)
    evv = 0
    trv = 0
    wait("sync", CV, CV.v)
    for i in range(nrow // 128):
        r = i % 2
        LD = p.LDs[r]
        if i >= 2:
            wait("sync", CV, guard[r])
        dma("sync", LD, t.wslab[r][:, 0:ncol], w_h[128 * i:128 * (i + 1), :])
        ldv = LD.v
        wait("vector", LD, ldv)
        if i >= 1:
            wait("vector", CT, trv)
        op("vector", lambda h: h.reduce_max(out=p.rm, in_=t.wslab[r][:, 0:ncol],
                                            axis=AX.X, apply_absolute_value=True))
        op("vector", lambda h: h.reciprocal(out=p.rr1, in_=p.rm))
        op("vector", lambda h: h.tensor_scalar(out=p.rr1, in0=p.rr1, scalar1=127.0,
                                               scalar2=None, op0=ALU.mult))
        op("vector", lambda h: h.tensor_scalar(out=p.rm, in0=p.rm, scalar1=1.0 / 127.0,
                                               scalar2=None, op0=ALU.mult), inc=(CV, 1))
        wait("vector", CV, CV.v)
        for (a, wdt) in _islices(ncol, 1408):
            op("vector", lambda h, a=a, wdt=wdt: h.tensor_scalar(
                out=t.w[:, 0:wdt], in0=t.wslab[r][:, a:a + wdt],
                scalar1=p.rr1, scalar2=None, op0=ALU.mult), inc=(CV, 1))
            _emit_round_clip(S, t.w[:, 0:wdt], t.y0[:, 0:wdt], wdt)
            op("vector", lambda h, a=a, wdt=wdt: h.tensor_scalar(
                out=t.wqb[:, a:a + wdt], in0=t.y0[:, 0:wdt],
                scalar1=p.rm, scalar2=None, op0=ALU.mult), inc=(CV, 1))
        cv = CV.v
        wait("tensor", CV, cv)
        if i >= 1:
            wait("tensor", CV, evv)
        for c in range(nblk):
            op("tensor", lambda h, c=c: h.transpose(
                p.PSB[:, 128 * c:128 * (c + 1)], wqb3[:, c, :], p.idb_sb))
        mark("tensor", CT)
        trv = CT.v
        wait("vector", CT, trv)
        op("vector", lambda h, i=i: h.tensor_copy(
            out=wfm_sb[:, :, 128 * i:128 * (i + 1)], in_=psv), inc=(CV, 1))
        evv = CV.v
        guard[r] = CV.v
    return evv


def _phase_B1(S):
    p = S.p
    op, dma, wait, mark = S.op, S.dma, S.wait, S.mark
    GW, CV, CS, CT = p.GW, p.CV, p.CS, p.CT
    n_sub = p.n_sub
    ev = {}
    _fence(S)
    _wquant_fp4(S, p.wg0_h, p.wg_fm, I0, D)
    wq_ev = _wquant_fp4(S, p.wu0_h, p.wu_fm, I0, D)
    wait("tensor", CV, wq_ev)
    wait("tensor", p.LDs[0], p.LDs[0].v)
    wait("tensor", p.LDs[1], p.LDs[1].v)
    wait("sync", CV, wq_ev)
    wait("sync", p.LDs[0], p.LDs[0].v)
    wait("sync", p.LDs[1], p.LDs[1].v)

    for s in range(n_sub):
        r = s % 2
        LD = p.LDs[r]
        ST = p.STs[r]
        if s >= 1:
            wait("sync", CV, ev[("xq", s - 1)])
        dma("sync", LD, p.bxn[r], p.xn_s[SUB * s:SUB * (s + 1), :])
        ev[("ld", s)] = LD.v
        wait("vector", LD, ev[("ld", s)])
        wait("scalar", LD, ev[("ld", s)])
        cv = _emit_fp4(S, p.bxn[r], p.bxq[:, 0:D], D)
        ev[("xq", s)] = cv
        bxq3 = p.bxq[:, 0:D].rearrange("p (a b) -> p a b", b=128)
        bxqfm_f = p.bxqfm.rearrange("p a b -> p (a b)")
        wait("tensor", CV, cv)
        if s >= 1:
            wait("tensor", CS, ev[("xqev", s - 1)])
            wait("tensor", CV, ev[("hq", s - 1)])
        for c in range(8):
            op("tensor", lambda h, c=c: h.transpose(
                p.PSB[:, 7168 + 128 * c:7168 + 128 * (c + 1)], bxq3[:, c, :],
                p.idb_sb))
        mark("tensor", CT)
        ev[("xqt", s)] = CT.v
        wait("scalar", CT, ev[("xqt", s)])
        op("scalar", lambda h: h.activation(out=bxqfm_f, in_=p.PSB[:, 7168:8192],
                                            func=AF.Copy), inc=(CS, 1))
        ev[("xqev", s)] = CS.v
        wait("tensor", CS, ev[("xqev", s)])
        for c in range(8):
            op("tensor", lambda h, c=c: h.ldweights(p.bxqfm[:, c, :]))
            for (a, wdt) in _islices(I0, 512):
                op("tensor", lambda h, c=c, a=a, wdt=wdt: h.matmul(
                    p.PS[:, a:a + wdt], p.bxqfm[:, c, :], p.wg_fm[:, c, a:a + wdt],
                    start=(c == 0), stop=(c == 7)))
        mark("tensor", CT)
        ev[("gate", s)] = CT.v
        wait("scalar", CT, ev[("gate", s)])
        op("scalar", lambda h: h.activation(out=p.bgs, in_=p.PS[:, 0:I0],
                                            func=AF.Silu), inc=(CS, 1))
        ev[("gs", s)] = CS.v
        wait("tensor", CS, ev[("gs", s)])
        for c in range(8):
            op("tensor", lambda h, c=c: h.ldweights(p.bxqfm[:, c, :]))
            for (a, wdt) in _islices(I0, 512):
                op("tensor", lambda h, c=c, a=a, wdt=wdt: h.matmul(
                    p.PS[:, a:a + wdt], p.bxqfm[:, c, :], p.wu_fm[:, c, a:a + wdt],
                    start=(c == 0), stop=(c == 7)))
        mark("tensor", CT)
        ev[("up", s)] = CT.v
        wait("vector", CT, ev[("up", s)])
        op("vector", lambda h: h.tensor_tensor(out=p.bgs, in0=p.PS[:, 0:I0],
                                               in1=p.bgs, op=ALU.mult), inc=(CV, 1))
        ev[("h", s)] = CV.v
        for (a, wdt) in _islices(I0, 1408):
            cv = _emit_fp4(S, p.bgs[:, a:a + wdt], p.bhq[:, a:a + wdt], wdt)
        ev[("hq", s)] = cv
        bhq3 = p.bhq[:, 0:I0].rearrange("p (a b) -> p a b", b=128)
        bhqfm_f = p.bhqfm.rearrange("p a b -> p (a b)")
        wait("tensor", CV, ev[("hq", s)])
        if s >= 1:
            wait("tensor", p.STs[1 - r], ev[("st_hq", s - 1)])
            wait("tensor", CS, ev[("hqev", s - 1, 1)])
        for w in range(2):
            if w == 1:
                wait("tensor", CS, ev[("hqev", s, 0)])
            for c in range(11):
                op("tensor", lambda h, w=w, c=c: h.transpose(
                    p.PSB[:, 5632 + 128 * c:5632 + 128 * (c + 1)],
                    bhq3[:, 11 * w + c, :], p.idb_sb))
            mark("tensor", CT)
            ev[("hqt", s, w)] = CT.v
            wait("scalar", CT, ev[("hqt", s, w)])
            op("scalar", lambda h, w=w: h.activation(
                out=bhqfm_f[:, 1408 * w:1408 * (w + 1)], in_=p.PSB[:, 5632:7040],
                func=AF.Copy), inc=(CS, 1))
            ev[("hqev", s, w)] = CS.v
        wait("gpsimd", CS, ev[("hqev", s, 1)])
        dma("gpsimd", ST, p.hq0f_s[:, I0 * s:I0 * (s + 1)], bhqfm_f)
        ev[("st_hq", s)] = ST.v

    p.ev_b1 = ev


def _phase_B2(S):
    p = S.p
    op, dma, wait, mark = S.op, S.dma, S.wait, S.mark
    GW, CV, CS, CT = p.GW, p.CV, p.CS, p.CT
    n_sub = p.n_sub
    ev = {}
    _fence(S)
    wq_ev = _wquant_fp4(S, p.wd0_h, p.wd_fm, D, I0)
    wait("tensor", CV, wq_ev)
    wait("tensor", p.LDs[0], p.LDs[0].v)
    wait("tensor", p.LDs[1], p.LDs[1].v)
    wait("sync", CV, wq_ev)
    wait("sync", p.LDs[0], p.LDs[0].v)
    wait("sync", p.LDs[1], p.LDs[1].v)

    for s in range(n_sub):
        r = s % 2
        LD = p.LDs[r]
        ST = p.STs[r]
        if s >= 1:
            wait("sync", CV, ev[("x1", s - 1)])
            wait("sync", CT, ev[("down", s - 1)])
        dma("sync", LD, p.bhqfm.rearrange("p a b -> p (a b)"),
            p.hq0f_s[:, I0 * s:I0 * (s + 1)])
        dma("sync", LD, p.bx[r], p.x_h[SUB * s:SUB * (s + 1), :])
        dma("sync", LD, p.bs0[r], p.s0_s[SUB * s:SUB * (s + 1), :])
        ev[("ld", s)] = LD.v
        wait("tensor", LD, ev[("ld", s)])
        if s >= 1:
            wait("tensor", CS, ev[("g0", s - 1)])
        for c in range(22):
            op("tensor", lambda h, c=c: h.ldweights(p.bhqfm[:, c, :]))
            for j in range(2):
                op("tensor", lambda h, c=c, j=j: h.matmul(
                    p.PS[:, 512 * j:512 * (j + 1)], p.bhqfm[:, c, :],
                    p.wd_fm[:, c, 512 * j:512 * (j + 1)],
                    start=(c == 0), stop=(c == 21)))
        mark("tensor", CT)
        ev[("down", s)] = CT.v
        wait("scalar", CT, ev[("down", s)])
        op("scalar", lambda h: h.activation(out=p.bg0, in_=p.PS[:, 0:1024],
                                            func=AF.Silu), inc=(CS, 1))
        ev[("g0", s)] = CS.v
        wait("vector", CS, ev[("g0", s)])
        wait("vector", LD, ev[("ld", s)])
        if s >= 1:
            wait("vector", p.STs[1 - r], ev[("st", s - 1)])
        op("vector", lambda h: h.tensor_tensor(out=p.bg0, in0=p.bs0[r], in1=p.bg0,
                                               op=ALU.mult))
        op("vector", lambda h: h.tensor_tensor(out=p.bx1[r], in0=p.bg0, in1=p.bx[r],
                                               op=ALU.add), inc=(CV, 1))
        ev[("x1", s)] = CV.v
        wait("gpsimd", CV, ev[("x1", s)])
        dma("gpsimd", ST, p.x1_s[SUB * s:SUB * (s + 1), :], p.bx1[r])
        ev[("st", s)] = ST.v

    p.ev_b2 = ev


def _phase_C(S):
    p = S.p
    op, dma, wait, mark = S.op, S.dma, S.wait, S.mark
    GW, CV, CS, CT = p.GW, p.CV, p.CS, p.CT
    n_sub = p.n_sub
    ev = {}
    _fence(S)
    dma("gpsimd", GW, p.wkv1_sb.rearrange("p a b -> p (a b)"), p.wkv1_h)
    wait("tensor", GW, GW.v)
    op("vector", lambda h: h.memset(p.PS[:, 2048:4096], 0.0))
    op("vector", lambda h: h.memset(p.cvext[0].rearrange("p a b -> p (a b)"), 0.0))
    op("vector", lambda h: h.memset(p.cvext[1].rearrange("p a b -> p (a b)"), 0.0))
    op("vector", lambda h: h.memset(p.cvext[0][:, :, 64:65].rearrange("p a b -> p (a b)"), 1.0))
    op("vector", lambda h: h.memset(p.cvext[1][:, :, 64:65].rearrange("p a b -> p (a b)"), 1.0))
    op("vector", lambda h: h.memset(p.mxg, 0.0), inc=(CV, 1))
    cstop = _STOP[0] if (_STOP[0] or "").startswith("C.") else None
    if cstop == "C.pre":
        return

    for s in range(n_sub):
        r = s % 2
        LD = p.LDs[r]
        ST = p.STs[r]
        if s >= 1:
            wait("sync", CV, ev[("xn1", s - 1)])
        dma("sync", LD, p.cx1[r], p.x1_s[SUB * s:SUB * (s + 1), :])
        ev[("ld", s)] = LD.v
        wait("vector", LD, ev[("ld", s)])
        wait("scalar", LD, ev[("ld", s)])
        if s >= 1:
            wait("vector", p.STs[1 - r], ev[("st", s - 1)])
            wait("vector", CS, ev[("xn1b", s - 1)])
            if ("xnt", s - 1) in ev:
                wait("vector", CT, ev[("xnt", s - 1)])
        _emit_ln(S, p.cx1[r], p.cxn1[r])
        ev[("xn1", s)] = CV.v
        wait("scalar", CV, ev[("xn1", s)])
        if ("xnt", s - 2) in ev:
            wait("scalar", CT, ev[("xnt", s - 2)])
        op("scalar", lambda h: h.activation(out=p.cxn1b[r], in_=p.cxn1[r],
                                            func=AF.Copy), inc=(CS, 1))
        ev[("xn1b", s)] = CS.v
        wait("gpsimd", CV, ev[("xn1", s)])
        dma("gpsimd", ST, p.xn1_s[SUB * s:SUB * (s + 1), :], p.cxn1[r])
        ev[("st", s)] = ST.v
        op("vector", lambda h: h.reduce_max(out=p.mx1, in_=p.cxn1[r], axis=AX.X,
                                            apply_absolute_value=True))
        op("vector", lambda h: h.tensor_tensor(out=p.mxg, in0=p.mxg, in1=p.mx1,
                                               op=ALU.max), inc=(CV, 1))
        if cstop == "C.ln":
            continue
        cxb3 = p.cxn1b[r].rearrange("p (a b) -> p a b", b=128)
        cxfm_f = p.cxfm.rearrange("p a b -> p (a b)")
        wait("tensor", CS, ev[("xn1b", s)])
        if ("vex", s - 1) in ev:
            wait("tensor", CV, ev[("vex", s - 1)])
        if ("xfmev", s - 1) in ev:
            wait("tensor", CS, ev[("xfmev", s - 1)])
        for c in range(8):
            op("tensor", lambda h, c=c: h.transpose(
                p.PSB[:, 128 * c:128 * (c + 1)], cxb3[:, c, :], p.idb_sb))
        mark("tensor", CT)
        ev[("xnt", s)] = CT.v
        wait("scalar", CT, ev[("xnt", s)])
        if s >= 1:
            wait("scalar", p.STs[1 - r], ev[("st_xf", s - 1)])
        op("scalar", lambda h: h.activation(out=cxfm_f, in_=p.PSB[:, 0:1024],
                                            func=AF.Copy), inc=(CS, 1))
        ev[("xfmev", s)] = CS.v
        wait("gpsimd", CS, ev[("xfmev", s)])
        dma("gpsimd", ST, p.xn1f_s[:, D * s:D * (s + 1)], cxfm_f)
        ev[("st_xf", s)] = ST.v
        if cstop == "C.xfm":
            continue
        wait("tensor", CS, ev[("xfmev", s)])
        for pr in range(8):
            op("tensor", lambda h, pr=pr: h.ldweights(p.cxfm[:, pr, :]))
            op("tensor", lambda h, pr=pr: h.matmul(
                p.PS[:, 256 * pr:256 * (pr + 1)],
                p.cxfm[:, pr, :], p.wkv1_sb[:, pr, :],
                start=(pr % 2 == 0), stop=(pr % 2 == 1)))
        mark("tensor", CT)
        ev[("kvproj", s)] = CT.v
        if cstop == "C.proj":
            continue
        kview = p.PS[:, 0:2048].rearrange("p (hd kv e) -> p hd kv e", kv=2, e=64)
        kq = kview[:, :, 0, :]
        vq = kview[:, :, 1, :]
        ckn3 = p.ckn.rearrange("p (a b) -> p a b", b=64)
        cke3 = p.cke.rearrange("p (a b) -> p a b", b=64)
        cphik3 = p.cphik.rearrange("p (a b) -> p a b", b=64)
        wait("vector", CT, ev[("kvproj", s)])
        op("vector", lambda h: h.tensor_scalar(out=ckn3, in0=kq, scalar1=0.0,
                                               scalar2=None, op0=ALU.min), inc=(CV, 1))
        ev[("kneg", s)] = CV.v
        if cstop == "C.kneg":
            continue
        wait("scalar", CV, ev[("kneg", s)])
        op("scalar", lambda h: h.activation(out=p.cke, in_=p.ckn, func=AF.Exp),
           inc=(CS, 1))
        ev[("kexp", s)] = CS.v
        if cstop == "C.kexp":
            continue
        wait("vector", CS, ev[("kexp", s)])
        if ("kv", s - 1) in ev:
            wait("vector", CT, ev[("kv", s - 1)])
        op("vector", lambda h: h.scalar_tensor_tensor(
            out=cphik3, in0=kq, scalar=0.0, in1=cke3, op0=ALU.max, op1=ALU.add),
           inc=(CV, 1))
        ev[("phik", s)] = CV.v
        if cstop == "C.stt":
            continue
        if ("kv", s - 2) in ev:
            wait("vector", CT, ev[("kv", s - 2)])
        op("vector", lambda h: h.tensor_copy(
            out=p.cvext[r][:, :, 0:64], in_=vq), inc=(CV, 1))
        ev[("vex", s)] = CV.v
        if cstop == "C.phik":
            continue
        wait("tensor", CV, ev[("vex", s)])
        phv = p.cphik.rearrange("p (hd e) -> p hd e", e=64)
        for hd in range(16):
            col = 2048 + 128 * hd
            op("tensor", lambda h, hd=hd: h.ldweights(phv[:, hd, :]))
            op("tensor", lambda h, hd=hd, col=col, s=s: h.matmul(
                p.PS[0:64, col:col + 72], phv[:, hd, :], p.cvext[r][:, hd, :],
                start=(s == 0 and hd % 4 == 0),
                stop=(s == n_sub - 1 and hd % 4 == 3)))
        mark("tensor", CT)
        ev[("kv", s)] = CT.v
        mark("vector", CV)
        ev[("con", s)] = CV.v

    p.ev_c = ev


def _x2_kv_max(S):
    p = S.p
    op, dma, wait, mark = S.op, S.dma, S.wait, S.mark
    GW, CV, CS, CT, CP = p.GW, p.CV, p.CS, p.CT, p.CP
    _fence(S)
    wait("vector", CT, p.ev_c[("kv", p.n_sub - 1)])
    op("vector", lambda h: h.tensor_copy(out=p.kvsb, in_=p.PS[:, 2048:4096]),
       inc=(CV, 1))
    wait("gpsimd", CV, CV.v)
    dma("gpsimd", p.STs[0], p.cckv_in, p.kvsb)
    wait("tensor", CV, CV.v)
    op("tensor", lambda h: h.transpose(p.PS[0:1, 0:128], p.mxg, p.identf_sb))
    mark("tensor", CT)
    wait("vector", CT, CT.v)
    op("vector", lambda h: h.reduce_max(out=p.mxs, in_=p.PS[0:1, 0:128], axis=AX.X),
       inc=(CV, 1))
    wait("tensor", CV, CV.v)
    op("tensor", lambda h: h.matmul(p.PS[:, 512:513], p.onesf_sb, p.mxs,
                                    start=True, stop=True))
    mark("tensor", CT)
    wait("vector", CT, CT.v)
    op("vector", lambda h: h.tensor_copy(out=p.mx1, in_=p.PS[:, 512:513]),
       inc=(CV, 1))
    wait("gpsimd", CV, CV.v)
    dma("gpsimd", p.STs[0], p.ccm1_in, p.mx1)
    wait("gpsimd", p.STs[0], p.STs[0].v)
    op("gpsimd", lambda h: h.collective_compute(
        "AllReduce", ALU.add, replica_groups=[[0, 1, 2, 3], [4, 5, 6, 7]],
        ins=[p.cckv_in.opt()], outs=[p.cckv_out.opt()]), inc=(CP, 1))
    op("gpsimd", lambda h: h.collective_compute(
        "AllReduce", ALU.max, replica_groups=[[0, 1, 2, 3, 4, 5, 6, 7]],
        ins=[p.ccm1_in.opt()], outs=[p.ccm1_out.opt()]), inc=(CP, 1))
    wait("gpsimd", CP, CP.v)
    dma("gpsimd", GW, p.kvsb, p.cckv_out)
    dma("gpsimd", GW, p.mx1, p.ccm1_out)
    wait("vector", GW, GW.v)
    op("vector", lambda h: h.tensor_copy(out=p.kvb, in_=p.kvsb), inc=(CV, 1))
    wait("scalar", GW, GW.v)
    op("scalar", lambda h: h.activation(out=p.clip_sg, in_=p.clip_sb,
                                        func=AF.Sigmoid), inc=(CS, 1))
    wait("tensor", CS, CS.v)
    op("tensor", lambda h: h.matmul(p.PS[:, 520:522], p.onesf_sb, p.clip_sg,
                                    start=True, stop=True))
    mark("tensor", CT)
    wait("vector", CT, CT.v)
    op("vector", lambda h: h.tensor_tensor(out=p.rsx, in0=p.mx1, in1=p.PS[:, 520:521],
                                           op=ALU.mult))
    op("vector", lambda h: h.tensor_scalar(out=p.rsx, in0=p.rsx, scalar1=1.0 / 127.0,
                                           scalar2=None, op0=ALU.mult), inc=(CV, 1))
    wait("vector", CV, CV.v)
    op("vector", lambda h: h.tensor_copy(out=p.sx, in_=p.rsx), inc=(CV, 1))
    wait("vector", CV, CV.v)
    op("vector", lambda h: h.reciprocal(out=p.rsx, in_=p.rsx))
    op("vector", lambda h: h.tensor_copy(out=p.sgd, in_=p.PS[:, 521:522]),
       inc=(CV, 1))
    p.ev_rsx = CV.v


def _phase_D1(S):
    p = S.p
    op, dma, wait, mark = S.op, S.dma, S.wait, S.mark
    GW, CV, CS, CT = p.GW, p.CV, p.CS, p.CT
    n_sub = p.n_sub
    ev = {}
    _fence(S)
    dma("gpsimd", GW, p.wq1_sb.rearrange("p a b -> p (a b)"), p.wq1_h)
    for c in range(8):
        dma("gpsimd", GW, p.wo1_sb[:, c, :], p.wo1_h[128 * c:128 * (c + 1), :])
    wait("tensor", GW, GW.v)

    for s in range(n_sub):
        r = s % 2
        LD = p.LDs[r]
        ST = p.STs[r]
        if s >= 2:
            wait("sync", CV, ev[("con", s - 2)])
        if s >= 1:
            wait("sync", CT, ev[("qproj", s - 1)])
        dma("sync", LD, p.cxfm.rearrange("p a b -> p (a b)"),
            p.xn1f_s[:, D * s:D * (s + 1)])
        ev[("ld", s)] = LD.v

        wait("tensor", LD, ev[("ld", s)])
        if s >= 1:
            wait("tensor", CV, ev[("phiq", s - 1)])
            wait("tensor", CS, ev[("s1evac", s - 1)])
        for hd in range(16):
            op("tensor", lambda h, hd=hd: h.ldweights(p.wq1_sb[:, hd, :]))
            op("tensor", lambda h, hd=hd: h.matmul(
                p.PS[0:64, 128 * hd:128 * (hd + 1)],
                p.wq1_sb[:, hd, :], p.cxfm[:, hd // 2, :],
                start=(hd % 4 == 0), stop=(hd % 4 == 3)))
        mark("tensor", CT)
        ev[("qproj", s)] = CT.v
        wait("vector", CT, ev[("qproj", s)])
        op("vector", lambda h: h.tensor_scalar(out=p.dknq[0:64, :],
                                               in0=p.PS[0:64, 0:2048],
                                               scalar1=0.0, scalar2=None, op0=ALU.min),
           inc=(CV, 1))
        ev[("qneg", s)] = CV.v
        wait("scalar", CV, ev[("qneg", s)])
        op("scalar", lambda h: h.activation(out=p.dkeq[0:64, :], in_=p.dknq[0:64, :],
                                            func=AF.Exp), inc=(CS, 1))
        ev[("qexp", s)] = CS.v
        wait("vector", CS, ev[("qexp", s)])
        op("vector", lambda h: h.scalar_tensor_tensor(
            out=p.dphiq[0:64, :], in0=p.PS[0:64, 0:2048], scalar=0.0,
            in1=p.dkeq[0:64, :], op0=ALU.max, op1=ALU.add), inc=(CV, 1))
        ev[("phiq", s)] = CV.v

        wait("tensor", CV, ev[("phiq", s)])
        if s >= 1:
            wait("tensor", CV, ev[("comb", s - 1)])
        for hd in range(16):
            lhs = p.dphiq[0:64, 128 * hd:128 * (hd + 1)]
            op("tensor", lambda h, lhs=lhs: h.ldweights(lhs))
            op("tensor", lambda h, lhs=lhs, hd=hd: h.matmul(
                p.PS[:, 2048 + 128 * hd:2048 + 128 * hd + 72],
                lhs, p.kvb3[0:64, hd, 0:72],
                start=(hd % 4 == 0), stop=(hd % 4 == 3)))
        mark("tensor", CT)
        ev[("ctx", s)] = CT.v
        wait("vector", CT, ev[("ctx", s)])
        if s >= 1:
            wait("vector", CT, ev[("dct", s - 1)])
        ctxv = p.PS[:, 2048:4096].rearrange("p (hd e) -> p hd e", e=128)
        op("vector", lambda h: h.tensor_scalar(
            out=p.drz, in0=ctxv[:, :, 64:65].rearrange("p hd o -> p (hd o)"),
            scalar1=EPS1, scalar2=None, op0=ALU.add))
        op("vector", lambda h: h.reciprocal(out=p.drz, in_=p.drz), inc=(CV, 1))
        wait("vector", CV, CV.v)
        for hd in range(16):
            op("vector", lambda h, hd=hd: h.tensor_scalar(
                out=p.dcomb[:, hd, :], in0=ctxv[:, hd, 0:64],
                scalar1=p.drz[:, hd:hd + 1], scalar2=None, op0=ALU.mult))
        mark("vector", CV)
        ev[("comb", s)] = CV.v
        dc3 = p.dcomb.rearrange("p hd (c b) -> p (hd c) b", b=64)
        dcfm_f = p.dcombfm.rearrange("p a b -> p (a b)")
        wait("tensor", CV, ev[("comb", s)])
        if s >= 1:
            wait("tensor", CS, ev[("combev", s - 1)])
        for c in range(8):
            op("tensor", lambda h, c=c: h.transpose(
                p.PSB[:, 4096 + 128 * c:4096 + 128 * (c + 1)],
                p.dcomb.rearrange("p hd e -> p (hd e)")[:, 128 * c:128 * (c + 1)],
                p.idb_sb))
        mark("tensor", CT)
        ev[("dct", s)] = CT.v
        wait("scalar", CT, ev[("dct", s)])
        op("scalar", lambda h: h.activation(out=dcfm_f, in_=p.PSB[:, 4096:5120],
                                            func=AF.Copy), inc=(CS, 1))
        ev[("combev", s)] = CS.v
        wait("tensor", CS, ev[("combev", s)])
        if s >= 1:
            wait("tensor", CS, ev[("s1evac", s - 1)])
        for c in range(8):
            op("tensor", lambda h, c=c: h.ldweights(p.dcombfm[:, c, :]))
            for j in range(2):
                op("tensor", lambda h, c=c, j=j: h.matmul(
                    p.PS[:, 1024 + 512 * j:1024 + 512 * (j + 1)],
                    p.dcombfm[:, c, :], p.wo1_sb[:, c, 512 * j:512 * (j + 1)],
                    start=(c == 0), stop=(c == 7)))
        mark("tensor", CT)
        ev[("s1mm", s)] = CT.v
        wait("scalar", CT, ev[("s1mm", s)])
        if s >= 2:
            wait("scalar", ST, ev[("st", s - 2)])
        op("scalar", lambda h: h.activation(out=p.ds1[r], in_=p.PS[:, 1024:2048],
                                            func=AF.Copy), inc=(CS, 1))
        ev[("s1evac", s)] = CS.v
        wait("gpsimd", CS, ev[("s1evac", s)])
        dma("gpsimd", ST, p.s1_s[SUB * s:SUB * (s + 1), :], p.ds1[r])
        ev[("st", s)] = ST.v
        mark("vector", CV)
        ev[("con", s)] = CV.v

    p.ev_d1 = ev


def _phase_D2(S):
    p = S.p
    op, dma, wait, mark = S.op, S.dma, S.wait, S.mark
    GW, CV, CS, CT = p.GW, p.CV, p.CS, p.CT
    n_sub = p.n_sub
    ev = {}
    _fence(S)
    _wquant_int8(S, p.wg1_h, p.wg_fm, I0, D)
    wq_ev = _wquant_int8(S, p.wu1_h, p.wu_fm, I0, D)
    wait("tensor", CV, wq_ev)
    wait("tensor", p.LDs[0], p.LDs[0].v)
    wait("tensor", p.LDs[1], p.LDs[1].v)
    wait("sync", CV, wq_ev)
    wait("sync", p.LDs[0], p.LDs[0].v)
    wait("sync", p.LDs[1], p.LDs[1].v)
    wait("vector", CV, p.ev_rsx)
    op("vector", lambda h: h.memset(p.mxg, 0.0), inc=(CV, 1))

    for s in range(n_sub):
        r = s % 2
        LD = p.LDs[r]
        ST = p.STs[r]
        if s >= 1:
            wait("sync", CV, ev[("xq1", s - 1)])
        dma("sync", LD, p.dxn1[r], p.xn1_s[SUB * s:SUB * (s + 1), :])
        ev[("ld", s)] = LD.v
        wait("vector", LD, ev[("ld", s)])
        op("vector", lambda h: h.tensor_scalar(out=p.qt.w[:, 0:D], in0=p.dxn1[r],
                                               scalar1=p.rsx, scalar2=None,
                                               op0=ALU.mult), inc=(CV, 1))
        cv = _emit_round_clip(S, p.qt.w[:, 0:D], p.bxq[:, 0:D], D)
        ev[("xq1", s)] = cv
        bxq3 = p.bxq[:, 0:D].rearrange("p (a b) -> p a b", b=128)
        bxqfm_f = p.bxqfm.rearrange("p a b -> p (a b)")
        wait("tensor", CV, cv)
        if s >= 1:
            wait("tensor", CS, ev[("xqev", s - 1)])
            wait("tensor", CV, ev[("h1", s - 1)])
        for c in range(8):
            op("tensor", lambda h, c=c: h.transpose(
                p.PSB[:, 7168 + 128 * c:7168 + 128 * (c + 1)], bxq3[:, c, :],
                p.idb_sb))
        mark("tensor", CT)
        ev[("xqt", s)] = CT.v
        wait("scalar", CT, ev[("xqt", s)])
        op("scalar", lambda h: h.activation(out=bxqfm_f, in_=p.PSB[:, 7168:8192],
                                            func=AF.Copy), inc=(CS, 1))
        ev[("xqev", s)] = CS.v
        wait("tensor", CS, ev[("xqev", s)])
        for c in range(8):
            op("tensor", lambda h, c=c: h.ldweights(p.bxqfm[:, c, :]))
            for (a, wdt) in _islices(I0, 512):
                op("tensor", lambda h, c=c, a=a, wdt=wdt: h.matmul(
                    p.PS[:, a:a + wdt], p.bxqfm[:, c, :], p.wg_fm[:, c, a:a + wdt],
                    start=(c == 0), stop=(c == 7)))
        mark("tensor", CT)
        ev[("gate", s)] = CT.v
        wait("scalar", CT, ev[("gate", s)])
        op("scalar", lambda h: h.activation(out=p.bgs, in_=p.PS[:, 0:I0],
                                            func=AF.Silu, scale=p.sx), inc=(CS, 1))
        ev[("gs", s)] = CS.v
        wait("tensor", CS, ev[("gs", s)])
        for c in range(8):
            op("tensor", lambda h, c=c: h.ldweights(p.bxqfm[:, c, :]))
            for (a, wdt) in _islices(I0, 512):
                op("tensor", lambda h, c=c, a=a, wdt=wdt: h.matmul(
                    p.PS[:, a:a + wdt], p.bxqfm[:, c, :], p.wu_fm[:, c, a:a + wdt],
                    start=(c == 0), stop=(c == 7)))
        mark("tensor", CT)
        ev[("up", s)] = CT.v
        wait("vector", CT, ev[("up", s)])
        if s >= 1:
            wait("vector", p.STs[1 - r], ev[("st", s - 1)])
        op("vector", lambda h: h.scalar_tensor_tensor(
            out=p.bh1[r], in0=p.PS[:, 0:I0], scalar=p.sx, in1=p.bgs,
            op0=ALU.mult, op1=ALU.mult), inc=(CV, 1))
        ev[("h1", s)] = CV.v
        wait("gpsimd", CV, ev[("h1", s)])
        dma("gpsimd", ST, p.h1_s[SUB * s:SUB * (s + 1), :], p.bh1[r])
        ev[("st", s)] = ST.v
        op("vector", lambda h: h.reduce_max(out=p.mx1, in_=p.bh1[r], axis=AX.X,
                                            apply_absolute_value=True))
        op("vector", lambda h: h.tensor_tensor(out=p.mxg, in0=p.mxg, in1=p.mx1,
                                               op=ALU.max), inc=(CV, 1))

    p.ev_d2 = ev


def _x3_maxh(S):
    p = S.p
    op, dma, wait, mark = S.op, S.dma, S.wait, S.mark
    GW, CV, CS, CT, CP = p.GW, p.CV, p.CS, p.CT, p.CP
    _fence(S)
    op("tensor", lambda h: h.transpose(p.PS[0:1, 0:128], p.mxg, p.identf_sb))
    mark("tensor", CT)
    wait("vector", CT, CT.v)
    op("vector", lambda h: h.reduce_max(out=p.mxs, in_=p.PS[0:1, 0:128], axis=AX.X),
       inc=(CV, 1))
    wait("tensor", CV, CV.v)
    op("tensor", lambda h: h.matmul(p.PS[:, 512:513], p.onesf_sb, p.mxs,
                                    start=True, stop=True))
    mark("tensor", CT)
    wait("vector", CT, CT.v)
    op("vector", lambda h: h.tensor_copy(out=p.mx1, in_=p.PS[:, 512:513]),
       inc=(CV, 1))
    wait("gpsimd", CV, CV.v)
    dma("gpsimd", p.STs[0], p.ccm2_in, p.mx1)
    wait("gpsimd", p.STs[0], p.STs[0].v)
    op("gpsimd", lambda h: h.collective_compute(
        "AllReduce", ALU.max, replica_groups=[[0, 1, 2, 3, 4, 5, 6, 7]],
        ins=[p.ccm2_in.opt()], outs=[p.ccm2_out.opt()]), inc=(CP, 1))
    wait("gpsimd", CP, CP.v)
    dma("gpsimd", GW, p.mx1, p.ccm2_out)
    wait("vector", GW, GW.v)
    op("vector", lambda h: h.tensor_tensor(out=p.rsh, in0=p.mx1, in1=p.sgd,
                                           op=ALU.mult))
    op("vector", lambda h: h.tensor_scalar(out=p.rsh, in0=p.rsh, scalar1=1.0 / 127.0,
                                           scalar2=None, op0=ALU.mult), inc=(CV, 1))
    wait("vector", CV, CV.v)
    op("vector", lambda h: h.tensor_copy(out=p.sh, in_=p.rsh), inc=(CV, 1))
    wait("vector", CV, CV.v)
    op("vector", lambda h: h.reciprocal(out=p.rsh, in_=p.rsh), inc=(CV, 1))
    p.ev_rsh = CV.v


def _phase_E(S):
    p = S.p
    op, dma, wait, mark = S.op, S.dma, S.wait, S.mark
    GW, CV, CS, CT = p.GW, p.CV, p.CS, p.CT
    n_sub = p.n_sub
    ev = {}
    _fence(S)
    wq_ev = _wquant_int8(S, p.wd1_h, p.wd_fm, D, I0)
    wait("tensor", CV, wq_ev)
    wait("tensor", p.LDs[0], p.LDs[0].v)
    wait("tensor", p.LDs[1], p.LDs[1].v)
    wait("sync", CV, wq_ev)
    wait("sync", p.LDs[0], p.LDs[0].v)
    wait("sync", p.LDs[1], p.LDs[1].v)
    wait("vector", CV, p.ev_rsh)

    for s in range(n_sub):
        r = s % 2
        LD = p.LDs[r]
        ST = p.STs[r]
        if s >= 1:
            wait("sync", CV, ev[("x2", s - 1)])
            wait("sync", CV, ev[("hq", s - 1)])
        dma("sync", LD, p.eh1[r], p.h1_s[SUB * s:SUB * (s + 1), :])
        dma("sync", LD, p.es1[r], p.s1_s[SUB * s:SUB * (s + 1), :])
        dma("sync", LD, p.ex1[r], p.x1_s[SUB * s:SUB * (s + 1), :])
        ev[("ld", s)] = LD.v
        wait("vector", LD, ev[("ld", s)])
        op("vector", lambda h: h.tensor_scalar(out=p.qt.hv[:, 0:I0], in0=p.eh1[r],
                                               scalar1=p.rsh, scalar2=None,
                                               op0=ALU.mult), inc=(CV, 1))
        for (a, wdt) in _islices(I0, 1408):
            cv = _emit_round_clip(S, p.qt.hv[:, a:a + wdt], p.bhq[:, a:a + wdt], wdt)
        ev[("hq", s)] = cv
        bhq3 = p.bhq[:, 0:I0].rearrange("p (a b) -> p a b", b=128)
        bhqfm_f = p.bhqfm.rearrange("p a b -> p (a b)")
        wait("tensor", CV, ev[("hq", s)])
        if s >= 1:
            wait("tensor", CS, ev[("hqev", s - 1)])
        for c in range(22):
            op("tensor", lambda h, c=c: h.transpose(
                p.PSB[:, 4096 + 128 * c:4096 + 128 * (c + 1)], bhq3[:, c, :],
                p.idb_sb))
        mark("tensor", CT)
        ev[("hqt", s)] = CT.v
        wait("scalar", CT, ev[("hqt", s)])
        op("scalar", lambda h: h.activation(out=bhqfm_f, in_=p.PSB[:, 4096:6912],
                                            func=AF.Copy), inc=(CS, 1))
        ev[("hqev", s)] = CS.v
        wait("tensor", CS, ev[("hqev", s)])
        if s >= 1:
            wait("tensor", CS, ev[("g1", s - 1)])
        for c in range(22):
            op("tensor", lambda h, c=c: h.ldweights(p.bhqfm[:, c, :]))
            for j in range(2):
                op("tensor", lambda h, c=c, j=j: h.matmul(
                    p.PS[:, 512 * j:512 * (j + 1)], p.bhqfm[:, c, :],
                    p.wd_fm[:, c, 512 * j:512 * (j + 1)],
                    start=(c == 0), stop=(c == 21)))
        mark("tensor", CT)
        ev[("down", s)] = CT.v
        wait("scalar", CT, ev[("down", s)])
        op("scalar", lambda h: h.activation(out=p.bg0, in_=p.PS[:, 0:1024],
                                            func=AF.Silu, scale=p.sh), inc=(CS, 1))
        ev[("g1", s)] = CS.v
        wait("vector", CS, ev[("g1", s)])
        if s >= 1:
            wait("vector", p.STs[1 - r], ev[("st", s - 1)])
        op("vector", lambda h: h.tensor_tensor(out=p.bg0, in0=p.es1[r], in1=p.bg0,
                                               op=ALU.mult))
        op("vector", lambda h: h.tensor_tensor(out=p.ex2[r], in0=p.bg0, in1=p.ex1[r],
                                               op=ALU.add), inc=(CV, 1))
        ev[("x2", s)] = CV.v
        wait("gpsimd", CV, ev[("x2", s)])
        dma("gpsimd", ST, p.y_h[SUB * s:SUB * (s + 1), :], p.ex2[r])
        ev[("st", s)] = ST.v

    p.ev_e = ev


def _mk_prog(nc, n_sub, stack):
    p = NS()
    p.n_sub = n_sub
    ntok = n_sub * SUB

    def din(name, shape, dt=F32):
        return nc.dram_tensor(name, list(shape), dt, kind="ExternalInput").ap()

    def scr(name, shape, dt=F32):
        return nc.dram_tensor(name, list(shape), dt, kind="Internal").ap()

    p.x_h = din("x", (ntok, D))
    p.ropec_h = din("ropec", (ntok, 128))
    p.ropes_h = din("ropes", (ntok, 128))
    p.ropecq_h = din("ropecq", (ntok, 128))
    p.ropesq_h = din("ropesq", (ntok, 128))
    p.kdect_h = din("kdect", (SUB, H0))
    p.crosst_h = din("crosst", (SUB, H0))
    p.dmaskt_h = din("dmaskt", (SUB, H0 * SUB))
    p.lnw_rep_h = din("lnwrep", (SUB, D))
    p.lnb_rep_h = din("lnbrep", (SUB, D))
    p.gnw_rep_h = din("gnwrep", (SUB, DV))
    p.sxw_h = din("sxw", (SUB, 3 * H0))
    p.sxm_h = din("sxm", (SUB, 4))
    p.actc_h = din("actc", (SUB, len(ACT_CONSTS)))
    p.onesf_h = din("onesf", (1, SUB))
    p.identf_h = din("identf", (SUB, SUB))
    p.clipx_h = din("clipx", (1, 2))
    p.wkv_h = din("wkv", (D, 3 * D))
    p.wqg_h = din("wqg", (D, 3 * D))
    p.wo0_h = din("wo0", (2 * D, D))
    p.wg0_h = din("wg0", (I0, D))
    p.wu0_h = din("wu0", (I0, D))
    p.wd0_h = din("wd0", (D, I0))
    p.wq1_h = din("wq1", (SUB, 16 * 64))
    p.wkv1_h = din("wkv1", (SUB, 8 * 256))
    p.wo1_h = din("wo1", (D, D))
    p.wg1_h = din("wg1", (I0, D))
    p.wu1_h = din("wu1", (I0, D))
    p.wd1_h = din("wd1", (D, I0))
    p.y_h = nc.dram_tensor("y", [ntok, D], F32, kind="ExternalOutput").ap()

    p.xn_s = scr("xn_s", (ntok, D))
    p.normed_f_s = scr("normed_f_s", (SUB, n_sub * D), BF16)
    p.ktm_f_s = scr("ktm_f_s", (SUB, n_sub * D), BF16)
    p.ktm_s = scr("ktm_s", (ntok, D), BF16)
    p.vtm_s = scr("vtm_s", (ntok, 2 * D), BF16)
    p.s0_s = scr("s0_s", (ntok, D))
    p.x1_s = scr("x1_s", (ntok, D))
    p.xn1_s = scr("xn1_s", (ntok, D))
    p.xn1f_s = scr("xn1f_s", (SUB, n_sub * D), BF16)
    p.hq0f_s = scr("hq0f_s", (SUB, n_sub * I0), BF16)
    p.s1_s = scr("s1_s", (ntok, D))
    p.h1_s = scr("h1_s", (ntok, I0))
    p.ccs_in = scr("ccs_in", (SUB, 3 * H0 * 2 * 512), BF16)
    p.ccs_out = scr("ccs_out", (SUB, 3 * H0 * 2 * 512), BF16)
    p.cckv_in = scr("cckv_in", (SUB, 2048))
    p.cckv_out = scr("cckv_out", (SUB, 2048))
    p.ccm1_in = scr("ccm1_in", (SUB, 1))
    p.ccm1_out = scr("ccm1_out", (SUB, 1))
    p.ccm2_in = scr("ccm2_in", (SUB, 1))
    p.ccm2_out = scr("ccm2_out", (SUB, 1))

    p.PS = stack.enter_context(nc.psum_tensor("PS", [128, 4096], F32))[:]
    p.PSB = p.PS.bitcast(BF16)

    def sb(name, shape, dt=F32):
        return stack.enter_context(nc.sbuf_tensor(name, list(shape), dt))[:]

    p.actc_sb = sb("actc_sb", (SUB, len(ACT_CONSTS)))
    for i, v in enumerate(ACT_CONSTS):
        nc.const_aps.register(F32, v, p.actc_sb[:, i:i + 1])
    p.kdect_sb = sb("kdect_sb", (SUB, H0))
    p.crosst_sb = sb("crosst_sb", (SUB, H0))
    p.sxw_sb = sb("sxw_sb", (SUB, 3 * H0))
    p.sxm_sb = sb("sxm_sb", (SUB, 4))
    p.onesf_sb = sb("onesf_sb", (1, SUB))
    p.identf_sb = sb("identf_sb", (SUB, SUB))
    p.idb_sb = sb("idb_sb", (SUB, SUB), BF16)
    p.clip_sb = sb("clip_sb", (1, 2))
    p.clip_sg = sb("clip_sg", (1, 2))
    p.trash = sb("trash", (SUB, D))
    p.trash2 = sb("trash2", (SUB, D))
    p.sums = sb("sums", (SUB, 4))
    p.mu = sb("mu", (SUB, 1))
    p.rstd = sb("rstd", (SUB, 1))
    p.mu2 = sb("mu2", (SUB, 1))
    p.mx1 = sb("mx1", (SUB, 1))
    p.mxg = sb("mxg", (SUB, 1))
    p.mxs = sb("mxs", (1, 1))
    p.rsx = sb("rsx", (SUB, 1))
    p.sx = sb("sx", (SUB, 1))
    p.rsh = sb("rsh", (SUB, 1))
    p.sh = sb("sh", (SUB, 1))
    p.sgd = sb("sgd", (SUB, 1))
    p.rm = sb("rm", (SUB, 1))
    p.rr1 = sb("rr1", (SUB, 1))
    p.tmp1 = sb("tmp1", (SUB, 128))
    p.tmp2 = sb("tmp2", (SUB, 128))
    p.tmp3 = sb("tmp3", (SUB, 512))
    p.drz = sb("drz", (SUB, 16))

    AR_COLS = 49872
    p.AR = stack.enter_context(nc.sbuf_tensor("AR", [128, AR_COLS], F32))[:]
    p._bump = [0]

    def take(cols, dt=F32, shape=None, at=None):
        a = p._bump[0] if at is None else at
        end = a + ((cols + 15) // 16) * 16
        assert end <= AR_COLS, f"arena overflow {end}"
        if at is None:
            p._bump[0] = end
        ap = p.AR[:, a:a + cols]
        if dt == BF16:
            ap = ap.bitcast(BF16)
        if shape is not None:
            names = " ".join(f"d{i}" for i in range(len(shape)))
            ap = ap.rearrange(f"p ({names}) -> p {names}",
                              **{f"d{i}": v for i, v in enumerate(shape)})
        return ap

    p.take = take
    return p


def _alloc_buffers(p):
    take = p.take
    w0 = p._bump[0]
    assert w0 == 0
    p.wkv_sb = take(12288, BF16, (8, 3072))
    p.wqg_sb = p.wkv_sb
    p.wo0_sb = take(8192, BF16, (16, 1024))
    wg_at = 0
    p.wg_fm = take(11264, BF16, (8, I0), at=0)
    p.wu_fm = take(11264, BF16, (8, I0), at=11264)
    p.wd_fm = take(11264, BF16, (22, 1024), at=0)
    p.wo1_sb = take(4096, BF16, (8, 1024), at=11264)
    p.wq1_sb = take(512, BF16, (16, 64), at=11264 + 4096)
    p.wkv1_sb = take(1024, BF16, (8, 256), at=11264 + 4608)
    p._bump[0] = max(p._bump[0], 22528)
    p.xch = take(6144, BF16, (3, H0, 2, 512), at=12288)

    p.a0x = [take(D) for _ in range(2)]
    p.a0xn = [take(D) for _ in range(2)]
    p.a0z = take(D)
    p.a0nb = [take(512, BF16) for _ in range(2)]
    p.normfm = [take(512, BF16, (8, 128)) for _ in range(2)]
    p.ropec_t = [take(128) for _ in range(2)]
    p.ropes_t = [take(128) for _ in range(2)]
    p.a0ktm = [take(512, BF16, (H0, 256)) for _ in range(2)]
    p.a0ktil = take(512, BF16, (H0, 256))
    p.a0vtm = [take(1024, BF16) for _ in range(2)]
    p.a1nfm = p.normfm
    p.a1kfm = [take(512, BF16, (8, 128)) for _ in range(2)]
    p.a1ktm = [take(512, BF16) for _ in range(2)]
    p.a1vtm = p.a0vtm
    p.a1xn = p.a0x
    p.a1qtm = take(512, BF16, (H0, 256))
    p.a1qctm = take(512, BF16, (H0, 256))
    p.a1qfm = take(512, BF16, (8, 128))
    p.a1qcfm = take(512, BF16, (8, 128))
    p.a1gs = take(2 * D)
    p.a1at = take(256, BF16, (H0, 128))
    p.a1og = take(1024, BF16, (H0, 512))
    p.a1ogfm = take(1024, BF16, (16, 128))
    p.a1s0 = [p.a0xn[0], p.a0xn[1]]
    p.lnw_sb = take(D)
    p.lnb_sb = take(D)
    p.gnw_sb = take(DV)
    p.dmask_sb = take(H0 * SUB)
    p.S_sb = take(H0 * 512, BF16, (H0, 2, 512))
    g1_end = p._bump[0]
    print(f"[kernel] arena group1 end {g1_end}")
    p._bump[0] = 22528

    q0 = p._bump[0]
    q = NS()
    q.aw = take(1408)
    q.y0 = take(1408)
    q.tt = take(1408)
    q.d = take(1408)
    q.sg = [take(704, BF16) for _ in range(2)]
    q.tmpb = take(704, BF16)
    q.acc1 = take(704, BF16)
    q.acc2 = take(704, BF16)
    q.sgn = take(704, BF16)
    q.bm = take(176)
    q.rr = take(176)
    q.w = take(1408)
    p.qt = q
    qend = p._bump[0]
    p.cxfm = take(512, BF16, (8, 128), at=q0)
    p.ckn = take(1024, at=q0 + 512)
    p.cke = take(1024, at=q0 + 1536)
    p.cphik = take(512, BF16, at=q0 + 2560)
    p.cvext = [take(576, BF16, (16, 72), at=q0 + 3072 + 576 * i) for i in range(2)]
    p.dknq = take(2048, at=q0 + 512)
    p.dkeq = take(2048, at=q0 + 2560)
    p.dphiq = take(1024, BF16, at=q0 + 4608)
    p.dcomb = take(512, BF16, (16, 64), at=q0 + 5632)
    p.dcombfm = take(512, BF16, (8, 128), at=q0 + 6144)
    p.ds1 = [take(1024, at=q0 + 6656 + 1024 * i) for i in range(2)]
    p.kvsb = take(2048, at=q0 + 7808)
    p.kvb = take(1024, BF16)
    p.kvb3 = p.kvb.rearrange("p (hd e) -> p hd e", e=128)
    assert q0 + 9856 <= qend, (q0, qend)

    bxn1 = take(D)
    p.bxn = [bxn1, bxn1]
    p.bx = p.bxn
    bs01 = take(D)
    p.bs0 = [bs01, bs01]
    p.bxq = take(512, BF16)
    bhqfm_cols = take(1408, BF16)
    p.bhqfm = bhqfm_cols.rearrange("p (a b) -> p a b", a=22)
    p.bxqfm = take(512, BF16, (8, 128))
    p.bgs = take(I0)
    p.bhq = take(1408, BF16)
    p.bg0 = take(D)
    bx11 = take(D)
    p.bx1 = [bx11, bx11]
    cx11 = bhqfm_cols.bitcast(F32)[:, 0:D]
    p.cx1 = [cx11, cx11]
    cxn11 = p.bgs[:, 0:D]
    p.cxn1 = [cxn11, cxn11]
    cxn1b1 = p.bhq[:, 0:D]
    p.cxn1b = [cxn1b1, cxn1b1]
    bh11 = take(I0)
    p.bh1 = [bh11, bh11]
    q.wslab = [p.bgs, p.bh1[0]]
    q.wqb = p.bhq
    q.hv = p.bgs
    p.dxn1 = p.cx1
    p.eh1 = p.bh1
    p.es1 = p.ds1
    p.ex1 = p.bs0
    p.ex2 = p.bx1
    print(f"[kernel] arena group2 end {p._bump[0]} / 49936 f32 cols")


_STOP = [None]


def build_nc(n_sub=NSUB, stop=None):
    _STOP[0] = stop
    import contextlib
    nc = bass.Bass("TRN2", target_bir_lowering=False)
    with contextlib.ExitStack() as stack:
        p = _mk_prog(nc, n_sub, stack)
        _alloc_buffers(p)
        with (
            nc.semaphore("sLD") as s_ld,
            nc.semaphore("sLD1") as s_ld1,
            nc.semaphore("sGW") as s_gw,
            nc.semaphore("sST") as s_st,
            nc.semaphore("sST1") as s_st1,
            nc.semaphore("sCV") as s_cv,
            nc.semaphore("sCS") as s_cs,
            nc.semaphore("sCT") as s_ct,
            nc.semaphore("sCP") as s_cp,
            nc.Block() as block,
        ):
            def fresh():
                p.LDs = [Ctr(s_ld, "LD0"), Ctr(s_ld1, "LD1")]
                p.GW = Ctr(s_gw, "GW")
                p.STs = [Ctr(s_st, "ST0"), Ctr(s_st1, "ST1")]
                p.CV = Ctr(s_cv, "CV")
                p.CS = Ctr(s_cs, "CS")
                p.CT = Ctr(s_ct, "CT")
                p.CP = Ctr(s_cp, "CP")

            @block.sync
            def _(h):
                fresh()
                _emit(p, "sync", h)

            @block.gpsimd
            def _(h):
                fresh()
                _emit(p, "gpsimd", h)

            @block.vector
            def _(h):
                fresh()
                _emit(p, "vector", h)

            @block.scalar
            def _(h):
                fresh()
                _emit(p, "scalar", h)

            @block.tensor
            def _(h):
                fresh()
                _emit(p, "tensor", h)

    return nc


def _prep_inputs(gi, n_sub=NSUB):
    ntok = n_sub * SUB
    tab = _host_tables(4 * ntok)
    x = np.ascontiguousarray(np.asarray(gi["L0_attn_x"], np.float32))
    wkv = np.ascontiguousarray(np.concatenate(
        [np.asarray(gi["L0_attn_Wk"], np.float32),
         np.asarray(gi["L0_attn_Wv"], np.float32)], axis=1))
    wqg = np.ascontiguousarray(np.concatenate(
        [np.asarray(gi["L0_attn_Wq"], np.float32),
         np.asarray(gi["L0_attn_Wg"], np.float32)], axis=1))
    rep = lambda v: np.ascontiguousarray(
        np.tile(np.asarray(v, np.float32)[None, :], (SUB, 1)))

    wqh = np.asarray(gi["L1_attn_Wq"], np.float32).reshape(8, 2, 64, 64)
    wq1b = np.zeros((8, 2, 2, 64, 64), np.float32)
    for hh in range(2):
        wq1b[:, hh, hh, :, :] = wqh[:, hh]
    wq1 = np.ascontiguousarray(
        wq1b.transpose(2, 3, 0, 1, 4).reshape(SUB, 16 * 64))
    wkh = np.asarray(gi["L1_attn_Wk"], np.float32).reshape(8, 2, 64, 64)
    wvh = np.asarray(gi["L1_attn_Wv"], np.float32).reshape(8, 2, 64, 64)
    wkv1b = np.zeros((8, 2, 64, 2, 2, 64), np.float32)
    for hh in range(2):
        wkv1b[:, hh, :, hh, 0, :] = wkh[:, hh]
        wkv1b[:, hh, :, hh, 1, :] = wvh[:, hh]
    wkv1 = np.ascontiguousarray(
        wkv1b.transpose(1, 2, 0, 3, 4, 5).reshape(SUB, 8 * 256))
    clip = np.array([[float(np.asarray(gi["L1_ffn_clip_x"]).reshape(-1)[0]),
                      float(np.asarray(gi["L1_ffn_clip_down"]).reshape(-1)[0])]],
                    np.float32)
    actc = np.tile(np.asarray(ACT_CONSTS, np.float32)[None, :], (SUB, 1))

    ins = []
    for c in range(8):
        b, pos = c // 4, c % 4
        t0 = pos * ntok
        sxw, sxm = _core_tables(pos, ntok)
        sl = slice(t0, t0 + ntok)
        ins.append({
            "x": np.ascontiguousarray(x[b, t0:t0 + ntok]),
            "ropec": np.ascontiguousarray(tab["ropec"][sl]),
            "ropes": np.ascontiguousarray(tab["ropes"][sl]),
            "ropecq": np.ascontiguousarray(tab["ropecq"][sl]),
            "ropesq": np.ascontiguousarray(tab["ropesq"][sl]),
            "kdect": tab["kdect"],
            "crosst": tab["crosst"],
            "dmaskt": np.ascontiguousarray(tab["dmaskt"].reshape(SUB, H0 * SUB)),
            "lnwrep": rep(gi["L0_attn_ln_w"]),
            "lnbrep": rep(gi["L0_attn_ln_b"]),
            "gnwrep": rep(gi["L0_attn_g_norm_w"]),
            "sxw": np.ascontiguousarray(np.tile(sxw.reshape(1, -1), (SUB, 1))),
            "sxm": np.ascontiguousarray(np.tile(sxm.reshape(1, 4), (SUB, 1))),
            "actc": actc,
            "onesf": np.ones((1, SUB), np.float32),
            "identf": np.eye(SUB, dtype=np.float32),
            "clipx": clip,
            "wkv": wkv, "wqg": wqg,
            "wo0": np.asarray(gi["L0_attn_Wo"], np.float32),
            "wg0": np.asarray(gi["L0_ffn_w_gate"], np.float32),
            "wu0": np.asarray(gi["L0_ffn_w_up"], np.float32),
            "wd0": np.asarray(gi["L0_ffn_w_down"], np.float32),
            "wq1": wq1, "wkv1": wkv1,
            "wo1": np.asarray(gi["L1_attn_Wo"], np.float32),
            "wg1": np.asarray(gi["L1_ffn_w_gate"], np.float32),
            "wu1": np.asarray(gi["L1_ffn_w_up"], np.float32),
            "wd1": np.asarray(gi["L1_ffn_w_down"], np.float32),
        })
    return ins


def kernel(**inputs):
    nc = build_nc(NSUB)
    ins = _prep_inputs(inputs, NSUB)
    res = run_bass_kernel_spmd(nc, ins, core_ids=list(range(8)))
    out = np.empty((B, T, D), np.float32)
    for c in range(8):
        b, pos = c // 4, c % 4
        out[b, pos * TPC:(pos + 1) * TPC] = res.results[c]["y"]
    return out


# version 06 (final)
# speedup vs baseline: 1.3189x; 1.0364x over previous
import numpy as np
import ml_dtypes
import concourse.bass as bass
import concourse.mybir as mybir
from concourse.bass_utils import run_bass_kernel_spmd

F32 = mybir.dt.float32
BF16 = mybir.dt.bfloat16
AF = mybir.ActivationFunctionType
ALU = mybir.AluOpType
AX = mybir.AxisListType
BF = ml_dtypes.bfloat16

B, T, D = 2, 16384, 1024
TPC = T // 4
SUB = 128
NSUB = TPC // SUB
H0, DK, DV = 4, 256, 512
H1, E1 = 16, 64
I0 = 2816
EPSM = 1e-5
EPS0 = 1e-5
EPS1 = 1e-6

GAMMA = [1.0 - 2.0 ** (-5.0 - h) for h in range(H0)]
GC = [g ** SUB for g in GAMMA]

FP4_TH = [0.5, 1.5, 2.5, 3.5, 5.0, 7.0, 10.0]
ACT_CONSTS = [-th for th in FP4_TH] + [-1.0, 1e-5, 1.0 / 512.0]


class Ctr:
    def __init__(self, sem, name):
        self.sem = sem
        self.name = name
        self.v = 0


class NS:
    pass


def _islices(total, step):
    out = []
    a = 0
    while a < total:
        out.append((a, min(step, total - a)))
        a += step
    return out


def _host_tables(t_total=T):
    t = {}
    invf = 10000.0 ** (-np.arange(0, DK, 2, dtype=np.float64) / DK)
    tt = np.arange(t_total, dtype=np.float64)[:, None]
    ang = tt * invf[None, :]
    t["ropec"] = np.cos(ang).astype(np.float32)
    t["ropes"] = np.sin(ang).astype(np.float32)
    s0 = DK ** (-0.5)
    t["ropecq"] = (np.cos(ang) * s0).astype(np.float32)
    t["ropesq"] = (np.sin(ang) * s0).astype(np.float32)
    i_ = np.arange(SUB, dtype=np.float64)[:, None]
    g_ = np.asarray(GAMMA, dtype=np.float64)[None, :]
    t["kdect"] = (g_ ** (SUB - 1.0 - i_)).astype(np.float32)
    t["crosst"] = (g_ ** (i_ + 1.0)).astype(np.float32)
    ii = np.arange(SUB, dtype=np.float64)
    dmk = np.zeros((SUB, H0, SUB), dtype=np.float64)
    for h in range(H0):
        m = np.where(ii[:, None] >= ii[None, :],
                     np.float64(GAMMA[h]) ** (ii[:, None] - ii[None, :]), 0.0)
        dmk[:, h, :] = m.T
    t["dmaskt"] = dmk.astype(np.float32)
    return t


def _core_tables(pos, tpc=TPC):
    w = np.zeros((3, H0), dtype=np.float64)
    for ps in (1, 2, 3):
        if ps > pos:
            for h in range(H0):
                w[ps - 1, h] = np.float64(GAMMA[h]) ** (tpc * (ps - 1 - pos))
    m = np.zeros((4,), dtype=np.float32)
    if pos >= 1:
        m[pos - 1] = 1.0
    return w.astype(np.float32), m


def _emit(p, active, h):
    S = NS()
    S.p = p
    S.active = active
    S.h = h

    S.last = None
    S.pend = {}
    S.pend_all = {e: set() for e in ("sync", "gpsimd", "vector", "scalar", "tensor")}

    def _flush():
        if S.last is not None:
            for ctr, amt in S.pend.items():
                S.last.then_inc(ctr.sem, amt)
        S.pend = {}

    _AUTOC = {"vector": "CV", "scalar": "CS"}

    def op(eng, fn, inc=None):
        ctr0 = getattr(p, _AUTOC[eng]) if eng in _AUTOC else None
        if ctr0 is not None and eng == active and ctr0.v > 0:
            _flush()
            S.h.wait_ge(ctr0.sem, ctr0.v)
        ins = None
        if eng == active:
            _flush()
            ins = fn(S.h)
            S.last = ins
        S.pend_all[eng] = set()
        if ctr0 is not None:
            ctr0.v += 1
            S.pend_all[eng].add(ctr0.name)
            if ins is not None:
                S.pend[ctr0] = S.pend.get(ctr0, 0) + 1
        if inc is not None:
            ctr, amt = inc
            ctr.v += amt
            S.pend_all[eng].add(ctr.name)
            if ins is not None:
                S.pend[ctr] = S.pend.get(ctr, 0) + amt
        return ins

    def mark(eng, ctr):
        if ctr.name in S.pend_all[eng]:
            return
        ctr.v += 1
        S.pend_all[eng].add(ctr.name)
        if eng == active:
            assert S.last is not None
            S.pend[ctr] = S.pend.get(ctr, 0) + 1

    S.flush = _flush

    S.drained = {"sync": set(), "gpsimd": set()}

    def dma(eng, ctr, out, in_, transpose=False):
        if eng in S.drained and ctr.name not in S.drained[eng]:
            S.drained[eng].add(ctr.name)
            if eng == active:
                S.h.wait_ge(ctr.sem, ctr.v)
        if eng == active:
            _flush()
            S.h.dma_start(out=out, in_=in_, transpose=transpose).then_inc(ctr.sem, 16)
        ctr.v += 16

    def wait(eng, ctr, val=None):
        if eng in S.drained:
            S.drained[eng] = set()
        v = ctr.v if val is None else val
        assert v <= ctr.v, (ctr.name, v, ctr.v)
        if eng == active:
            S.h.wait_ge(ctr.sem, v)

    S.op, S.dma, S.wait, S.mark = op, dma, wait, mark
    _program(S)
    _flush()


def _program(S):
    p = S.p
    dma, wait = S.dma, S.wait
    LD = p.LDs[0]

    dma("sync", LD, p.actc_sb, p.actc_h)
    dma("sync", LD, p.kdect_sb, p.kdect_h)
    dma("sync", LD, p.crosst_sb, p.crosst_h)
    dma("sync", LD, p.dmask_sb, p.dmaskt_h)
    dma("sync", LD, p.lnw_sb, p.lnw_rep_h)
    dma("sync", LD, p.lnb_sb, p.lnb_rep_h)
    dma("sync", LD, p.gnw_sb, p.gnw_rep_h)
    dma("sync", LD, p.sxw_sb, p.sxw_h)
    dma("sync", LD, p.sxm_sb, p.sxm_h)
    dma("sync", LD, p.onesf_sb, p.onesf_h)
    dma("sync", LD, p.identf_sb, p.identf_h)
    dma("sync", LD, p.clip_sb, p.clipx_h)
    ld_consts = LD.v
    dma("gpsimd", p.GW, p.idb_sb, p.identf_h)
    for e in ("vector", "scalar", "tensor", "gpsimd", "sync"):
        wait(e, LD, ld_consts)
    wait("tensor", p.GW, p.GW.v)

    phases = [("A0", _phase_A0), ("X1", _x1_state_exchange), ("A1", _phase_A1),
              ("B1", _phase_B1), ("B2", _phase_B2), ("C", _phase_C),
              ("X2", _x2_kv_max), ("D1", _phase_D1), ("D2", _phase_D2),
              ("X3", _x3_maxh), ("E", _phase_E)]
    for name, fn in phases:
        fn(S)
        st = _STOP[0] or ""
        if name == st or st.startswith(name + "."):
            break

    wait("gpsimd", p.STs[0], p.STs[0].v)
    wait("gpsimd", p.STs[1], p.STs[1].v)


def _emit_ln(S, x_ap, out_ap, affine_out=None):
    p = S.p
    op, wait, mark = S.op, S.wait, S.mark
    CV, CS = p.CV, p.CS
    wait("scalar", CV, CV.v)
    op("scalar", lambda h: h.activation(out=p.trash, in_=x_ap, func=AF.Copy,
                                        accum_out=p.sums[:, 0:1]))
    op("scalar", lambda h: h.activation(out=p.trash2, in_=x_ap, func=AF.Square,
                                        accum_out=p.sums[:, 1:2]), inc=(CS, 1))
    wait("vector", CS, CS.v)
    op("vector", lambda h: h.tensor_scalar(out=p.mu, in0=p.sums[:, 0:1],
                                           scalar1=1.0 / 1024.0, scalar2=None,
                                           op0=ALU.mult))
    op("vector", lambda h: h.tensor_scalar(out=p.rstd, in0=p.sums[:, 1:2],
                                           scalar1=1.0 / 1024.0, scalar2=None,
                                           op0=ALU.mult), inc=(CV, 1))
    wait("vector", CV, CV.v)
    op("vector", lambda h: h.scalar_tensor_tensor(out=p.rstd, in0=p.mu, scalar=p.mu,
                                                  in1=p.rstd, op0=ALU.mult,
                                                  op1=ALU.subtract), inc=(CV, 1))
    wait("scalar", CV, CV.v)
    op("scalar", lambda h: h.activation(out=p.rstd, in_=p.rstd, func=AF.Sqrt,
                                        scale=-1.0, bias=EPSM), inc=(CS, 1))
    wait("vector", CS, CS.v)
    op("vector", lambda h: h.reciprocal(out=p.rstd, in_=p.rstd), inc=(CV, 1))
    wait("vector", CV, CV.v)
    op("vector", lambda h: h.tensor_scalar(out=out_ap, in0=x_ap, scalar1=p.mu,
                                           scalar2=p.rstd, op0=ALU.subtract,
                                           op1=ALU.mult))
    if affine_out is not None:
        op("vector", lambda h: h.tensor_tensor(out=p.trash, in0=out_ap, in1=p.lnw_sb,
                                               op=ALU.mult))
        op("vector", lambda h: h.tensor_tensor(out=affine_out, in0=p.trash,
                                               in1=p.lnb_sb, op=ALU.add))
    mark("vector", CV)
    return CV.v


def _emit_fp4(S, x_ap, out_ap, F):
    p = S.p
    t = p.qt
    op, wait, mark = S.op, S.wait, S.mark
    CV, CS = p.CV, p.CS
    nb = F // 16
    x3 = x_ap.rearrange("p (b k) -> p b k", k=16)
    bm = t.bm[:, 0:nb]
    bm3 = bm.rearrange("p (b o) -> p b o", o=1)
    op("vector", lambda h: h.reduce_max(out=bm3, in_=x3, axis=AX.X,
                                        apply_absolute_value=True))
    op("vector", lambda h: h.reciprocal(out=t.rr[:, 0:nb], in_=bm))
    op("vector", lambda h: h.tensor_scalar(out=t.rr[:, 0:nb], in0=t.rr[:, 0:nb],
                                           scalar1=12.0, scalar2=2e12,
                                           op0=ALU.mult, op1=ALU.min), inc=(CV, 1))
    wait("vector", CV, CV.v)
    rrb = t.rr[:, 0:nb].unsqueeze(2).broadcast_to([128, nb, 16])
    w3 = t.w[:, 0:F].rearrange("p (b k) -> p b k", k=16)
    op("vector", lambda h: h.tensor_tensor(out=w3, in0=x3, in1=rrb, op=ALU.mult),
       inc=(CV, 1))
    wait("scalar", CV, CV.v)
    op("scalar", lambda h: h.activation(out=t.aw[:, 0:F], in_=t.w[:, 0:F],
                                        func=AF.Abs))
    for rnd in range(4):
        js = [2 * rnd, 2 * rnd + 1] if rnd < 3 else [6]
        for i, j in enumerate(js):
            last = (i == len(js) - 1) and rnd != 3
            op("scalar", lambda h, i=i, j=j: h.activation(
                out=t.sg[i][:, 0:F], in_=t.aw[:, 0:F], func=AF.Sign, bias=-FP4_TH[j]),
               inc=(CS, 1) if last else None)
        if rnd == 3:
            op("scalar", lambda h: h.activation(out=t.sgn[:, 0:F], in_=t.w[:, 0:F],
                                                func=AF.Sign), inc=(CS, 1))
        wait("vector", CS, CS.v)
        if rnd == 0:
            op("vector", lambda h: h.tensor_tensor(out=t.acc1[:, 0:F], in0=t.sg[0][:, 0:F],
                                                   in1=t.sg[1][:, 0:F], op=ALU.add))
        elif rnd == 1:
            op("vector", lambda h: h.tensor_tensor(out=t.tmpb[:, 0:F], in0=t.sg[0][:, 0:F],
                                                   in1=t.sg[1][:, 0:F], op=ALU.add))
            op("vector", lambda h: h.tensor_tensor(out=t.acc1[:, 0:F], in0=t.acc1[:, 0:F],
                                                   in1=t.tmpb[:, 0:F], op=ALU.add))
        elif rnd == 2:
            op("vector", lambda h: h.tensor_tensor(out=t.acc2[:, 0:F], in0=t.sg[0][:, 0:F],
                                                   in1=t.sg[1][:, 0:F], op=ALU.add))
        else:
            op("vector", lambda h: h.scalar_tensor_tensor(
                out=t.tmpb[:, 0:F], in0=t.acc1[:, 0:F], scalar=0.25,
                in1=t.sg[0][:, 0:F], op0=ALU.mult, op1=ALU.add))
            op("vector", lambda h: h.scalar_tensor_tensor(
                out=t.tmpb[:, 0:F], in0=t.acc2[:, 0:F], scalar=0.5,
                in1=t.tmpb[:, 0:F], op0=ALU.mult, op1=ALU.add))
        mark("vector", CV)
        if rnd < 3:
            wait("scalar", CV, CV.v)
    op("vector", lambda h: h.tensor_scalar(out=bm, in0=bm, scalar1=1.0 / 6.0,
                                           scalar2=1e-12, op0=ALU.mult, op1=ALU.max))
    scb = bm.unsqueeze(2).broadcast_to([128, nb, 16])
    sg3 = t.sgn[:, 0:F].rearrange("p (b k) -> p b k", k=16)
    ss3 = t.aw[:, 0:F].rearrange("p (b k) -> p b k", k=16)
    op("vector", lambda h: h.tensor_tensor(out=ss3, in0=sg3, in1=scb, op=ALU.mult))
    op("vector", lambda h: h.scalar_tensor_tensor(out=out_ap, in0=t.tmpb[:, 0:F],
                                                  scalar=3.0, in1=t.aw[:, 0:F],
                                                  op0=ALU.add, op1=ALU.mult),
       inc=(CV, 1))
    return CV.v


def _emit_round_clip(S, v_ap, out_ap, F):
    p = S.p
    t = p.qt
    op, wait, mark = S.op, S.wait, S.mark
    CV, CS = p.CV, p.CS
    op("vector", lambda h: h.tensor_scalar(out=v_ap, in0=v_ap, scalar1=-128.0,
                                           scalar2=127.0, op0=ALU.max, op1=ALU.min),
       inc=(CV, 1))
    wait("scalar", CV, CV.v)
    op("scalar", lambda h: h.activation(out=t.y0[:, 0:F], in_=v_ap, func=AF.Abs))
    op("scalar", lambda h: h.activation(out=t.sgn[:, 0:F], in_=v_ap, func=AF.Sign),
       inc=(CS, 1))
    wait("vector", CS, CS.v)
    op("vector", lambda h: h.tensor_copy(out=t.tt[:, 0:F], in_=t.y0[:, 0:F]),
       inc=(CV, 1))
    for m in (128.0, 64.0, 32.0, 16.0, 8.0, 4.0, 2.0, 1.0):
        op("vector", lambda h, m=m: h.tensor_scalar(out=t.d[:, 0:F], in0=t.tt[:, 0:F],
                                                    scalar1=m - 0.5, scalar2=-m,
                                                    op0=ALU.is_ge, op1=ALU.mult))
        op("vector", lambda h: h.tensor_tensor(out=t.tt[:, 0:F], in0=t.tt[:, 0:F],
                                               in1=t.d[:, 0:F], op=ALU.add))
    op("vector", lambda h: h.tensor_tensor(out=t.y0[:, 0:F], in0=t.y0[:, 0:F],
                                           in1=t.tt[:, 0:F], op=ALU.subtract))
    op("vector", lambda h: h.tensor_tensor(out=out_ap, in0=t.y0[:, 0:F],
                                           in1=t.sgn[:, 0:F], op=ALU.mult),
       inc=(CV, 1))
    return CV.v


def _phase_A0(S):
    p = S.p
    op, dma, wait, mark = S.op, S.dma, S.wait, S.mark
    GW, CV, CS, CT = p.GW, p.CV, p.CS, p.CT
    n_sub = p.n_sub
    ev = {}

    for c in range(8):
        dma("gpsimd", GW, p.wkv_sb[:, c, :], p.wkv_h[128 * c:128 * (c + 1), :])
    wait("tensor", GW, GW.v)
    op("vector", lambda h: h.memset(p.S_sb.rearrange("p h t d -> p (h t d)"), 0.0),
       inc=(CV, 1))

    for s in range(n_sub):
        r = s % 2
        LD = p.LDs[r]
        ST = p.STs[r]
        if s >= 2:
            wait("sync", CV, ev[("krope", s - 2)])
            wait("sync", CS, ev[("vevac", s - 2)])
        dma("sync", LD, p.a0x[r], p.x_h[SUB * s:SUB * (s + 1), :])
        dma("sync", LD, p.ropec_t[r], p.ropec_h[SUB * s:SUB * (s + 1), :])
        dma("sync", LD, p.ropes_t[r], p.ropes_h[SUB * s:SUB * (s + 1), :])
        ev[("ld", s)] = LD.v

        wait("vector", LD, ev[("ld", s)])
        wait("scalar", LD, ev[("ld", s)])
        if s >= 2:
            wait("vector", ST, ev[("st_xn", s - 2)])
        _emit_ln(S, p.a0x[r], p.a0xn[r])
        ev[("xn", s)] = CV.v
        wait("gpsimd", CV, ev[("xn", s)])
        dma("gpsimd", ST, p.xn_s[SUB * s:SUB * (s + 1), :], p.a0xn[r])
        ev[("st_xn", s)] = ST.v
        if s >= 2:
            wait("vector", CT, ev[("nt", s - 2)])
        _emit_ln(S, p.a0xn[r], p.a0z, affine_out=p.a0nb[r])
        ev[("nb", s)] = CV.v

        nb3 = p.a0nb[r].rearrange("p (a b) -> p a b", b=128)
        nfm_f = p.normfm[r].rearrange("p a b -> p (a b)")
        wait("tensor", CV, ev[("nb", s)])
        if s >= 1:
            wait("tensor", CV, ev[("krope", s - 1)])
            wait("tensor", CS, ev[("ktmev", s - 1)])
        for c in range(8):
            op("tensor", lambda h, c=c: h.transpose(
                p.PSB[:, 128 * c:128 * (c + 1)], nb3[:, c, :], p.idb_sb))
        mark("tensor", CT)
        ev[("nt", s)] = CT.v
        wait("scalar", CT, ev[("nt", s)])
        if s >= 2:
            wait("scalar", ST, ev[("st_nf", s - 2)])
        op("scalar", lambda h: h.activation(out=nfm_f, in_=p.PSB[:, 0:1024],
                                            func=AF.Copy), inc=(CS, 1))
        ev[("nev", s)] = CS.v
        wait("gpsimd", CS, ev[("nev", s)])
        dma("gpsimd", ST, p.normed_f_s[:, D * s:D * (s + 1)], nfm_f)
        ev[("st_nf", s)] = ST.v

        wait("tensor", CS, ev[("nev", s)])
        if s >= 1:
            wait("tensor", CS, ev[("vevac", s - 1)])
        for c in range(8):
            op("tensor", lambda h, c=c: h.ldweights(p.normfm[r][:, c, :]))
            for j in range(6):
                op("tensor", lambda h, c=c, j=j: h.matmul(
                    p.PS[:, 512 * j:512 * (j + 1)], p.normfm[r][:, c, :],
                    p.wkv_sb[:, c, 512 * j:512 * (j + 1)],
                    start=(c == 0), stop=(c == 7)))
        mark("tensor", CT)
        ev[("proj", s)] = CT.v

        wait("vector", CT, ev[("proj", s)])
        if s >= 2:
            wait("vector", ST, ev[("st_k", s - 2)])
        kps = p.PS[:, 0:1024].rearrange("p (h d) -> p h d", h=H0)
        for h0 in range(H0):
            k1, k2 = kps[:, h0, 0:128], kps[:, h0, 128:256]
            o1 = p.a0ktm[r][:, h0, 0:128]
            o2 = p.a0ktm[r][:, h0, 128:256]
            op("vector", lambda h, k1=k1: h.tensor_tensor(out=p.tmp1, in0=k1, in1=p.ropec_t[r], op=ALU.mult))
            op("vector", lambda h, k2=k2: h.tensor_tensor(out=p.tmp2, in0=k2, in1=p.ropes_t[r], op=ALU.mult))
            op("vector", lambda h, o1=o1: h.tensor_tensor(out=o1, in0=p.tmp1, in1=p.tmp2, op=ALU.subtract))
            op("vector", lambda h, k2=k2: h.tensor_tensor(out=p.tmp1, in0=k2, in1=p.ropec_t[r], op=ALU.mult))
            op("vector", lambda h, k1=k1: h.tensor_tensor(out=p.tmp2, in0=k1, in1=p.ropes_t[r], op=ALU.mult))
            op("vector", lambda h, o2=o2: h.tensor_tensor(out=o2, in0=p.tmp1, in1=p.tmp2, op=ALU.add))
        mark("vector", CV)
        ev[("krope", s)] = CV.v
        for h0 in range(H0):
            op("vector", lambda h, h0=h0: h.tensor_scalar(
                out=p.a0ktil[:, h0, :], in0=p.a0ktm[r][:, h0, :],
                scalar1=p.kdect_sb[:, h0:h0 + 1], scalar2=None, op0=ALU.mult))
        mark("vector", CV)
        ev[("ktil", s)] = CV.v
        wait("gpsimd", CV, ev[("krope", s)])
        dma("gpsimd", ST, p.ktm_s[SUB * s:SUB * (s + 1), :],
            p.a0ktm[r].rearrange("p h d -> p (h d)"))
        ev[("st_k", s)] = ST.v

        kt3 = p.a0ktm[r].rearrange("p h (c b) -> p (h c) b", b=128)
        kfm_f = p.a1kfm[r].rearrange("p a b -> p (a b)")
        wait("tensor", CV, ev[("krope", s)])
        wait("tensor", CS, ev[("nev", s)])
        for c in range(8):
            op("tensor", lambda h, c=c: h.transpose(
                p.PSB[:, 128 * c:128 * (c + 1)], kt3[:, c, :], p.idb_sb))
        mark("tensor", CT)
        ev[("ktt", s)] = CT.v
        wait("scalar", CT, ev[("ktt", s)])
        if s >= 2:
            wait("scalar", ST, ev[("st_kf", s - 2)])
        op("scalar", lambda h: h.activation(out=kfm_f, in_=p.PSB[:, 0:1024],
                                            func=AF.Copy), inc=(CS, 1))
        ev[("ktmev", s)] = CS.v
        wait("gpsimd", CS, ev[("ktmev", s)])
        dma("gpsimd", ST, p.ktm_f_s[:, D * s:D * (s + 1)], kfm_f)
        ev[("st_kf", s)] = ST.v

        wait("scalar", CT, ev[("proj", s)])
        if s >= 2:
            wait("scalar", ST, ev[("st_v", s - 2)])
        op("scalar", lambda h: h.activation(out=p.a0vtm[r], in_=p.PS[:, 1024:3072],
                                            func=AF.Copy), inc=(CS, 1))
        ev[("vevac", s)] = CS.v
        wait("gpsimd", CS, ev[("vevac", s)])
        dma("gpsimd", ST, p.vtm_s[SUB * s:SUB * (s + 1), :], p.a0vtm[r])
        ev[("st_v", s)] = ST.v

        wait("tensor", CV, ev[("ktil", s)])
        wait("tensor", CS, ev[("vevac", s)])
        for h0 in range(H0):
            if s + h0 > 0:
                prev = ev[("sst", s, h0 - 1)] if h0 > 0 else ev[("sst", s - 1, H0 - 1)]
                wait("tensor", CV, prev)
            for half in range(2):
                op("tensor", lambda h, h0=h0, half=half: h.ldweights(
                    p.a0ktil[:, h0, 128 * half:128 * (half + 1)]))
                op("tensor", lambda h, h0=h0, half=half: h.matmul(
                    p.PS[:, 3072 + 512 * half:3072 + 512 * (half + 1)],
                    p.a0ktil[:, h0, 128 * half:128 * (half + 1)],
                    p.a0vtm[r][:, 512 * h0:512 * (h0 + 1)],
                    start=True, stop=True))
            mark("tensor", CT)
            ev[("Y", s, h0)] = CT.v
            wait("vector", CT, ev[("Y", s, h0)])
            for half in range(2):
                op("vector", lambda h, h0=h0, half=half: h.scalar_tensor_tensor(
                    out=p.S_sb[:, h0, half, :], in0=p.S_sb[:, h0, half, :],
                    scalar=GC[h0], in1=p.PS[:, 3072 + 512 * half:3072 + 512 * (half + 1)],
                    op0=ALU.mult, op1=ALU.add))
            mark("vector", CV)
            ev[("sst", s, h0)] = CV.v

    p.ev_a0 = ev


def _x1_state_exchange(S):
    p = S.p
    op, dma, wait, mark = S.op, S.dma, S.wait, S.mark
    GW, CV, CS, CT, CP = p.GW, p.CV, p.CS, p.CT, p.CP
    xchf = p.xch.rearrange("p a h t d -> p (a h t d)")
    sflat = p.S_sb.rearrange("p h t d -> p (h t d)")
    for ps in range(3):
        for h0 in range(H0):
            op("vector", lambda h, ps=ps, h0=h0: h.tensor_scalar(
                out=p.xch[:, ps, h0, :, :], in0=p.S_sb[:, h0, :, :],
                scalar1=p.sxw_sb[:, ps * H0 + h0:ps * H0 + h0 + 1],
                scalar2=None, op0=ALU.mult))
    mark("vector", CV)
    wait("gpsimd", CV, CV.v)
    dma("gpsimd", p.STs[0], p.ccs_in, xchf)
    wait("gpsimd", p.STs[0], p.STs[0].v)
    wait("gpsimd", p.STs[1], p.STs[1].v)
    op("gpsimd", lambda h: h.collective_compute(
        "AllReduce", ALU.add, replica_groups=[[0, 1, 2, 3], [4, 5, 6, 7]],
        ins=[p.ccs_in.opt()], outs=[p.ccs_out.opt()]), inc=(CP, 1))
    wait("gpsimd", CP, CP.v)
    dma("gpsimd", GW, xchf, p.ccs_out)
    wait("vector", GW, GW.v)
    op("vector", lambda h: h.tensor_scalar(
        out=sflat, in0=p.xch[:, 0, :, :, :].rearrange("p h t d -> p (h t d)"),
        scalar1=p.sxm_sb[:, 0:1], scalar2=None, op0=ALU.mult))
    for ps in (1, 2):
        op("vector", lambda h, ps=ps: h.scalar_tensor_tensor(
            out=sflat, in0=p.xch[:, ps, :, :, :].rearrange("p h t d -> p (h t d)"),
            scalar=p.sxm_sb[:, ps:ps + 1], in1=sflat, op0=ALU.mult, op1=ALU.add))
    mark("vector", CV)
    p.ev_sinit = CV.v


def _phase_A1(S):
    p = S.p
    op, dma, wait, mark = S.op, S.dma, S.wait, S.mark
    GW, CV, CS, CT = p.GW, p.CV, p.CS, p.CT
    n_sub = p.n_sub
    ev = {}

    wait("gpsimd", CT, p.ev_a0[("proj", n_sub - 1)])
    wait("gpsimd", CV, p.ev_sinit)
    for c in range(8):
        dma("gpsimd", GW, p.wqg_sb[:, c, :], p.wqg_h[128 * c:128 * (c + 1), :])
    for c in range(16):
        dma("gpsimd", GW, p.wo0_sb[:, c, :], p.wo0_h[128 * c:128 * (c + 1), :])
    wait("tensor", GW, GW.v)

    for c in (p.LDs[0], p.LDs[1], p.GW, p.STs[0], p.STs[1], p.CV, p.CS, p.CT):
        wait("sync", c, c.v)

    for s in range(n_sub):
        r = s % 2
        LD = p.LDs[r]
        ST = p.STs[r]
        if s >= 2:
            wait("sync", CV, ev[("con", s - 2)])
        dma("sync", LD, p.a1nfm[r].rearrange("p a b -> p (a b)"),
            p.normed_f_s[:, D * s:D * (s + 1)])
        dma("sync", LD, p.a1kfm[r].rearrange("p a b -> p (a b)"),
            p.ktm_f_s[:, D * s:D * (s + 1)])
        dma("sync", LD, p.a1ktm[r], p.ktm_s[SUB * s:SUB * (s + 1), :])
        dma("sync", LD, p.a1vtm[r], p.vtm_s[SUB * s:SUB * (s + 1), :])
        dma("sync", LD, p.a1xn[r], p.xn_s[SUB * s:SUB * (s + 1), :])
        dma("sync", LD, p.ropec_t[r], p.ropecq_h[SUB * s:SUB * (s + 1), :])
        dma("sync", LD, p.ropes_t[r], p.ropesq_h[SUB * s:SUB * (s + 1), :])
        ev[("ld", s)] = LD.v

        wait("tensor", LD, ev[("ld", s)])
        if s >= 1:
            wait("tensor", CV, ev[("con", s - 1)])
            wait("tensor", CS, ev[("gsilu", s - 1)])
        if s == 0:
            wait("tensor", CV, p.ev_sinit)
            wait("tensor", CS, p.ev_a0[("vevac", n_sub - 1)])
        for c in range(8):
            op("tensor", lambda h, c=c: h.ldweights(p.a1nfm[r][:, c, :]))
            for j in range(6):
                op("tensor", lambda h, c=c, j=j: h.matmul(
                    p.PS[:, 512 * j:512 * (j + 1)], p.a1nfm[r][:, c, :],
                    p.wqg_sb[:, c, 512 * j:512 * (j + 1)],
                    start=(c == 0), stop=(c == 7)))
        mark("tensor", CT)
        ev[("qg", s)] = CT.v

        wait("vector", CT, ev[("qg", s)])
        wait("vector", LD, ev[("ld", s)])
        qps = p.PS[:, 0:1024].rearrange("p (h d) -> p h d", h=H0)
        for h0 in range(H0):
            q1, q2 = qps[:, h0, 0:128], qps[:, h0, 128:256]
            o1 = p.a1qtm[:, h0, 0:128]
            o2 = p.a1qtm[:, h0, 128:256]
            op("vector", lambda h, q1=q1: h.tensor_tensor(out=p.tmp1, in0=q1, in1=p.ropec_t[r], op=ALU.mult))
            op("vector", lambda h, q2=q2: h.tensor_tensor(out=p.tmp2, in0=q2, in1=p.ropes_t[r], op=ALU.mult))
            op("vector", lambda h, o1=o1: h.tensor_tensor(out=o1, in0=p.tmp1, in1=p.tmp2, op=ALU.subtract))
            op("vector", lambda h, q2=q2: h.tensor_tensor(out=p.tmp1, in0=q2, in1=p.ropec_t[r], op=ALU.mult))
            op("vector", lambda h, q1=q1: h.tensor_tensor(out=p.tmp2, in0=q1, in1=p.ropes_t[r], op=ALU.mult))
            op("vector", lambda h, o2=o2: h.tensor_tensor(out=o2, in0=p.tmp1, in1=p.tmp2, op=ALU.add))
        for h0 in range(H0):
            op("vector", lambda h, h0=h0: h.tensor_scalar(
                out=p.a1qctm[:, h0, :], in0=p.a1qtm[:, h0, :],
                scalar1=p.crosst_sb[:, h0:h0 + 1], scalar2=None, op0=ALU.mult))
        mark("vector", CV)
        ev[("qrope", s)] = CV.v

        wait("scalar", CT, ev[("qg", s)])
        op("scalar", lambda h: h.activation(out=p.a1gs, in_=p.PS[:, 1024:3072],
                                            func=AF.Silu), inc=(CS, 1))
        ev[("gsilu", s)] = CS.v

        qt3 = p.a1qtm.rearrange("p h (c b) -> p (h c) b", b=128)
        qct3 = p.a1qctm.rearrange("p h (c b) -> p (h c) b", b=128)
        qfm_f = p.a1qfm.rearrange("p a b -> p (a b)")
        qcfm_f = p.a1qcfm.rearrange("p a b -> p (a b)")
        wait("tensor", CV, ev[("qrope", s)])
        if s >= 1:
            wait("tensor", CV, ev[("og", s - 1, H0 - 1)])
            wait("tensor", CS, ev[("qev", s - 1)])
        for c in range(8):
            op("tensor", lambda h, c=c: h.transpose(
                p.PSB[:, 6144 + 128 * c:6144 + 128 * (c + 1)], qt3[:, c, :],
                p.idb_sb))
        for c in range(8):
            op("tensor", lambda h, c=c: h.transpose(
                p.PSB[:, 7168 + 128 * c:7168 + 128 * (c + 1)], qct3[:, c, :],
                p.idb_sb))
        mark("tensor", CT)
        ev[("qt", s)] = CT.v
        wait("scalar", CT, ev[("qt", s)])
        op("scalar", lambda h: h.activation(out=qfm_f, in_=p.PSB[:, 6144:7168],
                                            func=AF.Copy))
        op("scalar", lambda h: h.activation(out=qcfm_f, in_=p.PSB[:, 7168:8192],
                                            func=AF.Copy), inc=(CS, 1))
        ev[("qev", s)] = CS.v

        wait("vector", LD, ev[("ld", s)])
        ktmv = p.a1ktm[r].rearrange("p (hh d) -> p hh d", hh=H0)
        for h0 in range(H0):
            op("vector", lambda h, h0=h0: h.tensor_scalar(
                out=p.a0ktil[:, h0, :], in0=ktmv[:, h0, :],
                scalar1=p.kdect_sb[:, h0:h0 + 1], scalar2=None, op0=ALU.mult))
        mark("vector", CV)
        ev[("ktil", s)] = CV.v

        wait("tensor", CS, ev[("qev", s)])
        for h0 in range(H0):
            for half in range(2):
                c = 2 * h0 + half
                op("tensor", lambda h, c=c: h.ldweights(p.a1kfm[r][:, c, :]))
                op("tensor", lambda h, c=c, h0=h0, half=half: h.matmul(
                    p.PS[:, 3072 + 128 * h0:3072 + 128 * (h0 + 1)],
                    p.a1kfm[r][:, c, :], p.a1qfm[:, c, :],
                    start=(half == 0), stop=(half == 1)))
            mark("tensor", CT)
            ev[("attnT", s, h0)] = CT.v
            wait("vector", CT, ev[("attnT", s, h0)])
            op("vector", lambda h, h0=h0: h.tensor_tensor(
                out=p.a1at[:, h0, :], in0=p.PS[:, 3072 + 128 * h0:3072 + 128 * (h0 + 1)],
                in1=p.dmask_sb[:, 128 * h0:128 * (h0 + 1)], op=ALU.mult), inc=(CV, 1))
            ev[("atm", s, h0)] = CV.v
            wait("tensor", CV, ev[("atm", s, h0)])
            if h0 > 0:
                wait("tensor", CV, ev[("og", s, h0 - 1)])
            op("tensor", lambda h, h0=h0: h.ldweights(p.a1at[:, h0, :]))
            op("tensor", lambda h, h0=h0: h.matmul(
                p.PS[:, 3584:4096], p.a1at[:, h0, :],
                p.a1vtm[r][:, 512 * h0:512 * (h0 + 1)], start=True, stop=False))
            for half in range(2):
                c = 2 * h0 + half
                op("tensor", lambda h, c=c: h.ldweights(p.a1qcfm[:, c, :]))
                op("tensor", lambda h, c=c, h0=h0, half=half: h.matmul(
                    p.PS[:, 3584:4096], p.a1qcfm[:, c, :],
                    p.S_sb[:, h0, half, :], start=False, stop=(half == 1)))
            mark("tensor", CT)
            ev[("o", s, h0)] = CT.v
            wait("scalar", CT, ev[("o", s, h0)])
            op("scalar", lambda h: h.activation(out=p.trash[:, 0:512], in_=p.PS[:, 3584:4096],
                                                func=AF.Square, accum_out=p.sums[:, 2:3]),
               inc=(CS, 1))
            wait("scalar", CS, CS.v)
            op("scalar", lambda h: h.activation(out=p.mu2, in_=p.sums[:, 2:3],
                                                func=AF.Sqrt, scale=1.0 / 512.0,
                                                bias=EPS0), inc=(CS, 1))
            ev[("osd", s, h0)] = CS.v
            wait("vector", CS, ev[("osd", s, h0)])
            wait("vector", CS, ev[("gsilu", s)])
            op("vector", lambda h: h.reciprocal(out=p.mu2, in_=p.mu2), inc=(CV, 1))
            wait("vector", CV, CV.v)
            op("vector", lambda h: h.scalar_tensor_tensor(
                out=p.tmp3, in0=p.PS[:, 3584:4096], scalar=p.mu2,
                in1=p.gnw_sb, op0=ALU.mult, op1=ALU.mult))
            op("vector", lambda h, h0=h0: h.tensor_tensor(
                out=p.a1og[:, h0, :], in0=p.tmp3,
                in1=p.a1gs[:, 512 * h0:512 * (h0 + 1)], op=ALU.mult), inc=(CV, 1))
            ev[("og", s, h0)] = CV.v

        wait("tensor", CV, ev[("ktil", s)])
        for h0 in range(H0):
            wait("tensor", CV, ev[("og", s, h0)])
            if h0 > 0:
                wait("tensor", CV, ev[("sst", s, h0 - 1)])
            elif s >= 1:
                wait("tensor", CV, ev[("sst", s - 1, H0 - 1)])
            for half in range(2):
                op("tensor", lambda h, h0=h0, half=half: h.ldweights(
                    p.a0ktil[:, h0, 128 * half:128 * (half + 1)]))
                op("tensor", lambda h, h0=h0, half=half: h.matmul(
                    p.PS[:, 512 * half:512 * (half + 1)],
                    p.a0ktil[:, h0, 128 * half:128 * (half + 1)],
                    p.a1vtm[r][:, 512 * h0:512 * (h0 + 1)], start=True, stop=True))
            mark("tensor", CT)
            ev[("Y", s, h0)] = CT.v
            wait("vector", CT, ev[("Y", s, h0)])
            for half in range(2):
                op("vector", lambda h, h0=h0, half=half: h.scalar_tensor_tensor(
                    out=p.S_sb[:, h0, half, :], in0=p.S_sb[:, h0, half, :],
                    scalar=GC[h0], in1=p.PS[:, 512 * half:512 * (half + 1)],
                    op0=ALU.mult, op1=ALU.add))
            mark("vector", CV)
            ev[("sst", s, h0)] = CV.v

        og3 = p.a1og.rearrange("p h (c b) -> p (h c) b", b=128)
        ogfm_f = p.a1ogfm.rearrange("p a b -> p (a b)")
        wait("tensor", CV, ev[("og", s, H0 - 1)])
        wait("tensor", CV, ev[("sst", s, H0 - 1)])
        if s >= 1:
            wait("tensor", CS, ev[("ogev", s - 1)])
        for c in range(16):
            op("tensor", lambda h, c=c: h.transpose(
                p.PSB[:, 128 * c:128 * (c + 1)], og3[:, c, :], p.idb_sb))
        mark("tensor", CT)
        ev[("ogt", s)] = CT.v
        wait("scalar", CT, ev[("ogt", s)])
        op("scalar", lambda h: h.activation(out=ogfm_f, in_=p.PSB[:, 0:2048],
                                            func=AF.Copy), inc=(CS, 1))
        ev[("ogev", s)] = CS.v
        wait("tensor", CS, ev[("ogev", s)])
        if s >= 1:
            wait("tensor", CV, ev[("s0", s - 1)])
        for c in range(16):
            op("tensor", lambda h, c=c: h.ldweights(p.a1ogfm[:, c, :]))
            for j in range(2):
                op("tensor", lambda h, c=c, j=j: h.matmul(
                    p.PS[:, 1024 + 512 * j:1024 + 512 * (j + 1)],
                    p.a1ogfm[:, c, :], p.wo0_sb[:, c, 512 * j:512 * (j + 1)],
                    start=(c == 0), stop=(c == 15)))
        mark("tensor", CT)
        ev[("wo", s)] = CT.v
        wait("vector", CT, ev[("wo", s)])
        if s >= 2:
            wait("vector", ST, ev[("st_s0", s - 2)])
        op("vector", lambda h: h.tensor_tensor(out=p.a1s0[r], in0=p.PS[:, 1024:2048],
                                               in1=p.a1xn[r], op=ALU.add), inc=(CV, 1))
        ev[("s0", s)] = CV.v
        wait("gpsimd", CV, ev[("s0", s)])
        dma("gpsimd", ST, p.s0_s[SUB * s:SUB * (s + 1), :], p.a1s0[r])
        ev[("st_s0", s)] = ST.v
        mark("vector", CV)
        ev[("con", s)] = CV.v

    p.ev_a1 = ev


def _fence(S):
    p = S.p
    for e in ("sync", "gpsimd", "vector", "scalar", "tensor"):
        for c in (p.LDs[0], p.LDs[1], p.GW, p.STs[0], p.STs[1],
                  p.CV, p.CS, p.CT, p.CP):
            S.wait(e, c, c.v)


def _wquant_fp4(S, w_h, wfm_sb, nrow, ncol):
    p = S.p
    t = p.qt
    op, dma, wait, mark = S.op, S.dma, S.wait, S.mark
    CV, CT = p.CV, p.CT
    guard = [0, 0]
    nblk = ncol // 128
    wqb3 = t.wqb[:, 0:ncol].rearrange("p (a b) -> p a b", b=128)
    psv = p.PSB[:, 0:ncol].rearrange("p (a b) -> p a b", b=128)
    evv = 0
    wait("sync", CV, CV.v)
    for i in range(nrow // 128):
        r = i % 2
        LD = p.LDs[r]
        if i >= 2:
            wait("sync", CV, guard[r])
        dma("sync", LD, t.wslab[r][:, 0:ncol], w_h[128 * i:128 * (i + 1), :])
        ldv = LD.v
        wait("vector", LD, ldv)
        if i >= 1:
            wait("vector", CT, trv)
        cvs = []
        for (a, wdt) in _islices(ncol, 1408):
            cvs.append(_emit_fp4(S, t.wslab[r][:, a:a + wdt], t.wqb[:, a:a + wdt], wdt))
        wait("tensor", CV, cvs[-1])
        if i >= 1:
            wait("tensor", CV, evv)
        for c in range(nblk):
            op("tensor", lambda h, c=c: h.transpose(
                p.PSB[:, 128 * c:128 * (c + 1)], wqb3[:, c, :], p.idb_sb))
        mark("tensor", CT)
        trv = CT.v
        wait("vector", CT, trv)
        op("vector", lambda h, i=i: h.tensor_copy(
            out=wfm_sb[:, :, 128 * i:128 * (i + 1)], in_=psv), inc=(CV, 1))
        evv = CV.v
        guard[r] = CV.v
    return evv


def _wquant_int8(S, w_h, wfm_sb, nrow, ncol):
    p = S.p
    t = p.qt
    op, dma, wait, mark = S.op, S.dma, S.wait, S.mark
    CV, CT = p.CV, p.CT
    guard = [0, 0]
    nblk = ncol // 128
    wqb3 = t.wqb[:, 0:ncol].rearrange("p (a b) -> p a b", b=128)
    psv = p.PSB[:, 0:ncol].rearrange("p (a b) -> p a b", b=128)
    evv = 0
    trv = 0
    wait("sync", CV, CV.v)
    for i in range(nrow // 128):
        r = i % 2
        LD = p.LDs[r]
        if i >= 2:
            wait("sync", CV, guard[r])
        dma("sync", LD, t.wslab[r][:, 0:ncol], w_h[128 * i:128 * (i + 1), :])
        ldv = LD.v
        wait("vector", LD, ldv)
        if i >= 1:
            wait("vector", CT, trv)
        op("vector", lambda h: h.reduce_max(out=p.rm, in_=t.wslab[r][:, 0:ncol],
                                            axis=AX.X, apply_absolute_value=True))
        op("vector", lambda h: h.reciprocal(out=p.rr1, in_=p.rm))
        op("vector", lambda h: h.tensor_scalar(out=p.rr1, in0=p.rr1, scalar1=127.0,
                                               scalar2=None, op0=ALU.mult))
        op("vector", lambda h: h.tensor_scalar(out=p.rm, in0=p.rm, scalar1=1.0 / 127.0,
                                               scalar2=None, op0=ALU.mult), inc=(CV, 1))
        wait("vector", CV, CV.v)
        for (a, wdt) in _islices(ncol, 1408):
            op("vector", lambda h, a=a, wdt=wdt: h.tensor_scalar(
                out=t.w[:, 0:wdt], in0=t.wslab[r][:, a:a + wdt],
                scalar1=p.rr1, scalar2=None, op0=ALU.mult), inc=(CV, 1))
            _emit_round_clip(S, t.w[:, 0:wdt], t.y0[:, 0:wdt], wdt)
            op("vector", lambda h, a=a, wdt=wdt: h.tensor_scalar(
                out=t.wqb[:, a:a + wdt], in0=t.y0[:, 0:wdt],
                scalar1=p.rm, scalar2=None, op0=ALU.mult), inc=(CV, 1))
        cv = CV.v
        wait("tensor", CV, cv)
        if i >= 1:
            wait("tensor", CV, evv)
        for c in range(nblk):
            op("tensor", lambda h, c=c: h.transpose(
                p.PSB[:, 128 * c:128 * (c + 1)], wqb3[:, c, :], p.idb_sb))
        mark("tensor", CT)
        trv = CT.v
        wait("vector", CT, trv)
        op("vector", lambda h, i=i: h.tensor_copy(
            out=wfm_sb[:, :, 128 * i:128 * (i + 1)], in_=psv), inc=(CV, 1))
        evv = CV.v
        guard[r] = CV.v
    return evv


def _phase_B1(S):
    p = S.p
    op, dma, wait, mark = S.op, S.dma, S.wait, S.mark
    GW, CV, CS, CT = p.GW, p.CV, p.CS, p.CT
    n_sub = p.n_sub
    ev = {}
    _fence(S)
    _wquant_fp4(S, p.wg0_h, p.wg_fm, I0, D)
    wq_ev = _wquant_fp4(S, p.wu0_h, p.wu_fm, I0, D)
    wait("tensor", CV, wq_ev)
    wait("tensor", p.LDs[0], p.LDs[0].v)
    wait("tensor", p.LDs[1], p.LDs[1].v)
    wait("sync", CV, wq_ev)
    wait("sync", p.LDs[0], p.LDs[0].v)
    wait("sync", p.LDs[1], p.LDs[1].v)

    for s in range(n_sub):
        r = s % 2
        LD = p.LDs[r]
        ST = p.STs[r]
        if s >= 1:
            wait("sync", CV, ev[("xq", s - 1)])
        dma("sync", LD, p.bxn[r], p.xn_s[SUB * s:SUB * (s + 1), :])
        ev[("ld", s)] = LD.v
        wait("vector", LD, ev[("ld", s)])
        wait("scalar", LD, ev[("ld", s)])
        cv = _emit_fp4(S, p.bxn[r], p.bxq[:, 0:D], D)
        ev[("xq", s)] = cv
        bxq3 = p.bxq[:, 0:D].rearrange("p (a b) -> p a b", b=128)
        bxqfm_f = p.bxqfm.rearrange("p a b -> p (a b)")
        wait("tensor", CV, cv)
        if s >= 1:
            wait("tensor", CS, ev[("xqev", s - 1)])
            wait("tensor", CV, ev[("hq", s - 1)])
        for c in range(8):
            op("tensor", lambda h, c=c: h.transpose(
                p.PSB[:, 7168 + 128 * c:7168 + 128 * (c + 1)], bxq3[:, c, :],
                p.idb_sb))
        mark("tensor", CT)
        ev[("xqt", s)] = CT.v
        wait("scalar", CT, ev[("xqt", s)])
        op("scalar", lambda h: h.activation(out=bxqfm_f, in_=p.PSB[:, 7168:8192],
                                            func=AF.Copy), inc=(CS, 1))
        ev[("xqev", s)] = CS.v
        wait("tensor", CS, ev[("xqev", s)])
        for c in range(8):
            op("tensor", lambda h, c=c: h.ldweights(p.bxqfm[:, c, :]))
            for (a, wdt) in _islices(I0, 512):
                op("tensor", lambda h, c=c, a=a, wdt=wdt: h.matmul(
                    p.PS[:, a:a + wdt], p.bxqfm[:, c, :], p.wg_fm[:, c, a:a + wdt],
                    start=(c == 0), stop=(c == 7)))
        mark("tensor", CT)
        ev[("gate", s)] = CT.v
        wait("scalar", CT, ev[("gate", s)])
        op("scalar", lambda h: h.activation(out=p.bgs, in_=p.PS[:, 0:I0],
                                            func=AF.Silu), inc=(CS, 1))
        ev[("gs", s)] = CS.v
        wait("tensor", CS, ev[("gs", s)])
        for c in range(8):
            op("tensor", lambda h, c=c: h.ldweights(p.bxqfm[:, c, :]))
            for (a, wdt) in _islices(I0, 512):
                op("tensor", lambda h, c=c, a=a, wdt=wdt: h.matmul(
                    p.PS[:, a:a + wdt], p.bxqfm[:, c, :], p.wu_fm[:, c, a:a + wdt],
                    start=(c == 0), stop=(c == 7)))
        mark("tensor", CT)
        ev[("up", s)] = CT.v
        wait("vector", CT, ev[("up", s)])
        op("vector", lambda h: h.tensor_tensor(out=p.bgs, in0=p.PS[:, 0:I0],
                                               in1=p.bgs, op=ALU.mult), inc=(CV, 1))
        ev[("h", s)] = CV.v
        for (a, wdt) in _islices(I0, 1408):
            cv = _emit_fp4(S, p.bgs[:, a:a + wdt], p.bhq[:, a:a + wdt], wdt)
        ev[("hq", s)] = cv
        bhq3 = p.bhq[:, 0:I0].rearrange("p (a b) -> p a b", b=128)
        bhqfm_f = p.bhqfm.rearrange("p a b -> p (a b)")
        wait("tensor", CV, ev[("hq", s)])
        if s >= 1:
            wait("tensor", p.STs[1 - r], ev[("st_hq", s - 1)])
            wait("tensor", CS, ev[("hqev", s - 1, 1)])
        for w in range(2):
            if w == 1:
                wait("tensor", CS, ev[("hqev", s, 0)])
            for c in range(11):
                op("tensor", lambda h, w=w, c=c: h.transpose(
                    p.PSB[:, 5632 + 128 * c:5632 + 128 * (c + 1)],
                    bhq3[:, 11 * w + c, :], p.idb_sb))
            mark("tensor", CT)
            ev[("hqt", s, w)] = CT.v
            wait("scalar", CT, ev[("hqt", s, w)])
            op("scalar", lambda h, w=w: h.activation(
                out=bhqfm_f[:, 1408 * w:1408 * (w + 1)], in_=p.PSB[:, 5632:7040],
                func=AF.Copy), inc=(CS, 1))
            ev[("hqev", s, w)] = CS.v
        wait("gpsimd", CS, ev[("hqev", s, 1)])
        dma("gpsimd", ST, p.hq0f_s[:, I0 * s:I0 * (s + 1)], bhqfm_f)
        ev[("st_hq", s)] = ST.v

    p.ev_b1 = ev


def _phase_B2(S):
    p = S.p
    op, dma, wait, mark = S.op, S.dma, S.wait, S.mark
    GW, CV, CS, CT = p.GW, p.CV, p.CS, p.CT
    n_sub = p.n_sub
    ev = {}
    _fence(S)
    wq_ev = _wquant_fp4(S, p.wd0_h, p.wd_fm, D, I0)
    wait("tensor", CV, wq_ev)
    wait("tensor", p.LDs[0], p.LDs[0].v)
    wait("tensor", p.LDs[1], p.LDs[1].v)
    wait("sync", CV, wq_ev)
    wait("sync", p.LDs[0], p.LDs[0].v)
    wait("sync", p.LDs[1], p.LDs[1].v)

    for s in range(n_sub):
        r = s % 2
        LD = p.LDs[r]
        ST = p.STs[r]
        hb = [p.bhqfm, p.bhqfm_b][r]
        if s >= 2:
            wait("sync", CV, ev[("x1", s - 2)])
            wait("sync", CT, ev[("down", s - 2)])
        dma("sync", LD, hb.rearrange("p a b -> p (a b)"),
            p.hq0f_s[:, I0 * s:I0 * (s + 1)])
        dma("sync", LD, p.b2x[r], p.x_h[SUB * s:SUB * (s + 1), :])
        dma("sync", LD, p.b2s0[r], p.s0_s[SUB * s:SUB * (s + 1), :])
        ev[("ld", s)] = LD.v
        wait("tensor", LD, ev[("ld", s)])
        if s >= 1:
            wait("tensor", CS, ev[("g0", s - 1)])
        for c in range(22):
            op("tensor", lambda h, hb=hb, c=c: h.ldweights(hb[:, c, :]))
            for j in range(2):
                op("tensor", lambda h, hb=hb, c=c, j=j: h.matmul(
                    p.PS[:, 512 * j:512 * (j + 1)], hb[:, c, :],
                    p.wd_fm[:, c, 512 * j:512 * (j + 1)],
                    start=(c == 0), stop=(c == 21)))
        mark("tensor", CT)
        ev[("down", s)] = CT.v
        wait("scalar", CT, ev[("down", s)])
        if s >= 1:
            wait("scalar", CV, ev[("x1", s - 1)])
        op("scalar", lambda h: h.activation(out=p.bg0, in_=p.PS[:, 0:1024],
                                            func=AF.Silu), inc=(CS, 1))
        ev[("g0", s)] = CS.v
        wait("vector", CS, ev[("g0", s)])
        wait("vector", LD, ev[("ld", s)])
        if s >= 1:
            wait("vector", p.STs[1 - r], ev[("st", s - 1)])
        op("vector", lambda h: h.tensor_tensor(out=p.bg0, in0=p.b2s0[r], in1=p.bg0,
                                               op=ALU.mult))
        op("vector", lambda h: h.tensor_tensor(out=p.bx1[r], in0=p.bg0, in1=p.b2x[r],
                                               op=ALU.add), inc=(CV, 1))
        ev[("x1", s)] = CV.v
        wait("gpsimd", CV, ev[("x1", s)])
        dma("gpsimd", ST, p.x1_s[SUB * s:SUB * (s + 1), :], p.bx1[r])
        ev[("st", s)] = ST.v

    p.ev_b2 = ev


def _phase_C(S):
    p = S.p
    op, dma, wait, mark = S.op, S.dma, S.wait, S.mark
    GW, CV, CS, CT = p.GW, p.CV, p.CS, p.CT
    n_sub = p.n_sub
    ev = {}
    _fence(S)
    dma("gpsimd", GW, p.wkv1_sb.rearrange("p a b -> p (a b)"), p.wkv1_h)
    wait("tensor", GW, GW.v)
    op("vector", lambda h: h.memset(p.PS[:, 2048:4096], 0.0))
    op("vector", lambda h: h.memset(p.cvext[0].rearrange("p a b -> p (a b)"), 0.0))
    op("vector", lambda h: h.memset(p.cvext[1].rearrange("p a b -> p (a b)"), 0.0))
    op("vector", lambda h: h.memset(p.cvext[0][:, :, 64:65].rearrange("p a b -> p (a b)"), 1.0))
    op("vector", lambda h: h.memset(p.cvext[1][:, :, 64:65].rearrange("p a b -> p (a b)"), 1.0))
    op("vector", lambda h: h.memset(p.mxg, 0.0), inc=(CV, 1))
    cstop = _STOP[0] if (_STOP[0] or "").startswith("C.") else None
    if cstop == "C.pre":
        return

    for s in range(n_sub):
        r = s % 2
        LD = p.LDs[r]
        ST = p.STs[r]
        if s >= 1:
            wait("sync", CV, ev[("xn1", s - 1)])
        dma("sync", LD, p.cx1[r], p.x1_s[SUB * s:SUB * (s + 1), :])
        ev[("ld", s)] = LD.v
        wait("vector", LD, ev[("ld", s)])
        wait("scalar", LD, ev[("ld", s)])
        if s >= 1:
            wait("vector", p.STs[1 - r], ev[("st", s - 1)])
            wait("vector", CS, ev[("xn1b", s - 1)])
            if ("xnt", s - 1) in ev:
                wait("vector", CT, ev[("xnt", s - 1)])
        _emit_ln(S, p.cx1[r], p.cxn1[r])
        ev[("xn1", s)] = CV.v
        wait("scalar", CV, ev[("xn1", s)])
        if ("xnt", s - 2) in ev:
            wait("scalar", CT, ev[("xnt", s - 2)])
        op("scalar", lambda h: h.activation(out=p.cxn1b[r], in_=p.cxn1[r],
                                            func=AF.Copy), inc=(CS, 1))
        ev[("xn1b", s)] = CS.v
        wait("gpsimd", CV, ev[("xn1", s)])
        dma("gpsimd", ST, p.xn1_s[SUB * s:SUB * (s + 1), :], p.cxn1[r])
        ev[("st", s)] = ST.v
        op("vector", lambda h: h.reduce_max(out=p.mx1, in_=p.cxn1[r], axis=AX.X,
                                            apply_absolute_value=True))
        op("vector", lambda h: h.tensor_tensor(out=p.mxg, in0=p.mxg, in1=p.mx1,
                                               op=ALU.max), inc=(CV, 1))
        if cstop == "C.ln":
            continue
        cxb3 = p.cxn1b[r].rearrange("p (a b) -> p a b", b=128)
        cxfm_f = p.cxfm.rearrange("p a b -> p (a b)")
        wait("tensor", CS, ev[("xn1b", s)])
        if ("vex", s - 1) in ev:
            wait("tensor", CV, ev[("vex", s - 1)])
        if ("xfmev", s - 1) in ev:
            wait("tensor", CS, ev[("xfmev", s - 1)])
        for c in range(8):
            op("tensor", lambda h, c=c: h.transpose(
                p.PSB[:, 128 * c:128 * (c + 1)], cxb3[:, c, :], p.idb_sb))
        mark("tensor", CT)
        ev[("xnt", s)] = CT.v
        wait("scalar", CT, ev[("xnt", s)])
        if s >= 1:
            wait("scalar", p.STs[1 - r], ev[("st_xf", s - 1)])
        op("scalar", lambda h: h.activation(out=cxfm_f, in_=p.PSB[:, 0:1024],
                                            func=AF.Copy), inc=(CS, 1))
        ev[("xfmev", s)] = CS.v
        wait("gpsimd", CS, ev[("xfmev", s)])
        dma("gpsimd", ST, p.xn1f_s[:, D * s:D * (s + 1)], cxfm_f)
        ev[("st_xf", s)] = ST.v
        if cstop == "C.xfm":
            continue
        wait("tensor", CS, ev[("xfmev", s)])
        for pr in range(8):
            op("tensor", lambda h, pr=pr: h.ldweights(p.cxfm[:, pr, :]))
            op("tensor", lambda h, pr=pr: h.matmul(
                p.PS[:, 256 * pr:256 * (pr + 1)],
                p.cxfm[:, pr, :], p.wkv1_sb[:, pr, :],
                start=(pr % 2 == 0), stop=(pr % 2 == 1)))
        mark("tensor", CT)
        ev[("kvproj", s)] = CT.v
        if cstop == "C.proj":
            continue
        kview = p.PS[:, 0:2048].rearrange("p (hd kv e) -> p hd kv e", kv=2, e=64)
        kq = kview[:, :, 0, :]
        vq = kview[:, :, 1, :]
        ckn3 = p.ckn.rearrange("p (a b) -> p a b", b=64)
        cke3 = p.cke.rearrange("p (a b) -> p a b", b=64)
        cphik3 = p.cphik.rearrange("p (a b) -> p a b", b=64)
        wait("vector", CT, ev[("kvproj", s)])
        op("vector", lambda h: h.tensor_scalar(out=ckn3, in0=kq, scalar1=0.0,
                                               scalar2=None, op0=ALU.min), inc=(CV, 1))
        ev[("kneg", s)] = CV.v
        if cstop == "C.kneg":
            continue
        wait("scalar", CV, ev[("kneg", s)])
        op("scalar", lambda h: h.activation(out=p.cke, in_=p.ckn, func=AF.Exp),
           inc=(CS, 1))
        ev[("kexp", s)] = CS.v
        if cstop == "C.kexp":
            continue
        wait("vector", CS, ev[("kexp", s)])
        if ("kv", s - 1) in ev:
            wait("vector", CT, ev[("kv", s - 1)])
        op("vector", lambda h: h.scalar_tensor_tensor(
            out=cphik3, in0=kq, scalar=0.0, in1=cke3, op0=ALU.max, op1=ALU.add),
           inc=(CV, 1))
        ev[("phik", s)] = CV.v
        if cstop == "C.stt":
            continue
        if ("kv", s - 2) in ev:
            wait("vector", CT, ev[("kv", s - 2)])
        op("vector", lambda h: h.tensor_copy(
            out=p.cvext[r][:, :, 0:64], in_=vq), inc=(CV, 1))
        ev[("vex", s)] = CV.v
        if cstop == "C.phik":
            continue
        wait("tensor", CV, ev[("vex", s)])
        phv = p.cphik.rearrange("p (hd e) -> p hd e", e=64)
        for hd in range(16):
            col = 2048 + 128 * hd
            op("tensor", lambda h, hd=hd: h.ldweights(phv[:, hd, :]))
            op("tensor", lambda h, hd=hd, col=col, s=s: h.matmul(
                p.PS[0:64, col:col + 72], phv[:, hd, :], p.cvext[r][:, hd, :],
                start=(s == 0 and hd % 4 == 0),
                stop=(s == n_sub - 1 and hd % 4 == 3)))
        mark("tensor", CT)
        ev[("kv", s)] = CT.v
        mark("vector", CV)
        ev[("con", s)] = CV.v

    p.ev_c = ev


def _x2_kv_max(S):
    p = S.p
    op, dma, wait, mark = S.op, S.dma, S.wait, S.mark
    GW, CV, CS, CT, CP = p.GW, p.CV, p.CS, p.CT, p.CP
    _fence(S)
    wait("vector", CT, p.ev_c[("kv", p.n_sub - 1)])
    op("vector", lambda h: h.tensor_copy(out=p.kvsb, in_=p.PS[:, 2048:4096]),
       inc=(CV, 1))
    wait("gpsimd", CV, CV.v)
    dma("gpsimd", p.STs[0], p.cckv_in, p.kvsb)
    wait("tensor", CV, CV.v)
    op("tensor", lambda h: h.transpose(p.PS[0:1, 0:128], p.mxg, p.identf_sb))
    mark("tensor", CT)
    wait("vector", CT, CT.v)
    op("vector", lambda h: h.reduce_max(out=p.mxs, in_=p.PS[0:1, 0:128], axis=AX.X),
       inc=(CV, 1))
    wait("tensor", CV, CV.v)
    op("tensor", lambda h: h.matmul(p.PS[:, 512:513], p.onesf_sb, p.mxs,
                                    start=True, stop=True))
    mark("tensor", CT)
    wait("vector", CT, CT.v)
    op("vector", lambda h: h.tensor_copy(out=p.mx1, in_=p.PS[:, 512:513]),
       inc=(CV, 1))
    wait("gpsimd", CV, CV.v)
    dma("gpsimd", p.STs[0], p.ccm1_in, p.mx1)
    wait("gpsimd", p.STs[0], p.STs[0].v)
    op("gpsimd", lambda h: h.collective_compute(
        "AllReduce", ALU.add, replica_groups=[[0, 1, 2, 3], [4, 5, 6, 7]],
        ins=[p.cckv_in.opt()], outs=[p.cckv_out.opt()]), inc=(CP, 1))
    op("gpsimd", lambda h: h.collective_compute(
        "AllReduce", ALU.max, replica_groups=[[0, 1, 2, 3, 4, 5, 6, 7]],
        ins=[p.ccm1_in.opt()], outs=[p.ccm1_out.opt()]), inc=(CP, 1))
    wait("gpsimd", CP, CP.v)
    dma("gpsimd", GW, p.kvsb, p.cckv_out)
    dma("gpsimd", GW, p.mx1, p.ccm1_out)
    wait("vector", GW, GW.v)
    op("vector", lambda h: h.tensor_copy(out=p.kvb, in_=p.kvsb), inc=(CV, 1))
    wait("scalar", GW, GW.v)
    op("scalar", lambda h: h.activation(out=p.clip_sg, in_=p.clip_sb,
                                        func=AF.Sigmoid), inc=(CS, 1))
    wait("tensor", CS, CS.v)
    op("tensor", lambda h: h.matmul(p.PS[:, 520:522], p.onesf_sb, p.clip_sg,
                                    start=True, stop=True))
    mark("tensor", CT)
    wait("vector", CT, CT.v)
    op("vector", lambda h: h.tensor_tensor(out=p.rsx, in0=p.mx1, in1=p.PS[:, 520:521],
                                           op=ALU.mult))
    op("vector", lambda h: h.tensor_scalar(out=p.rsx, in0=p.rsx, scalar1=1.0 / 127.0,
                                           scalar2=None, op0=ALU.mult), inc=(CV, 1))
    wait("vector", CV, CV.v)
    op("vector", lambda h: h.tensor_copy(out=p.sx, in_=p.rsx), inc=(CV, 1))
    wait("vector", CV, CV.v)
    op("vector", lambda h: h.reciprocal(out=p.rsx, in_=p.rsx))
    op("vector", lambda h: h.tensor_copy(out=p.sgd, in_=p.PS[:, 521:522]),
       inc=(CV, 1))
    p.ev_rsx = CV.v


def _phase_D1(S):
    p = S.p
    op, dma, wait, mark = S.op, S.dma, S.wait, S.mark
    GW, CV, CS, CT = p.GW, p.CV, p.CS, p.CT
    n_sub = p.n_sub
    ev = {}
    _fence(S)
    dma("gpsimd", GW, p.wq1_sb.rearrange("p a b -> p (a b)"), p.wq1_h)
    for c in range(8):
        dma("gpsimd", GW, p.wo1_sb[:, c, :], p.wo1_h[128 * c:128 * (c + 1), :])
    wait("tensor", GW, GW.v)

    for s in range(n_sub):
        r = s % 2
        LD = p.LDs[r]
        ST = p.STs[r]
        if s >= 2:
            wait("sync", CV, ev[("con", s - 2)])
        if s >= 1:
            wait("sync", CT, ev[("qproj", s - 1)])
        dma("sync", LD, p.cxfm.rearrange("p a b -> p (a b)"),
            p.xn1f_s[:, D * s:D * (s + 1)])
        ev[("ld", s)] = LD.v

        wait("tensor", LD, ev[("ld", s)])
        if s >= 1:
            wait("tensor", CV, ev[("phiq", s - 1)])
            wait("tensor", CS, ev[("s1evac", s - 1)])
        for hd in range(16):
            op("tensor", lambda h, hd=hd: h.ldweights(p.wq1_sb[:, hd, :]))
            op("tensor", lambda h, hd=hd: h.matmul(
                p.PS[0:64, 128 * hd:128 * (hd + 1)],
                p.wq1_sb[:, hd, :], p.cxfm[:, hd // 2, :],
                start=(hd % 4 == 0), stop=(hd % 4 == 3)))
        mark("tensor", CT)
        ev[("qproj", s)] = CT.v
        wait("vector", CT, ev[("qproj", s)])
        op("vector", lambda h: h.tensor_scalar(out=p.dknq[0:64, :],
                                               in0=p.PS[0:64, 0:2048],
                                               scalar1=0.0, scalar2=None, op0=ALU.min),
           inc=(CV, 1))
        ev[("qneg", s)] = CV.v
        wait("scalar", CV, ev[("qneg", s)])
        op("scalar", lambda h: h.activation(out=p.dkeq[0:64, :], in_=p.dknq[0:64, :],
                                            func=AF.Exp), inc=(CS, 1))
        ev[("qexp", s)] = CS.v
        wait("vector", CS, ev[("qexp", s)])
        op("vector", lambda h: h.scalar_tensor_tensor(
            out=p.dphiq[0:64, :], in0=p.PS[0:64, 0:2048], scalar=0.0,
            in1=p.dkeq[0:64, :], op0=ALU.max, op1=ALU.add), inc=(CV, 1))
        ev[("phiq", s)] = CV.v

        wait("tensor", CV, ev[("phiq", s)])
        if s >= 1:
            wait("tensor", CV, ev[("comb", s - 1)])
        for hd in range(16):
            lhs = p.dphiq[0:64, 128 * hd:128 * (hd + 1)]
            op("tensor", lambda h, lhs=lhs: h.ldweights(lhs))
            op("tensor", lambda h, lhs=lhs, hd=hd: h.matmul(
                p.PS[:, 2048 + 128 * hd:2048 + 128 * hd + 72],
                lhs, p.kvb3[0:64, hd, 0:72],
                start=(hd % 4 == 0), stop=(hd % 4 == 3)))
        mark("tensor", CT)
        ev[("ctx", s)] = CT.v
        wait("vector", CT, ev[("ctx", s)])
        if s >= 1:
            wait("vector", CT, ev[("dct", s - 1)])
        ctxv = p.PS[:, 2048:4096].rearrange("p (hd e) -> p hd e", e=128)
        op("vector", lambda h: h.tensor_scalar(
            out=p.drz, in0=ctxv[:, :, 64:65].rearrange("p hd o -> p (hd o)"),
            scalar1=EPS1, scalar2=None, op0=ALU.add))
        op("vector", lambda h: h.reciprocal(out=p.drz, in_=p.drz), inc=(CV, 1))
        wait("vector", CV, CV.v)
        for hd in range(16):
            op("vector", lambda h, hd=hd: h.tensor_scalar(
                out=p.dcomb[:, hd, :], in0=ctxv[:, hd, 0:64],
                scalar1=p.drz[:, hd:hd + 1], scalar2=None, op0=ALU.mult))
        mark("vector", CV)
        ev[("comb", s)] = CV.v
        dc3 = p.dcomb.rearrange("p hd (c b) -> p (hd c) b", b=64)
        dcfm_f = p.dcombfm.rearrange("p a b -> p (a b)")
        wait("tensor", CV, ev[("comb", s)])
        if s >= 1:
            wait("tensor", CS, ev[("combev", s - 1)])
        for c in range(8):
            op("tensor", lambda h, c=c: h.transpose(
                p.PSB[:, 4096 + 128 * c:4096 + 128 * (c + 1)],
                p.dcomb.rearrange("p hd e -> p (hd e)")[:, 128 * c:128 * (c + 1)],
                p.idb_sb))
        mark("tensor", CT)
        ev[("dct", s)] = CT.v
        wait("scalar", CT, ev[("dct", s)])
        op("scalar", lambda h: h.activation(out=dcfm_f, in_=p.PSB[:, 4096:5120],
                                            func=AF.Copy), inc=(CS, 1))
        ev[("combev", s)] = CS.v
        wait("tensor", CS, ev[("combev", s)])
        if s >= 1:
            wait("tensor", CS, ev[("s1evac", s - 1)])
        for c in range(8):
            op("tensor", lambda h, c=c: h.ldweights(p.dcombfm[:, c, :]))
            for j in range(2):
                op("tensor", lambda h, c=c, j=j: h.matmul(
                    p.PS[:, 1024 + 512 * j:1024 + 512 * (j + 1)],
                    p.dcombfm[:, c, :], p.wo1_sb[:, c, 512 * j:512 * (j + 1)],
                    start=(c == 0), stop=(c == 7)))
        mark("tensor", CT)
        ev[("s1mm", s)] = CT.v
        wait("scalar", CT, ev[("s1mm", s)])
        if s >= 2:
            wait("scalar", ST, ev[("st", s - 2)])
        op("scalar", lambda h: h.activation(out=p.ds1[r], in_=p.PS[:, 1024:2048],
                                            func=AF.Copy), inc=(CS, 1))
        ev[("s1evac", s)] = CS.v
        wait("gpsimd", CS, ev[("s1evac", s)])
        dma("gpsimd", ST, p.s1_s[SUB * s:SUB * (s + 1), :], p.ds1[r])
        ev[("st", s)] = ST.v
        mark("vector", CV)
        ev[("con", s)] = CV.v

    p.ev_d1 = ev


def _phase_D2(S):
    p = S.p
    op, dma, wait, mark = S.op, S.dma, S.wait, S.mark
    GW, CV, CS, CT = p.GW, p.CV, p.CS, p.CT
    n_sub = p.n_sub
    ev = {}
    _fence(S)
    _wquant_int8(S, p.wg1_h, p.wg_fm, I0, D)
    wq_ev = _wquant_int8(S, p.wu1_h, p.wu_fm, I0, D)
    wait("tensor", CV, wq_ev)
    wait("tensor", p.LDs[0], p.LDs[0].v)
    wait("tensor", p.LDs[1], p.LDs[1].v)
    wait("sync", CV, wq_ev)
    wait("sync", p.LDs[0], p.LDs[0].v)
    wait("sync", p.LDs[1], p.LDs[1].v)
    wait("vector", CV, p.ev_rsx)
    op("vector", lambda h: h.memset(p.mxg, 0.0), inc=(CV, 1))

    for s in range(n_sub):
        r = s % 2
        LD = p.LDs[r]
        ST = p.STs[r]
        if s >= 1:
            wait("sync", CV, ev[("xq1", s - 1)])
        dma("sync", LD, p.dxn1[r], p.xn1_s[SUB * s:SUB * (s + 1), :])
        ev[("ld", s)] = LD.v
        wait("vector", LD, ev[("ld", s)])
        op("vector", lambda h: h.tensor_scalar(out=p.qt.w[:, 0:D], in0=p.dxn1[r],
                                               scalar1=p.rsx, scalar2=None,
                                               op0=ALU.mult), inc=(CV, 1))
        cv = _emit_round_clip(S, p.qt.w[:, 0:D], p.bxq[:, 0:D], D)
        ev[("xq1", s)] = cv
        bxq3 = p.bxq[:, 0:D].rearrange("p (a b) -> p a b", b=128)
        bxqfm_f = p.bxqfm.rearrange("p a b -> p (a b)")
        wait("tensor", CV, cv)
        if s >= 1:
            wait("tensor", CS, ev[("xqev", s - 1)])
            wait("tensor", CV, ev[("h1", s - 1)])
        for c in range(8):
            op("tensor", lambda h, c=c: h.transpose(
                p.PSB[:, 7168 + 128 * c:7168 + 128 * (c + 1)], bxq3[:, c, :],
                p.idb_sb))
        mark("tensor", CT)
        ev[("xqt", s)] = CT.v
        wait("scalar", CT, ev[("xqt", s)])
        op("scalar", lambda h: h.activation(out=bxqfm_f, in_=p.PSB[:, 7168:8192],
                                            func=AF.Copy), inc=(CS, 1))
        ev[("xqev", s)] = CS.v
        wait("tensor", CS, ev[("xqev", s)])
        for c in range(8):
            op("tensor", lambda h, c=c: h.ldweights(p.bxqfm[:, c, :]))
            for (a, wdt) in _islices(I0, 512):
                op("tensor", lambda h, c=c, a=a, wdt=wdt: h.matmul(
                    p.PS[:, a:a + wdt], p.bxqfm[:, c, :], p.wg_fm[:, c, a:a + wdt],
                    start=(c == 0), stop=(c == 7)))
        mark("tensor", CT)
        ev[("gate", s)] = CT.v
        wait("scalar", CT, ev[("gate", s)])
        op("scalar", lambda h: h.activation(out=p.bgs, in_=p.PS[:, 0:I0],
                                            func=AF.Silu, scale=p.sx), inc=(CS, 1))
        ev[("gs", s)] = CS.v
        wait("tensor", CS, ev[("gs", s)])
        for c in range(8):
            op("tensor", lambda h, c=c: h.ldweights(p.bxqfm[:, c, :]))
            for (a, wdt) in _islices(I0, 512):
                op("tensor", lambda h, c=c, a=a, wdt=wdt: h.matmul(
                    p.PS[:, a:a + wdt], p.bxqfm[:, c, :], p.wu_fm[:, c, a:a + wdt],
                    start=(c == 0), stop=(c == 7)))
        mark("tensor", CT)
        ev[("up", s)] = CT.v
        wait("vector", CT, ev[("up", s)])
        if s >= 1:
            wait("vector", p.STs[1 - r], ev[("st", s - 1)])
        op("vector", lambda h: h.scalar_tensor_tensor(
            out=p.bh1[r], in0=p.PS[:, 0:I0], scalar=p.sx, in1=p.bgs,
            op0=ALU.mult, op1=ALU.mult), inc=(CV, 1))
        ev[("h1", s)] = CV.v
        wait("gpsimd", CV, ev[("h1", s)])
        dma("gpsimd", ST, p.h1_s[SUB * s:SUB * (s + 1), :], p.bh1[r])
        ev[("st", s)] = ST.v
        op("vector", lambda h: h.reduce_max(out=p.mx1, in_=p.bh1[r], axis=AX.X,
                                            apply_absolute_value=True))
        op("vector", lambda h: h.tensor_tensor(out=p.mxg, in0=p.mxg, in1=p.mx1,
                                               op=ALU.max), inc=(CV, 1))

    p.ev_d2 = ev


def _x3_maxh(S):
    p = S.p
    op, dma, wait, mark = S.op, S.dma, S.wait, S.mark
    GW, CV, CS, CT, CP = p.GW, p.CV, p.CS, p.CT, p.CP
    _fence(S)
    op("tensor", lambda h: h.transpose(p.PS[0:1, 0:128], p.mxg, p.identf_sb))
    mark("tensor", CT)
    wait("vector", CT, CT.v)
    op("vector", lambda h: h.reduce_max(out=p.mxs, in_=p.PS[0:1, 0:128], axis=AX.X),
       inc=(CV, 1))
    wait("tensor", CV, CV.v)
    op("tensor", lambda h: h.matmul(p.PS[:, 512:513], p.onesf_sb, p.mxs,
                                    start=True, stop=True))
    mark("tensor", CT)
    wait("vector", CT, CT.v)
    op("vector", lambda h: h.tensor_copy(out=p.mx1, in_=p.PS[:, 512:513]),
       inc=(CV, 1))
    wait("gpsimd", CV, CV.v)
    dma("gpsimd", p.STs[0], p.ccm2_in, p.mx1)
    wait("gpsimd", p.STs[0], p.STs[0].v)
    op("gpsimd", lambda h: h.collective_compute(
        "AllReduce", ALU.max, replica_groups=[[0, 1, 2, 3, 4, 5, 6, 7]],
        ins=[p.ccm2_in.opt()], outs=[p.ccm2_out.opt()]), inc=(CP, 1))
    wait("gpsimd", CP, CP.v)
    dma("gpsimd", GW, p.mx1, p.ccm2_out)
    wait("vector", GW, GW.v)
    op("vector", lambda h: h.tensor_tensor(out=p.rsh, in0=p.mx1, in1=p.sgd,
                                           op=ALU.mult))
    op("vector", lambda h: h.tensor_scalar(out=p.rsh, in0=p.rsh, scalar1=1.0 / 127.0,
                                           scalar2=None, op0=ALU.mult), inc=(CV, 1))
    wait("vector", CV, CV.v)
    op("vector", lambda h: h.tensor_copy(out=p.sh, in_=p.rsh), inc=(CV, 1))
    wait("vector", CV, CV.v)
    op("vector", lambda h: h.reciprocal(out=p.rsh, in_=p.rsh), inc=(CV, 1))
    p.ev_rsh = CV.v


def _phase_E(S):
    p = S.p
    op, dma, wait, mark = S.op, S.dma, S.wait, S.mark
    GW, CV, CS, CT = p.GW, p.CV, p.CS, p.CT
    n_sub = p.n_sub
    ev = {}
    _fence(S)
    wq_ev = _wquant_int8(S, p.wd1_h, p.wd_fm, D, I0)
    wait("tensor", CV, wq_ev)
    wait("tensor", p.LDs[0], p.LDs[0].v)
    wait("tensor", p.LDs[1], p.LDs[1].v)
    wait("sync", CV, wq_ev)
    wait("sync", p.LDs[0], p.LDs[0].v)
    wait("sync", p.LDs[1], p.LDs[1].v)
    wait("vector", CV, p.ev_rsh)

    for s in range(n_sub):
        r = s % 2
        LD = p.LDs[r]
        ST = p.STs[r]
        if s >= 2:
            wait("sync", CV, ev[("x2", s - 2)])
            wait("sync", CV, ev[("hq", s - 2)])
        dma("sync", LD, p.eh1[r], p.h1_s[SUB * s:SUB * (s + 1), :])
        dma("sync", LD, p.es1[r], p.s1_s[SUB * s:SUB * (s + 1), :])
        dma("sync", LD, p.ex1[r], p.x1_s[SUB * s:SUB * (s + 1), :])
        ev[("ld", s)] = LD.v
        wait("vector", LD, ev[("ld", s)])
        op("vector", lambda h: h.tensor_scalar(out=p.qt.hv[:, 0:I0], in0=p.eh1[r],
                                               scalar1=p.rsh, scalar2=None,
                                               op0=ALU.mult), inc=(CV, 1))
        for (a, wdt) in _islices(I0, 1408):
            cv = _emit_round_clip(S, p.qt.hv[:, a:a + wdt], p.bhq[:, a:a + wdt], wdt)
        ev[("hq", s)] = cv
        bhq3 = p.bhq[:, 0:I0].rearrange("p (a b) -> p a b", b=128)
        bhqfm_f = p.bhqfm.rearrange("p a b -> p (a b)")
        wait("tensor", CV, ev[("hq", s)])
        if s >= 1:
            wait("tensor", CS, ev[("hqev", s - 1)])
        for c in range(22):
            op("tensor", lambda h, c=c: h.transpose(
                p.PSB[:, 4096 + 128 * c:4096 + 128 * (c + 1)], bhq3[:, c, :],
                p.idb_sb))
        mark("tensor", CT)
        ev[("hqt", s)] = CT.v
        wait("scalar", CT, ev[("hqt", s)])
        op("scalar", lambda h: h.activation(out=bhqfm_f, in_=p.PSB[:, 4096:6912],
                                            func=AF.Copy), inc=(CS, 1))
        ev[("hqev", s)] = CS.v
        wait("tensor", CS, ev[("hqev", s)])
        if s >= 1:
            wait("tensor", CS, ev[("g1", s - 1)])
        for c in range(22):
            op("tensor", lambda h, c=c: h.ldweights(p.bhqfm[:, c, :]))
            for j in range(2):
                op("tensor", lambda h, c=c, j=j: h.matmul(
                    p.PS[:, 512 * j:512 * (j + 1)], p.bhqfm[:, c, :],
                    p.wd_fm[:, c, 512 * j:512 * (j + 1)],
                    start=(c == 0), stop=(c == 21)))
        mark("tensor", CT)
        ev[("down", s)] = CT.v
        wait("scalar", CT, ev[("down", s)])
        if s >= 1:
            wait("scalar", CV, ev[("x2", s - 1)])
        op("scalar", lambda h: h.activation(out=p.bg0, in_=p.PS[:, 0:1024],
                                            func=AF.Silu, scale=p.sh), inc=(CS, 1))
        ev[("g1", s)] = CS.v
        wait("vector", CS, ev[("g1", s)])
        if s >= 1:
            wait("vector", p.STs[1 - r], ev[("st", s - 1)])
        op("vector", lambda h: h.tensor_tensor(out=p.bg0, in0=p.es1[r], in1=p.bg0,
                                               op=ALU.mult))
        op("vector", lambda h: h.tensor_tensor(out=p.ex2[r], in0=p.bg0, in1=p.ex1[r],
                                               op=ALU.add), inc=(CV, 1))
        ev[("x2", s)] = CV.v
        wait("gpsimd", CV, ev[("x2", s)])
        dma("gpsimd", ST, p.y_h[SUB * s:SUB * (s + 1), :], p.ex2[r])
        ev[("st", s)] = ST.v

    p.ev_e = ev


def _mk_prog(nc, n_sub, stack):
    p = NS()
    p.n_sub = n_sub
    ntok = n_sub * SUB

    def din(name, shape, dt=F32):
        return nc.dram_tensor(name, list(shape), dt, kind="ExternalInput").ap()

    def scr(name, shape, dt=F32):
        return nc.dram_tensor(name, list(shape), dt, kind="Internal").ap()

    p.x_h = din("x", (ntok, D))
    p.ropec_h = din("ropec", (ntok, 128))
    p.ropes_h = din("ropes", (ntok, 128))
    p.ropecq_h = din("ropecq", (ntok, 128))
    p.ropesq_h = din("ropesq", (ntok, 128))
    p.kdect_h = din("kdect", (SUB, H0))
    p.crosst_h = din("crosst", (SUB, H0))
    p.dmaskt_h = din("dmaskt", (SUB, H0 * SUB))
    p.lnw_rep_h = din("lnwrep", (SUB, D))
    p.lnb_rep_h = din("lnbrep", (SUB, D))
    p.gnw_rep_h = din("gnwrep", (SUB, DV))
    p.sxw_h = din("sxw", (SUB, 3 * H0))
    p.sxm_h = din("sxm", (SUB, 4))
    p.actc_h = din("actc", (SUB, len(ACT_CONSTS)))
    p.onesf_h = din("onesf", (1, SUB))
    p.identf_h = din("identf", (SUB, SUB))
    p.clipx_h = din("clipx", (1, 2))
    p.wkv_h = din("wkv", (D, 3 * D))
    p.wqg_h = din("wqg", (D, 3 * D))
    p.wo0_h = din("wo0", (2 * D, D))
    p.wg0_h = din("wg0", (I0, D))
    p.wu0_h = din("wu0", (I0, D))
    p.wd0_h = din("wd0", (D, I0))
    p.wq1_h = din("wq1", (SUB, 16 * 64))
    p.wkv1_h = din("wkv1", (SUB, 8 * 256))
    p.wo1_h = din("wo1", (D, D))
    p.wg1_h = din("wg1", (I0, D))
    p.wu1_h = din("wu1", (I0, D))
    p.wd1_h = din("wd1", (D, I0))
    p.y_h = nc.dram_tensor("y", [ntok, D], F32, kind="ExternalOutput").ap()

    p.xn_s = scr("xn_s", (ntok, D))
    p.normed_f_s = scr("normed_f_s", (SUB, n_sub * D), BF16)
    p.ktm_f_s = scr("ktm_f_s", (SUB, n_sub * D), BF16)
    p.ktm_s = scr("ktm_s", (ntok, D), BF16)
    p.vtm_s = scr("vtm_s", (ntok, 2 * D), BF16)
    p.s0_s = scr("s0_s", (ntok, D))
    p.x1_s = scr("x1_s", (ntok, D))
    p.xn1_s = scr("xn1_s", (ntok, D))
    p.xn1f_s = scr("xn1f_s", (SUB, n_sub * D), BF16)
    p.hq0f_s = scr("hq0f_s", (SUB, n_sub * I0), BF16)
    p.s1_s = scr("s1_s", (ntok, D))
    p.h1_s = scr("h1_s", (ntok, I0))
    p.ccs_in = scr("ccs_in", (SUB, 3 * H0 * 2 * 512), BF16)
    p.ccs_out = scr("ccs_out", (SUB, 3 * H0 * 2 * 512), BF16)
    p.cckv_in = scr("cckv_in", (SUB, 2048))
    p.cckv_out = scr("cckv_out", (SUB, 2048))
    p.ccm1_in = scr("ccm1_in", (SUB, 1))
    p.ccm1_out = scr("ccm1_out", (SUB, 1))
    p.ccm2_in = scr("ccm2_in", (SUB, 1))
    p.ccm2_out = scr("ccm2_out", (SUB, 1))

    p.PS = stack.enter_context(nc.psum_tensor("PS", [128, 4096], F32))[:]
    p.PSB = p.PS.bitcast(BF16)

    def sb(name, shape, dt=F32):
        return stack.enter_context(nc.sbuf_tensor(name, list(shape), dt))[:]

    p.actc_sb = sb("actc_sb", (SUB, len(ACT_CONSTS)))
    for i, v in enumerate(ACT_CONSTS):
        nc.const_aps.register(F32, v, p.actc_sb[:, i:i + 1])
    p.kdect_sb = sb("kdect_sb", (SUB, H0))
    p.crosst_sb = sb("crosst_sb", (SUB, H0))
    p.sxw_sb = sb("sxw_sb", (SUB, 3 * H0))
    p.sxm_sb = sb("sxm_sb", (SUB, 4))
    p.onesf_sb = sb("onesf_sb", (1, SUB))
    p.identf_sb = sb("identf_sb", (SUB, SUB))
    p.idb_sb = sb("idb_sb", (SUB, SUB), BF16)
    p.clip_sb = sb("clip_sb", (1, 2))
    p.clip_sg = sb("clip_sg", (1, 2))
    p.trash = sb("trash", (SUB, D))
    p.trash2 = sb("trash2", (SUB, D))
    p.sums = sb("sums", (SUB, 4))
    p.mu = sb("mu", (SUB, 1))
    p.rstd = sb("rstd", (SUB, 1))
    p.mu2 = sb("mu2", (SUB, 1))
    p.mx1 = sb("mx1", (SUB, 1))
    p.mxg = sb("mxg", (SUB, 1))
    p.mxs = sb("mxs", (1, 1))
    p.rsx = sb("rsx", (SUB, 1))
    p.sx = sb("sx", (SUB, 1))
    p.rsh = sb("rsh", (SUB, 1))
    p.sh = sb("sh", (SUB, 1))
    p.sgd = sb("sgd", (SUB, 1))
    p.rm = sb("rm", (SUB, 1))
    p.rr1 = sb("rr1", (SUB, 1))
    p.tmp1 = sb("tmp1", (SUB, 128))
    p.tmp2 = sb("tmp2", (SUB, 128))
    p.tmp3 = sb("tmp3", (SUB, 512))
    p.drz = sb("drz", (SUB, 16))

    AR_COLS = 49872
    p.AR = stack.enter_context(nc.sbuf_tensor("AR", [128, AR_COLS], F32))[:]
    p._bump = [0]

    def take(cols, dt=F32, shape=None, at=None):
        a = p._bump[0] if at is None else at
        end = a + ((cols + 15) // 16) * 16
        assert end <= AR_COLS, f"arena overflow {end}"
        if at is None:
            p._bump[0] = end
        ap = p.AR[:, a:a + cols]
        if dt == BF16:
            ap = ap.bitcast(BF16)
        if shape is not None:
            names = " ".join(f"d{i}" for i in range(len(shape)))
            ap = ap.rearrange(f"p ({names}) -> p {names}",
                              **{f"d{i}": v for i, v in enumerate(shape)})
        return ap

    p.take = take
    return p


def _alloc_buffers(p):
    take = p.take
    w0 = p._bump[0]
    assert w0 == 0
    p.wkv_sb = take(12288, BF16, (8, 3072))
    p.wqg_sb = p.wkv_sb
    p.wo0_sb = take(8192, BF16, (16, 1024))
    wg_at = 0
    p.wg_fm = take(11264, BF16, (8, I0), at=0)
    p.wu_fm = take(11264, BF16, (8, I0), at=11264)
    p.wd_fm = take(11264, BF16, (22, 1024), at=0)
    p.wo1_sb = take(4096, BF16, (8, 1024), at=11264)
    p.wq1_sb = take(512, BF16, (16, 64), at=11264 + 4096)
    p.wkv1_sb = take(1024, BF16, (8, 256), at=11264 + 4608)
    p._bump[0] = max(p._bump[0], 22528)
    p.xch = take(6144, BF16, (3, H0, 2, 512), at=12288)

    p.a0x = [take(D) for _ in range(2)]
    p.a0xn = [take(D) for _ in range(2)]
    p.a0z = take(D)
    p.a0nb = [take(512, BF16) for _ in range(2)]
    p.normfm = [take(512, BF16, (8, 128)) for _ in range(2)]
    p.ropec_t = [take(128) for _ in range(2)]
    p.ropes_t = [take(128) for _ in range(2)]
    p.a0ktm = [take(512, BF16, (H0, 256)) for _ in range(2)]
    p.a0ktil = take(512, BF16, (H0, 256))
    p.a0vtm = [take(1024, BF16) for _ in range(2)]
    p.a1nfm = p.normfm
    p.a1kfm = [take(512, BF16, (8, 128)) for _ in range(2)]
    p.a1ktm = [take(512, BF16) for _ in range(2)]
    p.a1vtm = p.a0vtm
    p.a1xn = p.a0x
    p.a1qtm = take(512, BF16, (H0, 256))
    p.a1qctm = take(512, BF16, (H0, 256))
    p.a1qfm = take(512, BF16, (8, 128))
    p.a1qcfm = take(512, BF16, (8, 128))
    p.a1gs = take(2 * D)
    p.a1at = take(256, BF16, (H0, 128))
    p.a1og = take(1024, BF16, (H0, 512))
    p.a1ogfm = take(1024, BF16, (16, 128))
    p.a1s0 = [p.a0xn[0], p.a0xn[1]]
    p.lnw_sb = take(D)
    p.lnb_sb = take(D)
    p.gnw_sb = take(DV)
    p.dmask_sb = take(H0 * SUB)
    p.S_sb = take(H0 * 512, BF16, (H0, 2, 512))
    g1_end = p._bump[0]
    print(f"[kernel] arena group1 end {g1_end}")
    p._bump[0] = 22528

    q0 = p._bump[0]
    q = NS()
    q.aw = take(1408)
    q.y0 = take(1408)
    q.tt = take(1408)
    q.d = take(1408)
    q.sg = [take(704, BF16) for _ in range(2)]
    q.tmpb = take(704, BF16)
    q.acc1 = take(704, BF16)
    q.acc2 = take(704, BF16)
    q.sgn = take(704, BF16)
    q.bm = take(176)
    q.rr = take(176)
    q.w = take(1408)
    p.qt = q
    qend = p._bump[0]
    p.cxfm = take(512, BF16, (8, 128), at=q0)
    p.ckn = take(1024, at=q0 + 512)
    p.cke = take(1024, at=q0 + 1536)
    p.cphik = take(512, BF16, at=q0 + 2560)
    p.cvext = [take(576, BF16, (16, 72), at=q0 + 3072 + 576 * i) for i in range(2)]
    p.dknq = take(2048, at=q0 + 512)
    p.dkeq = take(2048, at=q0 + 2560)
    p.dphiq = take(1024, BF16, at=q0 + 4608)
    p.dcomb = take(512, BF16, (16, 64), at=q0 + 5632)
    p.dcombfm = take(512, BF16, (8, 128), at=q0 + 6144)
    p.ds1 = [take(1024, at=q0 + 6656 + 1024 * i) for i in range(2)]
    p.kvsb = take(2048, at=q0 + 7808)
    p.kvb = take(1024, BF16)
    p.kvb3 = p.kvb.rearrange("p (hd e) -> p hd e", e=128)
    assert q0 + 9856 <= qend, (q0, qend)

    bxn1 = take(D)
    p.bxn = [bxn1, bxn1]
    p.bx = p.bxn
    bs01 = take(D)
    p.bs0 = [bs01, bs01]
    p.bxq = take(512, BF16)
    bhqfm_cols = take(1408, BF16)
    p.bhqfm = bhqfm_cols.rearrange("p (a b) -> p a b", a=22)
    p.bxqfm = take(512, BF16, (8, 128))
    p.bgs = take(I0)
    p.bhq = take(1408, BF16)
    p.bg0 = take(D)
    bx11 = take(D)
    p.bx1 = [bx11, bx11]
    cx11 = bhqfm_cols.bitcast(F32)[:, 0:D]
    p.cx1 = [cx11, cx11]
    cxn11 = p.bgs[:, 0:D]
    p.cxn1 = [cxn11, cxn11]
    cxn1b1 = p.bhq[:, 0:D]
    p.cxn1b = [cxn1b1, cxn1b1]
    bh11 = take(I0)
    p.bh1 = [bh11, bh11]
    q.wslab = [p.bgs, p.bh1[0]]
    q.wqb = p.bhq
    q.hv = p.bgs
    p.dxn1 = p.cx1
    p.eh1 = [p.bh1[0], take(I0, at=11264)]
    p.es1 = p.ds1
    p.ex1 = [p.bs0[0], take(D, at=11264 + I0)]
    p.ex2 = p.bx1
    p.bhqfm_b = take(1408, BF16, (22, 128), at=11264 + I0 + D)
    p.b2x = [p.bx[0], take(D, at=11264 + I0 + D + 1408)]
    p.b2s0 = [p.bs0[0], take(D, at=11264 + I0 + D + 1408 + D)]
    print(f"[kernel] arena group2 end {p._bump[0]} / 49936 f32 cols")


_STOP = [None]


def build_nc(n_sub=NSUB, stop=None):
    _STOP[0] = stop
    import contextlib
    nc = bass.Bass("TRN2", target_bir_lowering=False)
    with contextlib.ExitStack() as stack:
        p = _mk_prog(nc, n_sub, stack)
        _alloc_buffers(p)
        with (
            nc.semaphore("sLD") as s_ld,
            nc.semaphore("sLD1") as s_ld1,
            nc.semaphore("sGW") as s_gw,
            nc.semaphore("sST") as s_st,
            nc.semaphore("sST1") as s_st1,
            nc.semaphore("sCV") as s_cv,
            nc.semaphore("sCS") as s_cs,
            nc.semaphore("sCT") as s_ct,
            nc.semaphore("sCP") as s_cp,
            nc.Block() as block,
        ):
            def fresh():
                p.LDs = [Ctr(s_ld, "LD0"), Ctr(s_ld1, "LD1")]
                p.GW = Ctr(s_gw, "GW")
                p.STs = [Ctr(s_st, "ST0"), Ctr(s_st1, "ST1")]
                p.CV = Ctr(s_cv, "CV")
                p.CS = Ctr(s_cs, "CS")
                p.CT = Ctr(s_ct, "CT")
                p.CP = Ctr(s_cp, "CP")

            @block.sync
            def _(h):
                fresh()
                _emit(p, "sync", h)

            @block.gpsimd
            def _(h):
                fresh()
                _emit(p, "gpsimd", h)

            @block.vector
            def _(h):
                fresh()
                _emit(p, "vector", h)

            @block.scalar
            def _(h):
                fresh()
                _emit(p, "scalar", h)

            @block.tensor
            def _(h):
                fresh()
                _emit(p, "tensor", h)

    return nc


def _prep_inputs(gi, n_sub=NSUB):
    ntok = n_sub * SUB
    tab = _host_tables(4 * ntok)
    x = np.ascontiguousarray(np.asarray(gi["L0_attn_x"], np.float32))
    wkv = np.ascontiguousarray(np.concatenate(
        [np.asarray(gi["L0_attn_Wk"], np.float32),
         np.asarray(gi["L0_attn_Wv"], np.float32)], axis=1))
    wqg = np.ascontiguousarray(np.concatenate(
        [np.asarray(gi["L0_attn_Wq"], np.float32),
         np.asarray(gi["L0_attn_Wg"], np.float32)], axis=1))
    rep = lambda v: np.ascontiguousarray(
        np.tile(np.asarray(v, np.float32)[None, :], (SUB, 1)))

    wqh = np.asarray(gi["L1_attn_Wq"], np.float32).reshape(8, 2, 64, 64)
    wq1b = np.zeros((8, 2, 2, 64, 64), np.float32)
    for hh in range(2):
        wq1b[:, hh, hh, :, :] = wqh[:, hh]
    wq1 = np.ascontiguousarray(
        wq1b.transpose(2, 3, 0, 1, 4).reshape(SUB, 16 * 64))
    wkh = np.asarray(gi["L1_attn_Wk"], np.float32).reshape(8, 2, 64, 64)
    wvh = np.asarray(gi["L1_attn_Wv"], np.float32).reshape(8, 2, 64, 64)
    wkv1b = np.zeros((8, 2, 64, 2, 2, 64), np.float32)
    for hh in range(2):
        wkv1b[:, hh, :, hh, 0, :] = wkh[:, hh]
        wkv1b[:, hh, :, hh, 1, :] = wvh[:, hh]
    wkv1 = np.ascontiguousarray(
        wkv1b.transpose(1, 2, 0, 3, 4, 5).reshape(SUB, 8 * 256))
    clip = np.array([[float(np.asarray(gi["L1_ffn_clip_x"]).reshape(-1)[0]),
                      float(np.asarray(gi["L1_ffn_clip_down"]).reshape(-1)[0])]],
                    np.float32)
    actc = np.tile(np.asarray(ACT_CONSTS, np.float32)[None, :], (SUB, 1))

    ins = []
    for c in range(8):
        b, pos = c // 4, c % 4
        t0 = pos * ntok
        sxw, sxm = _core_tables(pos, ntok)
        sl = slice(t0, t0 + ntok)
        ins.append({
            "x": np.ascontiguousarray(x[b, t0:t0 + ntok]),
            "ropec": np.ascontiguousarray(tab["ropec"][sl]),
            "ropes": np.ascontiguousarray(tab["ropes"][sl]),
            "ropecq": np.ascontiguousarray(tab["ropecq"][sl]),
            "ropesq": np.ascontiguousarray(tab["ropesq"][sl]),
            "kdect": tab["kdect"],
            "crosst": tab["crosst"],
            "dmaskt": np.ascontiguousarray(tab["dmaskt"].reshape(SUB, H0 * SUB)),
            "lnwrep": rep(gi["L0_attn_ln_w"]),
            "lnbrep": rep(gi["L0_attn_ln_b"]),
            "gnwrep": rep(gi["L0_attn_g_norm_w"]),
            "sxw": np.ascontiguousarray(np.tile(sxw.reshape(1, -1), (SUB, 1))),
            "sxm": np.ascontiguousarray(np.tile(sxm.reshape(1, 4), (SUB, 1))),
            "actc": actc,
            "onesf": np.ones((1, SUB), np.float32),
            "identf": np.eye(SUB, dtype=np.float32),
            "clipx": clip,
            "wkv": wkv, "wqg": wqg,
            "wo0": np.asarray(gi["L0_attn_Wo"], np.float32),
            "wg0": np.asarray(gi["L0_ffn_w_gate"], np.float32),
            "wu0": np.asarray(gi["L0_ffn_w_up"], np.float32),
            "wd0": np.asarray(gi["L0_ffn_w_down"], np.float32),
            "wq1": wq1, "wkv1": wkv1,
            "wo1": np.asarray(gi["L1_attn_Wo"], np.float32),
            "wg1": np.asarray(gi["L1_ffn_w_gate"], np.float32),
            "wu1": np.asarray(gi["L1_ffn_w_up"], np.float32),
            "wd1": np.asarray(gi["L1_ffn_w_down"], np.float32),
        })
    return ins


def kernel(**inputs):
    nc = build_nc(NSUB)
    ins = _prep_inputs(inputs, NSUB)
    res = run_bass_kernel_spmd(nc, ins, core_ids=list(range(8)))
    out = np.empty((B, T, D), np.float32)
    for c in range(8):
        b, pos = c // 4, c % 4
        out[b, pos * TPC:(pos + 1) * TPC] = res.results[c]["y"]
    return out
